# version 05
# baseline (speedup 1.0000x reference)
import numpy as np
import ml_dtypes
import concourse.bass as bass
import concourse.mybir as mybir
from concourse.bass_utils import run_bass_kernel_spmd
from concourse.bass_isa import ReduceOp

F32 = mybir.dt.float32
BF16 = mybir.dt.bfloat16
I32 = mybir.dt.int32
AF = mybir.ActivationFunctionType
ALU = mybir.AluOpType
AX = mybir.AxisListType

NCORES = 8
SEQ = 16384
D = 1024
T = SEQ // NCORES
NSLAB = D // 128
NTILE = T // 512
MAGIC = 12582912.0

STUB_L0ATTN = False
STUB_L1ATTN = False
import os
KDBG = os.environ.get("KDBG", "") == "1"
KCUT = int(os.environ.get("KCUT", "99"))
KCUT2 = int(os.environ.get("KCUT2", "99"))
KCUT3 = int(os.environ.get("KCUT3", "99"))
LAST_RES = None


class EM:

    def __init__(self, nc, ctx):
        self.nc = nc
        self.eng = {"t": nc.tensor, "v": nc.vector, "s": nc.scalar,
                    "g": nc.gpsimd, "q": nc.sync}
        self.esem = {e: ctx.enter_context(nc.semaphore(f"esem_{e}"))
                     for e in self.eng}
        self.ecnt = {e: 0 for e in self.eng}
        self.dmaq = {}
        for e, n in (("q", 8), ("g", 8), ("s", 4)):
            slots = [{"sem": ctx.enter_context(nc.semaphore(f"ds_{e}{i}")),
                      "key": f"d:{e}{i}", "cnt": 0} for i in range(n)]
            self.dmaq[e] = {"slots": slots, "idx": 0}
        self.ccsem = ctx.enter_context(nc.semaphore("ccsem"))
        self.cccnt = 0
        self.bufs = {}
        self.marks = {e: {} for e in self.eng}
        self.sems = {f"e:{e}": s for e, s in self.esem.items()}
        for e in self.dmaq:
            for sl in self.dmaq[e]["slots"]:
                self.sems[sl["key"]] = sl["sem"]
        self.sems["cc"] = self.ccsem
        self.prog = {e: [] for e in self.eng}
        self.aliases = {}

    def _expand(self, keys):
        out = list(keys)
        for k in keys:
            out.extend(self.aliases.get(k, ()))
        return tuple(out)

    def _buf(self, k):
        b = self.bufs.get(k)
        if b is None:
            b = {"w": None, "r": []}
            self.bufs[k] = b
        return b

    def _wait(self, e, token):
        if token is None:
            return
        semkey, val = token
        m = self.marks[e]
        if m.get(semkey, -1) >= val:
            return
        m[semkey] = val
        self.prog[e].append(("wait", semkey, val))

    def _deps(self, e, reads, writes):
        reads = self._expand(reads)
        writes = self._expand(writes)
        for k in reads:
            self._wait(e, self._buf(k)["w"])
        for k in writes:
            b = self._buf(k)
            self._wait(e, b["w"])
            for t in b["r"]:
                self._wait(e, t)

    def _finish(self, e, reads, writes, token):
        reads = self._expand(reads)
        writes = self._expand(writes)
        for k in reads:
            self._buf(k)["r"].append(token)
        for k in writes:
            b = self._buf(k)
            b["w"] = token
            b["r"] = []

    @staticmethod
    def _site():
        import sys
        f = sys._getframe(2)
        return f"{f.f_code.co_filename}:{f.f_lineno}"

    def emit(self, e, fn, reads=(), writes=()):
        fn._site = self._site()
        self._deps(e, reads, writes)
        self.ecnt[e] += 1
        tok = (f"e:{e}", self.ecnt[e])
        self.prog[e].append(("inst", fn, tok[0], tok[1], 1))
        self._finish(e, reads, writes, tok)
        return tok

    def mm(self, lhsT, rhs, out, bank, start, stop, reads=(),
           tile_position=None):
        site = self._site()
        self._deps("t", reads, (bank,) if start else ())
        self.ecnt["t"] += 1
        tok = ("e:t", self.ecnt["t"])

        def fn(eng, lhsT=lhsT, rhs=rhs, out=out, start=start, stop=stop,
               tp=tile_position):
            return eng.matmul(out, lhsT, rhs, start=start, stop=stop,
                              tile_position=tp)
        fn._site = site
        self.prog["t"].append(("inst", fn, tok[0], tok[1], 1))
        for k in self._expand(reads):
            self._buf(k)["r"].append(tok)
        b = self._buf(bank)
        b["w"] = tok
        if start:
            b["r"] = []
        return tok

    def transpose(self, out, in_, ident, bank, reads=()):
        site = self._site()
        self._deps("t", reads, (bank,))
        self.ecnt["t"] += 1
        tok = ("e:t", self.ecnt["t"])

        def fn(eng, out=out, in_=in_, ident=ident):
            return eng.transpose(out, in_, ident)
        fn._site = site
        self.prog["t"].append(("inst", fn, tok[0], tok[1], 1))
        for k in self._expand(reads):
            self._buf(k)["r"].append(tok)
        b = self._buf(bank)
        b["w"] = tok
        b["r"] = []
        return tok

    def dma(self, e, out_ap, in_ap, reads=(), writes=()):
        q = self.dmaq[e]
        sl = q["slots"][q["idx"]]
        q["idx"] = (q["idx"] + 1) % len(q["slots"])
        if sl["cnt"] > 0:
            self._wait(e, (sl["key"], 16 * sl["cnt"]))
        self._deps(e, reads, writes)
        sl["cnt"] += 1
        tok = (sl["key"], 16 * sl["cnt"])

        def fn(eng, out_ap=out_ap, in_ap=in_ap):
            return eng.dma_start(out=out_ap, in_=in_ap)
        self.prog[e].append(("inst", fn, tok[0], tok[1], 16))
        self._finish(e, reads, writes, tok)
        return tok

    def collective(self, kind, op, in_ap, out_ap, reads=(), writes=()):
        self._deps("g", reads, writes)
        self.cccnt += 1
        tok = ("cc", self.cccnt)

        def fn(eng, in_ap=in_ap, out_ap=out_ap, kind=kind, op=op):
            return eng.collective_compute(
                kind, op, replica_groups=[list(range(NCORES))],
                ins=[in_ap], outs=[out_ap])
        self.prog["g"].append(("inst", fn, "cc", self.cccnt, 1))
        self._finish("g", reads, writes, tok)
        return tok

    def finalize(self, block):
        sems = self.sems
        waited = {}
        for e in self.prog:
            for it in self.prog[e]:
                if it[0] == "wait" and it[1].startswith("e:"):
                    waited.setdefault(it[1], set()).add(it[2])
        rank = {}
        for semkey, vals in waited.items():
            rank[semkey] = {v: i + 1 for i, v in enumerate(sorted(vals))}

        def mk(ename):
            items = self.prog[ename]

            def emitter(eng):
                for it in items:
                    if it[0] == "wait":
                        if it[1].startswith("e:"):
                            eng.wait_ge(sems[it[1]], rank[it[1]][it[2]])
                        else:
                            eng.wait_ge(sems[it[1]], it[2])
                    else:
                        _, fn, semkey, val, delta = it
                        try:
                            r = fn(eng)
                            if semkey.startswith("e:"):
                                if val in waited.get(semkey, ()):
                                    r.then_inc(sems[semkey], delta)
                            else:
                                r.then_inc(sems[semkey], delta)
                        except Exception:
                            print("EMIT-SITE:", getattr(fn, "_site", "?"))
                            raise
            return emitter

        block.gpsimd(mk("g"))
        block.vector(mk("v"))
        block.scalar(mk("s"))
        block.tensor(mk("t"))
        block.sync(mk("q"))


def build_nc():
    nc = bass.Bass("TRN2", target_bir_lowering=False, debug=False)
    from contextlib import ExitStack
    ctx = ExitStack()
    dram = {}

    def din(name, shape, dt=F32):
        dram[name] = nc.dram_tensor(name, list(shape), dt,
                                    kind="ExternalInput").ap()
        return dram[name]

    din("x_in", (128, NSLAB, T))
    din("l0f_wfcT", (D, 4096))
    din("l0f_wprT", (4096, D))
    din("l0f_afcT", (D, 16))
    din("l0f_bfcT", (16, 4096))
    din("l0f_aprT", (4096, 16))
    din("l0f_bprT", (16, D))
    din("l0f_bfc_col", (128, 32))
    din("l0f_bpr_col", (128, 8))
    din("coremask", (128, 16))
    din("l0a_lng_col", (128, 24))
    din("l0a_lnb_col", (128, 24))
    din("l0a_wpot", (D, 2))
    din("l0a_bpot", (2, 1))
    din("l0a_wbk", (2, D))
    din("l0a_bbk_col", (128, 8))
    din("l0a_wqkv", (D, 1536))
    din("l0a_bqkv_col", (128, 12))
    din("l0a_bkv_row", (1, 1024))
    din("l0a_wout", (512, D))
    din("l0a_bout_col", (128, 8))
    din("l0a_wff1", (D, 4096))
    din("l0a_bff1_col", (128, 32))
    din("l0a_wff2", (4096, D))
    din("l0a_bff2_col", (128, 8))
    din("l1f_wgT", (D, 3072), I32)
    din("l1f_wuT", (D, 3072), I32)
    din("l1f_wdT", (3072, D), I32)
    din("l1a_wqkv", (D, 3072))
    din("l1a_wout", (D, D))
    din("l1a_lnr", (128, 16))
    din("l1a_cos", (128, T))
    din("l1a_sin", (128, T))
    din("l1a_musg", (128, 128))
    din("l1a_cm32", (64, 32))
    din("l1a_segdiag", (128, 4))
    din("l1a_beta", (128, 16))

    out_ap = nc.dram_tensor("out", [128, NSLAB, T], F32,
                            kind="ExternalOutput").ap()

    def dbg(name, src_ap, reads):
        if not KDBG:
            return
        shp = list(src_ap.shape)
        t = nc.dram_tensor("dbg_" + name, shp, F32, kind="ExternalOutput").ap()
        eng = "g" if src_ap.dtype == BF16 else "q"
        em.dma(eng, t, src_ap, reads=reads)

    wq_fc = nc.dram_tensor("wq_fc", [128, NSLAB, 4096], BF16,
                           kind="Internal").ap()
    cf_in = nc.dram_tensor("cf_in", [1, 16], F32, kind="Internal").ap()
    cf_out = nc.dram_tensor("cf_out", [NCORES, 16], F32, kind="Internal").ap()
    heb_in = nc.dram_tensor("heb_in", [65, 512], F32, kind="Internal").ap()
    heb_out = nc.dram_tensor("heb_out", [NCORES * 65, 512], F32,
                             kind="Internal").ap()
    wspill = nc.dram_tensor("wspill", [16, 64, 512], F32, kind="Internal").ap()
    crspill = nc.dram_tensor("crspill", [1, 2048], F32, kind="Internal").ap()
    gspill = nc.dram_tensor("gspill", [1, 2048], F32, kind="Internal").ap()
    xspill = nc.dram_tensor("xspill", [128, NSLAB, T], F32, kind="Internal").ap()
    vspill = nc.dram_tensor("vspill", [1, 2048], F32, kind="Internal").ap()
    wq_pr = nc.dram_tensor("wq_pr", [128, 32, D], BF16, kind="Internal").ap()
    qspill = nc.dram_tensor("qspill", [128, NSLAB, T], BF16,
                            kind="Internal").ap()
    yspill = nc.dram_tensor("yspill", [16, 128, 2048], BF16,
                            kind="Internal").ap()
    abspill = nc.dram_tensor("abspill", [16, 64, 2048], BF16,
                             kind="Internal").ap()
    mcm = nc.dram_tensor("mcm", [16, 64, 1024], F32, kind="Internal").ap()
    mca = nc.dram_tensor("mca", [16, 64, 1024], BF16, kind="Internal").ap()
    z_in = nc.dram_tensor("z_in", [128, 8], F32, kind="Internal").ap()
    z_out = nc.dram_tensor("z_out", [NCORES * 128, 8], F32,
                           kind="Internal").ap()
    mag_in = nc.dram_tensor("mag_in", [128, 1024], F32, kind="Internal").ap()
    mag_out = nc.dram_tensor("mag_out", [NCORES * 128, 1024], F32,
                             kind="Internal").ap()

    em = EM(nc, ctx)
    for _k in ("st_m", "st_r", "lngrow", "bkvrow"):
        em.aliases[_k] = ("fA",)
    for _k in ("st_q", "st_m2", "st_v", "st_sd", "st_t", "hebW",
               "cf_dre", "cf_dim", "cf_gre", "cf_gim", "cfMf", "cfMb",
               "cfg", "cfIrow", "cfI", "cfv", "cfvc", "cfME", "cftv",
               "cfsc", "cfT", "cfW", "cfD"):
        em.aliases[_k] = ("fC",)
    for _k in ("pot", "crow64"):
        em.aliases[_k] = ("fA", "fC")
    em.aliases["crsp"] = ()
    for _k in ("hctot", "hincl", "hexcl", "hcc", "hccd", "hscol", "hdecC",
               "hdtot", "hdj", "hfac", "hcd", "fD0", "fD1", "fD2", "fD3",
               "fD4"):
        em.aliases[_k] = ("fD",)
    for _c in range(1024, 1480, 4):
        em.aliases[f"cf{_c}"] = ("fC",)

    def sb(name, shape, dt):
        return ctx.enter_context(nc.sbuf_tensor(name, list(shape), dt))

    xbuf = sb("xbuf", (128, NSLAB, T), F32)
    xnbuf = sb("xnbuf", (128, NSLAB, T), BF16)
    big1 = sb("big1", (128, 32, 1024), BF16)
    bigv = big1[:, :, :].rearrange("p a c -> p (a c)")

    def bgr(c0, c1):
        return tuple(f"bg{k}" for k in range(c0 // 1024, (c1 + 1023) // 1024))
    fbuf = sb("fbuf", (128, 2112), F32)
    wb = sb("wb", (128, 6144), BF16)

    def wbr(c0, c1):
        return tuple(f"w{k}" for k in range(c0 // 512, (c1 + 511) // 512))
    scr = sb("scr", (128, 2048), BF16)
    scrf = sb("scrf", (128, 1024), F32)
    biasb = sb("biasb", (128, 168), F32)
    consts = sb("consts", (128, 16), F32)
    onesb = sb("onesb", (128, 8), BF16)
    onesrow = sb("onesrow", (65, 128), F32)
    identf = sb("identf", (128, 128), F32)
    qround = sb("qround", (128, 256), I32)
    zrow = sb("zrow", (65, 128), F32)
    etam = sb("etam", (128, 128), BF16)
    afcT_sb = sb("afcT_sb", (128, NSLAB, 16), BF16)
    loraB = sb("loraB", (16, 512), BF16)
    aprT_sb = sb("aprT_sb", (128, 32, 16), BF16)
    l1_sb = sb("l1_sb", (16, T), BF16)
    lp_sb = sb("lp_sb", (16, 1024), BF16)

    banks = [ctx.enter_context(nc.psum_tensor(f"pb{i}", [128, 512], F32))
             for i in range(7)]
    BK = [f"ps{i}" for i in range(7)]
    psb = ctx.enter_context(nc.psum_tensor("psb", [128, 1024], BF16))
    PSB = "psb"

    em.emit("g", lambda e: e.memset(onesb[:, :], 1.0), writes=("ones",))
    em.emit("g", lambda e: e.memset(consts[:, 0:1], 1e-5), writes=("c_eps",))
    em.emit("g", lambda e: e.memset(onesrow[:, :], 1.0), writes=("onesrow",))
    em.emit("g", lambda e: e.memset(identf[:, :], 0.0), writes=("identf",))
    em.emit("g", lambda e: e.affine_select(
            out=identf[:, :], in_=identf[:, :], compare_op=ALU.not_equal,
            fill=1.0, base=0, pattern=[[-1, 128]], channel_multiplier=1),
            reads=("identf",), writes=("identf",))
    em.emit("g", lambda e: e.memset(zrow[:, :], 0.0), writes=("zrow",))
    em.emit("g", lambda e: e.memset(consts[:, 1:2], 1.0), writes=("c_one",))
    em.emit("g", lambda e: e.memset(consts[:, 2:3], float(np.log(0.1))),
            writes=("c_lneta",))
    em.emit("g", lambda e: e.memset(etam[:, :], 0.1), writes=("etam",))
    em.emit("g", lambda e: e.affine_select(
            out=etam[:, :], in_=etam[:, :], compare_op=ALU.is_ge, fill=0.0,
            base=0, pattern=[[1, 128]], channel_multiplier=-1),
            reads=("etam",), writes=("etam",))
    em.dma("q", biasb[:, 40:64], dram["l0a_lnb_col"], writes=("lnb",))
    em.dma("q", biasb[:, 0:32], dram["l0f_bfc_col"], writes=("b_fc",))
    em.dma("q", biasb[:, 32:40], dram["l0f_bpr_col"], writes=("b_pr",))
    em.dma("q", xbuf[:, :, :], dram["x_in"], writes=("x",))
    em.dma("g", afcT_sb[:, :, :],
           dram["l0f_afcT"].rearrange("(a p) r -> p a r", p=128),
           writes=("afcT",))
    em.dma("g", aprT_sb[:, :, :],
           dram["l0f_aprT"].rearrange("(a p) r -> p a r", p=128),
           writes=("aprT",))

    def module_ln():
        for ti in range(NTILE):
            c0 = ti * 512
            for d in range(NSLAB):
                ra = (d % 2) * 1024
                ka, kb = f"scr{ra}", f"scr{ra + 512}"
                em.emit("v", lambda e, d=d, c0=c0, ra=ra:
                        e.tensor_copy(scr[:, ra:ra + 512],
                                      xbuf[:, d, c0:c0 + 512]),
                        reads=("x",), writes=(ka,))
                em.emit("s", lambda e, d=d, c0=c0, ra=ra:
                        e.activation(scr[:, ra + 512:ra + 1024],
                                     xbuf[:, d, c0:c0 + 512], AF.Square),
                        reads=("x",), writes=(kb,))
                em.mm(onesb[:, 0:1], scr[:, ra:ra + 512], banks[4][0:1, :],
                      BK[4], start=(d == 0), stop=(d == NSLAB - 1),
                      reads=(ka, "ones"))
                em.mm(onesb[:, 1:2], scr[:, ra + 512:ra + 1024],
                      banks[5][0:1, :], BK[5], start=(d == 0),
                      stop=(d == NSLAB - 1), reads=(kb, "ones"))
            em.emit("v", lambda e: e.tensor_scalar_mul(
                    fbuf[0:1, 0:512], banks[4][0:1, :], 1.0 / D),
                    reads=(BK[4],), writes=("st_m", "fA"))
            em.emit("v", lambda e: e.tensor_scalar_mul(
                    fbuf[0:1, 1024:1536], banks[5][0:1, :], 1.0 / D),
                    reads=(BK[5],), writes=("st_q", "fC"))
            em.emit("v", lambda e: e.tensor_mul(
                    fbuf[0:1, 1536:2048], fbuf[0:1, 0:512], fbuf[0:1, 0:512]),
                    reads=("st_m",), writes=("st_m2", "fC"))
            em.emit("v", lambda e: e.tensor_sub(
                    fbuf[0:1, 1024:1536], fbuf[0:1, 1024:1536],
                    fbuf[0:1, 1536:2048]),
                    reads=("st_q", "st_m2"), writes=("st_v", "fC"))
            em.emit("s", lambda e: e.activation(
                    fbuf[0:1, 1024:1536], fbuf[0:1, 1024:1536], AF.Sqrt,
                    bias=consts[0:1, 0:1]),
                    reads=("st_v", "c_eps"), writes=("st_sd", "fC"))
            em.emit("v", lambda e: e.reciprocal(
                    fbuf[0:1, 512:1024], fbuf[0:1, 1024:1536]),
                    reads=("st_sd",), writes=("st_r", "fA"))
            em.mm(onesrow[0:1, :], fbuf[0:1, 0:512], banks[6][:, :],
                  BK[6], start=True, stop=True, reads=("st_m", "onesrow"))
            em.mm(onesrow[0:1, :], fbuf[0:1, 512:1024], banks[3][:, :],
                  BK[3], start=True, stop=True, reads=("st_r", "onesrow"))
            for d in range(NSLAB):
                em.emit("v", lambda e, d=d, c0=c0:
                        e.tensor_sub(scrf[:, 0:512], xbuf[:, d, c0:c0 + 512],
                                     banks[6][:, :]),
                        reads=("x", BK[6]), writes=("rf0",))
                em.emit("v", lambda e, d=d, c0=c0:
                        e.tensor_mul(xnbuf[:, d, c0:c0 + 512], scrf[:, 0:512],
                                     banks[3][:, :]),
                        reads=("rf0", BK[3]), writes=(f"xn:{d}",))

    def weight_maxabs(name, srcT, rows, cols, qinv_key, qs_key):
        view = srcT.rearrange("(a p) c -> a p c", p=128)
        em.emit("g", lambda e: e.memset(fbuf[:, 2048:2049], 0.0),
                writes=("fD0",))
        for a in range(rows // 128):
            for cc in range(cols // 1024):
                reg = (a * (cols // 1024) + cc) % 2
                fo = reg * 1024
                wkey = "fA" if reg == 0 else "fC"
                em.dma("q", fbuf[:, fo:fo + 1024],
                       view[a, :, cc * 1024:(cc + 1) * 1024], writes=(wkey,))
                em.emit("v", lambda e, fo=fo, reg=reg: e.tensor_reduce(
                        fbuf[:, 2050 + reg:2051 + reg], fbuf[:, fo:fo + 1024],
                        AX.X, ALU.max, apply_absolute_value=True),
                        reads=(wkey,), writes=(f"fD{2 + reg}",))
                em.emit("v", lambda e, reg=reg: e.tensor_max(
                        fbuf[:, 2048:2049], fbuf[:, 2048:2049],
                        fbuf[:, 2050 + reg:2051 + reg]),
                        reads=(f"fD{2 + reg}", "fD0"), writes=("fD0",))
        em.transpose(banks[6][0:1, 0:128], fbuf[:, 2048:2049], identf[:, :],
                     BK[6], reads=("fD0", "identf"))
        em.emit("v", lambda e: e.reduce_max(
                fbuf[0:1, 2052:2053], banks[6][0:1, 0:128], AX.X),
                reads=(BK[6],), writes=("fD4",))
        em.mm(onesrow[0:1, :], fbuf[0:1, 2052:2053], banks[6][:, 0:1], BK[6],
              start=True, stop=True, reads=("fD4", "onesrow"))
        em.emit("v", lambda e: e.tensor_copy(fbuf[:, 2048:2049],
                                             banks[6][:, 0:1]),
                reads=(BK[6],), writes=("fD0",))
        em.emit("v", lambda e: e.reciprocal(fbuf[:, 2049:2050],
                                            fbuf[:, 2048:2049]),
                reads=("fD0",), writes=("fD1",))
        ic, sc = QCOL[qinv_key], QCOL[qs_key]
        em.emit("v", lambda e, ic=ic: e.tensor_scalar_mul(
                consts[:, ic:ic + 1], fbuf[:, 2049:2050], 127.0),
                reads=("fD1",), writes=(qinv_key,))
        em.emit("v", lambda e, sc=sc: e.tensor_scalar_mul(
                consts[:, sc:sc + 1], fbuf[:, 2048:2049], 1.0 / 127.0),
                reads=("fD0",), writes=(qs_key,))

    QCOL = {"qinv_fc": 4, "qs_fc": 5, "qinv_pr": 6, "qs_pr": 7}

    def quant_chunk(src_key, out_bf_ap, qinv_key, qs_key, wkeys, eng="v"):
        ic, sc = QCOL[qinv_key], QCOL[qs_key]
        em.emit(eng, lambda e, ic=ic: e.tensor_scalar(
                fbuf[:, 1024:2048], fbuf[:, 0:1024],
                consts[:, ic:ic + 1], MAGIC, ALU.mult, ALU.add),
                reads=(src_key, qinv_key), writes=("fC",))
        em.emit(eng, lambda e: e.tensor_scalar_add(
                fbuf[:, 1024:2048], fbuf[:, 1024:2048], -MAGIC),
                reads=("fC",), writes=("fC",))
        for hh in range(4):
            o0 = hh * 256
            em.emit(eng, lambda e, o0=o0: e.tensor_copy(
                    qround[:, :], fbuf[:, 1024 + o0:1280 + o0]),
                    reads=("fC",), writes=("qint",))
            em.emit(eng, lambda e, sc=sc, o0=o0, out_bf_ap=out_bf_ap:
                    e.tensor_scalar_mul(out_bf_ap[:, o0:o0 + 256],
                                        qround[:, :], consts[:, sc:sc + 1]),
                    reads=("qint", qs_key), writes=wkeys)

    wfcT_pac = dram["l0f_wfcT"].rearrange("(a p) c -> p a c", p=128)
    wprT_pac = dram["l0f_wprT"].rearrange("(a p) c -> p a c", p=128)

    def l0_ffn(first_call):
        if first_call:
            weight_maxabs("fc", dram["l0f_wfcT"], D, 4096, "qinv_fc", "qs_fc")
            weight_maxabs("pr", dram["l0f_wprT"], 4096, D, "qinv_pr", "qs_pr")
        for ti in range(NTILE):
            c0 = ti * 512
            for d in range(NSLAB):
                em.mm(afcT_sb[:, d, :], xnbuf[:, d, c0:c0 + 512],
                      banks[6][0:16, :], BK[6], start=(d == 0),
                      stop=(d == NSLAB - 1), reads=("afcT", f"xn:{d}"))
            em.emit("s", lambda e, c0=c0: e.activation(
                    l1_sb[:, c0:c0 + 512], banks[6][0:16, :], AF.Copy,
                    scale=2.0),
                    reads=(BK[6],), writes=("l1",))
        for half in range(2):
            h0 = half * 1024
            for c in range(32):
                w0 = (c % 2) * 1024
                wkeys = wbr(w0, w0 + 1024)
                if first_call and half == 0:
                    em.dma("q", fbuf[:, 0:1024]
                           .rearrange("p (a c) -> p a c", c=128),
                           wfcT_pac[:, :, c * 128:(c + 1) * 128],
                           writes=("fA",))
                    quant_chunk("fA", wb[:, w0:w0 + 1024], "qinv_fc", "qs_fc",
                                wkeys)
                    em.dma("q", wq_fc[:, :, c * 128:(c + 1) * 128],
                           wb[:, w0:w0 + 1024]
                           .rearrange("p (a c) -> p a c", c=128),
                           reads=wkeys, writes=(f"wqfcd:{c}",))
                else:
                    em.dma("q", wb[:, w0:w0 + 1024]
                           .rearrange("p (a c) -> p a c", c=128),
                           wq_fc[:, :, c * 128:(c + 1) * 128],
                           reads=(f"wqfcd:{c}",), writes=wkeys)
                bslot = c % 4
                bkey = f"lB{bslot}"
                em.dma("g", loraB[:, bslot * 128:(bslot + 1) * 128],
                       dram["l0f_bfcT"][:, c * 128:(c + 1) * 128],
                       writes=(bkey,))
                wv = wb[:, w0:w0 + 1024].rearrange("p (a c) -> p a c", c=128)
                for tt in range(2):
                    c0 = h0 + tt * 512
                    bank = tt + 2 * (c % 2)
                    for d in range(NSLAB):
                        em.mm(wv[:, d, :], xnbuf[:, d, c0:c0 + 512],
                              banks[bank][:, :], BK[bank], start=(d == 0),
                              stop=False, reads=wkeys + (f"xn:{d}",))
                    em.mm(loraB[:, bslot * 128:(bslot + 1) * 128],
                          l1_sb[:, c0:c0 + 512], banks[bank][:, :], BK[bank],
                          start=False, stop=True, reads=(bkey, "l1"))
                    em.emit("s", lambda e, c=c, tt=tt, bank=bank:
                            e.activation(big1[:, c, tt * 512:tt * 512 + 512],
                                         banks[bank][:, :], AF.Gelu,
                                         bias=biasb[:, c:c + 1]),
                            reads=(BK[bank], "b_fc"), writes=(f"bg{c}",))
            for tt in range(2):
                c0 = tt * 512
                for f in range(32):
                    em.mm(aprT_sb[:, f, :], big1[:, f, c0:c0 + 512],
                          banks[6][0:16, :], BK[6], start=(f == 0),
                          stop=(f == 31), reads=("aprT", f"bg{f}"))
                em.emit("s", lambda e, c0=c0: e.activation(
                        lp_sb[:, c0:c0 + 512],
                        banks[6][0:16, :], AF.Copy, scale=2.0),
                        reads=(BK[6],), writes=("lp",))
            for od in range(8):
                bslot = od % 4
                bkey = f"lB{bslot}"
                em.dma("g", loraB[:, bslot * 128:(bslot + 1) * 128],
                       dram["l0f_bprT"][:, od * 128:(od + 1) * 128],
                       writes=(bkey,))
                gslots = [2048 + ((od * 4 + g) % 4) * 1024 for g in range(4)]
                gkeys = [wbr(s, s + 1024) for s in gslots]
                if first_call and half == 0:
                    for g in range(4):
                        em.dma("q", fbuf[:, 0:1024]
                               .rearrange("p (a c) -> p a c", c=128),
                               wprT_pac[:, g * 8:(g + 1) * 8,
                                        od * 128:(od + 1) * 128],
                               writes=("fA",))
                        quant_chunk("fA",
                                    wb[:, gslots[g]:gslots[g] + 1024],
                                    "qinv_pr", "qs_pr", gkeys[g])
                        em.dma("q", wq_pr[:, g * 8:(g + 1) * 8,
                                          od * 128:(od + 1) * 128],
                               wb[:, gslots[g]:gslots[g] + 1024]
                               .rearrange("p (a c) -> p a c", c=128),
                               reads=gkeys[g], writes=(f"wqprd:{od}:{g}",))
                else:
                    for g in range(4):
                        em.dma("q", wb[:, gslots[g]:gslots[g] + 1024]
                               .rearrange("p (a c) -> p a c", c=128),
                               wq_pr[:, g * 8:(g + 1) * 8,
                                     od * 128:(od + 1) * 128],
                               reads=(f"wqprd:{od}:{g}",),
                               writes=gkeys[g])
                for tt in range(2):
                    c0 = tt * 512
                    bank = tt + 2 * (od % 2)
                    for f in range(32):
                        g = f // 8
                        wv = wb[:, gslots[g]:gslots[g] + 1024].rearrange(
                            "p (a c) -> p a c", c=128)
                        em.mm(wv[:, f % 8, :], big1[:, f, c0:c0 + 512],
                              banks[bank][:, :], BK[bank], start=(f == 0),
                              stop=False, reads=gkeys[g] + (f"bg{f}",))
                    em.mm(loraB[:, bslot * 128:(bslot + 1) * 128],
                          lp_sb[:, c0:c0 + 512],
                          banks[bank][:, :], BK[bank],
                          start=False, stop=True, reads=(bkey, "lp"))
                    g0 = h0 + c0
                    em.emit("s", lambda e, bank=bank, od=od, tt=tt:
                            e.activation(scrf[:, tt * 512:tt * 512 + 512],
                                         banks[bank][:, :], AF.Identity,
                                         bias=biasb[:, 32 + od:33 + od]),
                            reads=(BK[bank], "b_pr"), writes=(f"rf{tt}",))
                    em.emit("v", lambda e, od=od, g0=g0, tt=tt:
                            e.scalar_tensor_tensor(
                                xbuf[:, od, g0:g0 + 512],
                                scrf[:, tt * 512:tt * 512 + 512], 0.5,
                                xbuf[:, od, g0:g0 + 512], ALU.mult, ALU.add),
                            reads=(f"rf{tt}", "x"), writes=("x",))

    wg_pac = dram["l1f_wgT"].rearrange("(a p) c -> p a c", p=128)
    wu_pac = dram["l1f_wuT"].rearrange("(a p) c -> p a c", p=128)
    wd_pac = dram["l1f_wdT"].rearrange("(a p) c -> p a c", p=128)

    def l1_ffn():
        for half in range(2):
            h0 = half * 1024
            for c in range(24):
                g0 = (c % 2) * 2048
                gk, uk = wbr(g0, g0 + 1024), wbr(g0 + 1024, g0 + 2048)
                em.dma("g", wb[:, g0:g0 + 1024]
                       .rearrange("p (a c) -> p a c", c=128),
                       wg_pac[:, :, c * 128:(c + 1) * 128], writes=gk)
                em.dma("g", wb[:, g0 + 1024:g0 + 2048]
                       .rearrange("p (a c) -> p a c", c=128),
                       wu_pac[:, :, c * 128:(c + 1) * 128], writes=uk)
                gv = wb[:, g0:g0 + 1024].rearrange("p (a c) -> p a c", c=128)
                uv = wb[:, g0 + 1024:g0 + 2048].rearrange("p (a c) -> p a c", c=128)
                for tt in range(2):
                    c0 = h0 + tt * 512
                    bg = tt + 2 * (c % 2)
                    bu = 4 + tt
                    for d in range(NSLAB):
                        em.mm(gv[:, d, :], xnbuf[:, d, c0:c0 + 512],
                              banks[bg][:, :], BK[bg], start=(d == 0),
                              stop=(d == NSLAB - 1), reads=gk + (f"xn:{d}",))
                    for d in range(NSLAB):
                        em.mm(uv[:, d, :], xnbuf[:, d, c0:c0 + 512],
                              banks[bu][:, :], BK[bu], start=(d == 0),
                              stop=(d == NSLAB - 1), reads=uk + (f"xn:{d}",))
                    ra = (c % 2) * 1024 + tt * 512
                    em.emit("s", lambda e, bg=bg, ra=ra: e.activation(
                            scr[:, ra:ra + 512], banks[bg][:, :], AF.Silu,
                            scale=1.0 / 127.0),
                            reads=(BK[bg],), writes=(f"scr{ra}",))
                    em.emit("v", lambda e, c=c, tt=tt, bu=bu, ra=ra:
                            e.tensor_mul(big1[:, c, tt * 512:tt * 512 + 512],
                                         scr[:, ra:ra + 512],
                                         banks[bu][:, :]),
                            reads=(f"scr{ra}", BK[bu]), writes=(f"bg{c}",))
            for od in range(8):
                d0 = (od % 2) * 3072
                dk = wbr(d0, d0 + 3072)
                em.dma("g", wb[:, d0:d0 + 3072]
                       .rearrange("p (a c) -> p a c", c=128),
                       wd_pac[:, :, od * 128:(od + 1) * 128], writes=dk)
                dv = wb[:, d0:d0 + 3072].rearrange("p (a c) -> p a c", c=128)
                for tt in range(2):
                    c0 = tt * 512
                    bank = tt + 2 * (od % 2)
                    for f in range(24):
                        em.mm(dv[:, f, :], big1[:, f, c0:c0 + 512],
                              banks[bank][:, :], BK[bank], start=(f == 0),
                              stop=(f == 23), reads=dk + (f"bg{f}",))
                    g0 = h0 + c0
                    em.emit("s", lambda e, bank=bank, tt=tt: e.activation(
                            scrf[:, tt * 512:tt * 512 + 512],
                            banks[bank][:, :], AF.Copy,
                            scale=1.0 / (127.0 * 127.0)),
                            reads=(BK[bank],), writes=(f"rf{tt}",))
                    em.emit("v", lambda e, od=od, g0=g0, tt=tt:
                            e.scalar_tensor_tensor(
                                xbuf[:, od, g0:g0 + 512],
                                scrf[:, tt * 512:tt * 512 + 512], 0.5,
                                xbuf[:, od, g0:g0 + 512], ALU.mult, ALU.add),
                            reads=(f"rf{tt}", "x"), writes=("x",))

    DT_ = 0.01
    QFM0, KFM0, KTOK0, VTOK0 = 0, 8192, 16384, 24576

    cmask = sb("cmask", (128, 16), F32)
    lngcol = sb("lngcol", (128, 24), F32)
    em.dma("q", lngcol[:, :], dram["l0a_lng_col"], writes=("lngcol",))
    em.dma("q", cmask[:, :], dram["coremask"], writes=("cmask",))

    def ln_affine(which, composed, write_y=True):
        b0 = 40 + which * 8
        g0 = which * 8
        for ti in range(NTILE):
            c0 = ti * 512
            for d in range(NSLAB):
                ra = (d % 2) * 1024
                ka, kb = f"scr{ra}", f"scr{ra + 512}"
                em.emit("v", lambda e, d=d, c0=c0, ra=ra:
                        e.tensor_copy(scr[:, ra:ra + 512],
                                      xbuf[:, d, c0:c0 + 512]),
                        reads=("x",), writes=(ka,))
                em.emit("s", lambda e, d=d, c0=c0, ra=ra:
                        e.activation(scr[:, ra + 512:ra + 1024],
                                     xbuf[:, d, c0:c0 + 512], AF.Square),
                        reads=("x",), writes=(kb,))
                em.mm(onesb[:, 0:1], scr[:, ra:ra + 512], banks[4][0:1, :],
                      BK[4], start=(d == 0), stop=(d == NSLAB - 1),
                      reads=(ka, "ones"))
                em.mm(onesb[:, 1:2], scr[:, ra + 512:ra + 1024],
                      banks[5][0:1, :], BK[5], start=(d == 0),
                      stop=(d == NSLAB - 1), reads=(kb, "ones"))
            em.emit("v", lambda e: e.tensor_scalar_mul(
                    fbuf[0:1, 0:512], banks[4][0:1, :], 1.0 / D),
                    reads=(BK[4],), writes=("st_m", "fA"))
            em.emit("v", lambda e: e.tensor_scalar_mul(
                    fbuf[0:1, 1024:1536], banks[5][0:1, :], 1.0 / D),
                    reads=(BK[5],), writes=("st_q", "fC"))
            em.emit("v", lambda e: e.tensor_mul(
                    fbuf[0:1, 1536:2048], fbuf[0:1, 0:512], fbuf[0:1, 0:512]),
                    reads=("st_m",), writes=("st_m2", "fC"))
            em.emit("v", lambda e: e.tensor_sub(
                    fbuf[0:1, 1024:1536], fbuf[0:1, 1024:1536],
                    fbuf[0:1, 1536:2048]),
                    reads=("st_q", "st_m2"), writes=("st_v", "fC"))
            em.emit("s", lambda e: e.activation(
                    fbuf[0:1, 1536:2048], fbuf[0:1, 1024:1536], AF.Sqrt,
                    bias=consts[0:1, 0:1]),
                    reads=("st_v", "c_eps"), writes=("st_sd", "fC"))
            em.emit("v", lambda e: e.reciprocal(
                    fbuf[0:1, 512:1024], fbuf[0:1, 1536:2048]),
                    reads=("st_sd",), writes=("st_r", "fA"))
            if composed:
                em.emit("v", lambda e: e.tensor_mul(
                        fbuf[0:1, 1536:2048], fbuf[0:1, 1024:1536],
                        fbuf[0:1, 512:1024]),
                        reads=("st_v", "st_r"), writes=("st_t", "fC"))
                em.emit("v", lambda e: e.tensor_mul(
                        fbuf[0:1, 1536:2048], fbuf[0:1, 1536:2048],
                        fbuf[0:1, 512:1024]),
                        reads=("st_t", "st_r"), writes=("st_t", "fC"))
                em.emit("s", lambda e: e.activation(
                        fbuf[0:1, 1536:2048], fbuf[0:1, 1536:2048], AF.Sqrt,
                        bias=consts[0:1, 0:1]),
                        reads=("st_t", "c_eps"), writes=("st_t", "fC"))
                em.emit("v", lambda e: e.reciprocal(
                        fbuf[0:1, 1536:2048], fbuf[0:1, 1536:2048]),
                        reads=("st_t",), writes=("st_t", "fC"))
                if write_y:
                    em.mm(onesrow[0:1, :], fbuf[0:1, 512:1024], banks[5][:, :],
                          BK[5], start=True, stop=True,
                          reads=("st_r", "onesrow"))
                em.emit("v", lambda e: e.tensor_mul(
                        fbuf[0:1, 512:1024], fbuf[0:1, 512:1024],
                        fbuf[0:1, 1536:2048]),
                        reads=("st_r", "st_t"), writes=("st_r", "fA"))
            em.mm(onesrow[0:1, :], fbuf[0:1, 0:512], banks[6][:, :],
                  BK[6], start=True, stop=True, reads=("st_m", "onesrow"))
            em.mm(onesrow[0:1, :], fbuf[0:1, 512:1024], banks[3][:, :],
                  BK[3], start=True, stop=True, reads=("st_r", "onesrow"))
            for d in range(NSLAB):
                em.emit("v", lambda e, d=d, c0=c0:
                        e.tensor_sub(scrf[:, 0:512], xbuf[:, d, c0:c0 + 512],
                                     banks[6][:, :]),
                        reads=("x", BK[6]), writes=("rf0",))
                em.emit("v", lambda e, d=d, c0=c0:
                        e.tensor_mul(scrf[:, 512:1024], scrf[:, 0:512],
                                     banks[3][:, :]),
                        reads=("rf0", BK[3]), writes=("rf1",))
                if which == 3:
                    em.emit("s", lambda e, d=d, c0=c0:
                            e.activation(xnbuf[:, d, c0:c0 + 512],
                                         scrf[:, 512:1024], AF.Identity,
                                         bias=l1ln_sb[:, 8 + d:9 + d],
                                         scale=l1ln_sb[:, d:d + 1]),
                            reads=("rf1", "l1ln"), writes=(f"xn:{d}",))
                else:
                    em.emit("s", lambda e, d=d, c0=c0:
                            e.activation(xnbuf[:, d, c0:c0 + 512],
                                         scrf[:, 512:1024], AF.Identity,
                                         bias=biasb[:, b0 + d:b0 + d + 1],
                                         scale=lngcol[:, g0 + d:g0 + d + 1]),
                            reads=("rf1", "lnb", "lngcol"), writes=(f"xn:{d}",))
                if composed and write_y:
                    em.emit("v", lambda e, d=d, c0=c0:
                            e.tensor_mul(xbuf[:, d, c0:c0 + 512],
                                         scrf[:, 0:512], banks[5][:, :]),
                            reads=("rf0", BK[5]), writes=("x",))

    CFK = ("fC",)

    def cfe(eng, fn, reads, writes):
        em.emit(eng, fn, reads=tuple(reads) + CFK, writes=tuple(writes) + CFK)

    def cvw(base, part=(0, 128)):
        g = fbuf[part[0]:part[1], base:base + 4].rearrange(
            "p (c two) -> p c two", two=2)
        return g[:, :, 0:1], g[:, :, 1:2]

    def cf_pass1(eng, Mcols, s_order, rpool, tmp1, tmp2, mkey):
        k1, k2, kf = [f"cf{b}" for b in rpool]
        cur1, cur2, free = rpool
        cfe(eng, lambda e, c=cur1: e.memset(fbuf[:, c:c + 4], 0.0), (), (f"cf{cur1}",))
        cfe(eng, lambda e, c=cur1: e.memset(fbuf[:, c:c + 1], 1.0),
            (f"cf{cur1}",), (f"cf{cur1}",))
        cfe(eng, lambda e, c=cur2: e.memset(fbuf[:, c:c + 4], 0.0), (), (f"cf{cur2}",))
        cfe(eng, lambda e, c=cur2: e.memset(fbuf[:, c + 2:c + 3], 1.0),
            (f"cf{cur2}",), (f"cf{cur2}",))
        for s in s_order:
            dsre = fbuf[:, 1024 + s:1025 + s]
            dsim = fbuf[:, 1040 + s:1041 + s]
            c1re, c1im = cvw(cur1)
            c2re, c2im = cvw(cur2)
            nre, nim = cvw(free)
            t1re, t1im = cvw(tmp1)
            kc1, kc2, kn = f"cf{cur1}", f"cf{cur2}", f"cf{free}"
            kt = f"cf{tmp1}"
            cfe(eng, lambda e, a=t1re, b=c1re, d=dsre: e.tensor_scalar_mul(a, b, d),
                (kc1, "cf_dre"), (kt,))
            cfe(eng, lambda e, a=t1im, b=c1im, d=dsim: e.tensor_scalar_mul(a, b, d),
                (kc1, "cf_dim"), (kt,))
            cfe(eng, lambda e, a=t1re, b=t1im: e.tensor_sub(a, a, b), (kt,), (kt,))
            cfe(eng, lambda e, a=nre, b=t1re, c=c2re: e.tensor_sub(a, b, c),
                (kt, kc2), (kn,))
            cfe(eng, lambda e, a=t1re, b=c1re, d=dsim: e.tensor_scalar_mul(a, b, d),
                (kc1, "cf_dim"), (kt,))
            cfe(eng, lambda e, a=t1im, b=c1im, d=dsre: e.tensor_scalar_mul(a, b, d),
                (kc1, "cf_dre"), (kt,))
            cfe(eng, lambda e, a=t1re, b=t1im: e.tensor_add(a, a, b), (kt,), (kt,))
            cfe(eng, lambda e, a=nim, b=t1re, c=c2im: e.tensor_sub(a, b, c),
                (kt, kc2), (kn,))
            cur1, cur2, free = free, cur1, cur2
        kc1, kc2 = f"cf{cur1}", f"cf{cur2}"
        cfe(eng, lambda e, c=cur1, M=Mcols: e.tensor_copy(
            fbuf[:, M:M + 4], fbuf[:, c:c + 4]), (kc1,), (mkey,))
        cfe(eng, lambda e, c=cur2, M=Mcols: e.tensor_copy(
            fbuf[:, M + 4:M + 8], fbuf[:, c:c + 4]), (kc2,), (mkey,))
        cfe("v", lambda e, M=Mcols, t=tmp1: e.tensor_reduce(
            fbuf[:, t:t + 1], fbuf[:, M:M + 8], AX.X, ALU.max,
            apply_absolute_value=True), (mkey,), (f"cf{tmp1}",))
        cfe("v", lambda e, t=tmp1: e.reciprocal(
            fbuf[:, t + 1:t + 2], fbuf[:, t:t + 1]),
            (f"cf{tmp1}",), (f"cf{tmp1}",))
        cfe(eng, lambda e, M=Mcols, t=tmp1: e.tensor_scalar_mul(
            fbuf[:, M:M + 8], fbuf[:, M:M + 8], fbuf[:, t + 1:t + 2]),
            (mkey, f"cf{tmp1}"), (mkey,))

    def cf_compose(eng, Pc, Sc, Cc, tmp1, tmp2):
        kP, kS, kC = f"cf{Pc}", f"cf{Sc}", f"cf{Cc}"
        kt = f"cf{tmp1}"
        P = fbuf[:, Pc:Pc + 8]
        for row in range(2):
            a1re = P[:, 4 * row + 0:4 * row + 1]
            a1im = P[:, 4 * row + 1:4 * row + 2]
            a2re = P[:, 4 * row + 2:4 * row + 3]
            a2im = P[:, 4 * row + 3:4 * row + 4]
            Sg = fbuf[:, Sc:Sc + 8].rearrange("p (r c two) -> p r c two",
                                              r=2, two=2)
            b1re, b1im = Sg[:, 0, :, 0:1], Sg[:, 0, :, 1:2]
            b2re, b2im = Sg[:, 1, :, 0:1], Sg[:, 1, :, 1:2]
            Cg = fbuf[:, Cc:Cc + 8].rearrange("p (r c two) -> p r c two",
                                              r=2, two=2)
            cre, cim = Cg[:, row, :, 0:1], Cg[:, row, :, 1:2]
            t1re, t1im = cvw(tmp1)
            t2re, t2im = cvw(tmp2)
            cfe(eng, lambda e, o=t1re, x=b1re, s=a1re: e.tensor_scalar_mul(o, x, s), (kP, kS), (kt,))
            cfe(eng, lambda e, o=t2re, x=b1im, s=a1im: e.tensor_scalar_mul(o, x, s), (kP, kS), (f"cf{tmp2}",))
            cfe(eng, lambda e, o=t1re, x=t2re: e.tensor_sub(o, o, x), (kt, f"cf{tmp2}"), (kt,))
            cfe(eng, lambda e, o=t2re, x=b2re, s=a2re: e.tensor_scalar_mul(o, x, s), (kP, kS), (f"cf{tmp2}",))
            cfe(eng, lambda e, o=t1re, x=t2re: e.tensor_add(o, o, x), (kt, f"cf{tmp2}"), (kt,))
            cfe(eng, lambda e, o=t2re, x=b2im, s=a2im: e.tensor_scalar_mul(o, x, s), (kP, kS), (f"cf{tmp2}",))
            cfe(eng, lambda e, o=cre, x=t1re, y=t2re: e.tensor_sub(o, x, y), (kt, f"cf{tmp2}"), (kC,))
            cfe(eng, lambda e, o=t1re, x=b1im, s=a1re: e.tensor_scalar_mul(o, x, s), (kP, kS), (kt,))
            cfe(eng, lambda e, o=t2re, x=b1re, s=a1im: e.tensor_scalar_mul(o, x, s), (kP, kS), (f"cf{tmp2}",))
            cfe(eng, lambda e, o=t1re, x=t2re: e.tensor_add(o, o, x), (kt, f"cf{tmp2}"), (kt,))
            cfe(eng, lambda e, o=t2re, x=b2im, s=a2re: e.tensor_scalar_mul(o, x, s), (kP, kS), (f"cf{tmp2}",))
            cfe(eng, lambda e, o=t1re, x=t2re: e.tensor_add(o, o, x), (kt, f"cf{tmp2}"), (kt,))
            cfe(eng, lambda e, o=t2re, x=b2re, s=a2im: e.tensor_scalar_mul(o, x, s), (kP, kS), (f"cf{tmp2}",))
            cfe(eng, lambda e, o=cim, x=t1re, y=t2re: e.tensor_add(o, x, y), (kt, f"cf{tmp2}"), (kC,))

    def cf_rescale(eng, Pc, tmp1):
        cfe("v", lambda e, P=Pc, t=tmp1: e.tensor_reduce(
            fbuf[:, t:t + 1], fbuf[:, P:P + 8], AX.X, ALU.max,
            apply_absolute_value=True), (f"cf{Pc}",), (f"cf{tmp1}",))
        cfe("v", lambda e, t=tmp1: e.reciprocal(
            fbuf[:, t + 1:t + 2], fbuf[:, t:t + 1]),
            (f"cf{tmp1}",), (f"cf{tmp1}",))
        cfe(eng, lambda e, P=Pc, t=tmp1: e.tensor_scalar_mul(
            fbuf[:, P:P + 8], fbuf[:, P:P + 8], fbuf[:, t + 1:t + 2]),
            (f"cf{Pc}", f"cf{tmp1}"), (f"cf{Pc}",))

    def cf_ident_fill(rows, Sc):
        r0, r1 = rows
        if r1 <= r0:
            return
        em.dma("q", fbuf[r0:r1, Sc:Sc + 8], fbuf[0:r1 - r0, 1464:1472],
               reads=("cfIrow", "fC"), writes=(f"cf{Sc}", "fC"))

    def cf_hs(eng, Mcols, Pc, Shc, P2c, tmp1, tmp2, up, mkey):
        cfe(eng, lambda e, M=Mcols, P=Pc: e.tensor_copy(
            fbuf[:, P:P + 8], fbuf[:, M:M + 8]), (mkey,), (f"cf{Pc}",))
        cur, alt = Pc, P2c
        for k in (1, 2, 4, 8, 16, 32, 64):
            if not up:
                em.dma("q", fbuf[k:128, Shc:Shc + 8], fbuf[0:128 - k, cur:cur + 8],
                       reads=(f"cf{cur}", "fC"), writes=(f"cf{Shc}", "fC"))
                cf_ident_fill((0, k), Shc)
            else:
                em.dma("q", fbuf[0:128 - k, Shc:Shc + 8], fbuf[k:128, cur:cur + 8],
                       reads=(f"cf{cur}", "fC"), writes=(f"cf{Shc}", "fC"))
                cf_ident_fill((128 - k, 128), Shc)
            cf_compose(eng, cur, Shc, alt, tmp1, tmp2)
            cf_rescale(eng, alt, tmp1)
            cur, alt = alt, cur
        return cur

    def cf_mobius_bnd(eng, Pexc, vrow, outc, tmp1, tmp2):
        em.mm(onesrow[0:1, :], fbuf[0:1, vrow:vrow + 4], banks[3][:, 0:4],
              BK[3], start=True, stop=True, reads=("cfv", "onesrow", "fC"))
        cfe(eng, lambda e: e.tensor_copy(fbuf[:, 1412:1416], banks[3][:, 0:4]),
            (BK[3],), ("cfvc",))
        pre = fbuf[:, 1412:1413]
        pim = fbuf[:, 1413:1414]
        qre = fbuf[:, 1414:1415]
        qim = fbuf[:, 1415:1416]
        P = fbuf[:, Pexc:Pexc + 8]
        t = fbuf[:, tmp1:tmp1 + 4]
        u = fbuf[:, tmp2:tmp2 + 4]
        kP, kt, ku = f"cf{Pexc}", f"cf{tmp1}", f"cf{tmp2}"
        for half, oc in ((0, tmp1), (1, tmp2)):
            kk = f"cf{oc}"
            e1re = P[:, 4 * half + 0:4 * half + 1]
            e1im = P[:, 4 * half + 1:4 * half + 2]
            e2re = P[:, 4 * half + 2:4 * half + 3]
            e2im = P[:, 4 * half + 3:4 * half + 4]
            o = fbuf[:, oc:oc + 4]
            cfe(eng, lambda e, o=o, a=e1re, b=pre: e.tensor_mul(o[:, 0:1], a, b), (kP, "cfvc"), (kk,))
            cfe(eng, lambda e, o=o, a=e1im, b=pim: e.tensor_mul(o[:, 2:3], a, b), (kP, "cfvc"), (kk,))
            cfe(eng, lambda e, o=o: e.tensor_sub(o[:, 0:1], o[:, 0:1], o[:, 2:3]), (kk,), (kk,))
            cfe(eng, lambda e, o=o, a=e2re, b=qre: e.tensor_mul(o[:, 2:3], a, b), (kP, "cfvc"), (kk,))
            cfe(eng, lambda e, o=o: e.tensor_add(o[:, 0:1], o[:, 0:1], o[:, 2:3]), (kk,), (kk,))
            cfe(eng, lambda e, o=o, a=e2im, b=qim: e.tensor_mul(o[:, 2:3], a, b), (kP, "cfvc"), (kk,))
            cfe(eng, lambda e, o=o: e.tensor_sub(o[:, 0:1], o[:, 0:1], o[:, 2:3]), (kk,), (kk,))
            cfe(eng, lambda e, o=o, a=e1re, b=pim: e.tensor_mul(o[:, 1:2], a, b), (kP, "cfvc"), (kk,))
            cfe(eng, lambda e, o=o, a=e1im, b=pre: e.tensor_mul(o[:, 2:3], a, b), (kP, "cfvc"), (kk,))
            cfe(eng, lambda e, o=o: e.tensor_add(o[:, 1:2], o[:, 1:2], o[:, 2:3]), (kk,), (kk,))
            cfe(eng, lambda e, o=o, a=e2re, b=qim: e.tensor_mul(o[:, 2:3], a, b), (kP, "cfvc"), (kk,))
            cfe(eng, lambda e, o=o: e.tensor_add(o[:, 1:2], o[:, 1:2], o[:, 2:3]), (kk,), (kk,))
            cfe(eng, lambda e, o=o, a=e2im, b=qre: e.tensor_mul(o[:, 2:3], a, b), (kP, "cfvc"), (kk,))
            cfe(eng, lambda e, o=o: e.tensor_add(o[:, 1:2], o[:, 1:2], o[:, 2:3]), (kk,), (kk,))
        cfe(eng, lambda e: e.tensor_mul(u[:, 2:3], u[:, 0:1], u[:, 0:1]), (ku,), (ku,))
        cfe(eng, lambda e: e.tensor_mul(u[:, 3:4], u[:, 1:2], u[:, 1:2]), (ku,), (ku,))
        cfe(eng, lambda e: e.tensor_add(u[:, 2:3], u[:, 2:3], u[:, 3:4]), (ku,), (ku,))
        cfe(eng, lambda e: e.tensor_scalar_add(u[:, 2:3], u[:, 2:3], 1e-30), (ku,), (ku,))
        cfe("v", lambda e: e.reciprocal(u[:, 3:4], u[:, 2:3]), (ku,), (ku,))
        o = fbuf[:, outc:outc + 2]
        ko = f"cf{outc}"
        cfe(eng, lambda e: e.tensor_mul(t[:, 2:3], t[:, 0:1], u[:, 0:1]), (kt, ku), (kt,))
        cfe(eng, lambda e: e.tensor_mul(t[:, 3:4], t[:, 1:2], u[:, 1:2]), (kt, ku), (kt,))
        cfe(eng, lambda e: e.tensor_add(t[:, 2:3], t[:, 2:3], t[:, 3:4]), (kt,), (kt,))
        cfe(eng, lambda e, o=o: e.tensor_mul(o[:, 0:1], t[:, 2:3], u[:, 3:4]), (kt, ku), (ko,))
        cfe(eng, lambda e: e.tensor_mul(t[:, 2:3], t[:, 1:2], u[:, 0:1]), (kt, ku), (kt,))
        cfe(eng, lambda e: e.tensor_mul(t[:, 3:4], t[:, 0:1], u[:, 1:2]), (kt, ku), (kt,))
        cfe(eng, lambda e: e.tensor_sub(t[:, 2:3], t[:, 2:3], t[:, 3:4]), (kt,), (kt,))
        cfe(eng, lambda e, o=o: e.tensor_mul(o[:, 1:2], t[:, 2:3], u[:, 3:4]), (kt, ku), (ko,))
        cfe(eng, lambda e, o=o: e.tensor_scalar_min(o, o, 1.0e5), (ko,), (ko,))
        cfe(eng, lambda e, o=o: e.tensor_scalar_max(o, o, -1.0e5), (ko,), (ko,))

    def cf_pass3(eng, bndc, s_order, Tre, Tim, tmp1):
        stre, stim = fbuf[:, bndc:bndc + 1], fbuf[:, bndc + 1:bndc + 2]
        kst = f"cf{bndc}"
        t = fbuf[:, tmp1:tmp1 + 4]
        kt = f"cf{tmp1}"
        for s in s_order:
            dsre = fbuf[:, 1024 + s:1025 + s]
            dsim = fbuf[:, 1040 + s:1041 + s]
            ore = fbuf[:, Tre + s:Tre + s + 1]
            oim = fbuf[:, Tim + s:Tim + s + 1]
            cfe(eng, lambda e, a=stre: e.tensor_mul(t[:, 0:1], a, a), (kst,), (kt,))
            cfe(eng, lambda e, a=stim: e.tensor_mul(t[:, 1:2], a, a), (kst,), (kt,))
            cfe(eng, lambda e: e.tensor_add(t[:, 0:1], t[:, 0:1], t[:, 1:2]), (kt,), (kt,))
            cfe(eng, lambda e: e.tensor_scalar_add(t[:, 0:1], t[:, 0:1], 1e-20), (kt,), (kt,))
            cfe("v", lambda e: e.reciprocal(t[:, 1:2], t[:, 0:1]), (kt,), (kt,))
            cfe(eng, lambda e, a=stre: e.tensor_mul(t[:, 2:3], a, t[:, 1:2]), (kst, kt), (kt,))
            cfe(eng, lambda e, a=stim: e.tensor_mul(t[:, 3:4], a, t[:, 1:2]), (kst, kt), (kt,))
            cfe(eng, lambda e, o=ore, d=dsre: e.tensor_scalar(
                o, t[:, 2:3], -1.0, d, ALU.mult, ALU.add),
                (kt, "cf_dre"), ("cfT",))
            cfe(eng, lambda e, o=oim, d=dsim: e.tensor_scalar(
                o, t[:, 3:4], 1.0, d, ALU.mult, ALU.add),
                (kt, "cf_dim"), ("cfT",))
            stre, stim = ore, oim
            kst = "cfT"

    def cf_vcombine(eng, half, j_order, vrow, tmp):
        cfe(eng, lambda e: e.memset(fbuf[0:1, 1376:1384], 0.0), (), ("cfI",))
        cfe(eng, lambda e: e.memset(fbuf[0:1, 1376:1377], 1.0), ("cfI",), ("cfI",))
        cfe(eng, lambda e: e.memset(fbuf[0:1, 1382:1383], 1.0), ("cfI",), ("cfI",))
        cfe(eng, lambda e, v=vrow: e.memset(fbuf[0:1, v:v + 4], 0.0), (), ("cfv",))
        cfe(eng, lambda e, v=vrow: e.memset(fbuf[0:1, v:v + 1], 1.0), ("cfv",), ("cfv",))
        ME, TM = 1392, 1400
        for j in j_order:
            g0 = 1248 + j * 16 + half * 8
            sel = cmask[0:1, half * 8 + j:half * 8 + j + 1]
            cfe(eng, lambda e, g0=g0: e.tensor_sub(
                fbuf[0:1, ME:ME + 8], fbuf[0:1, g0:g0 + 8],
                fbuf[0:1, 1376:1384]), ("cfg", "cfI"), ("cfME",))
            cfe(eng, lambda e, sel=sel: e.tensor_scalar_mul(
                fbuf[0:1, ME:ME + 8], fbuf[0:1, ME:ME + 8], sel),
                ("cfME", "cmask"), ("cfME",))
            cfe(eng, lambda e: e.tensor_add(
                fbuf[0:1, ME:ME + 8], fbuf[0:1, ME:ME + 8],
                fbuf[0:1, 1376:1384]), ("cfME", "cfI"), ("cfME",))
            M = fbuf[0:1, ME:ME + 8]
            v = fbuf[0:1, vrow:vrow + 4]
            tv = fbuf[0:1, TM:TM + 8]
            for row, ob in ((0, 0), (1, 2)):
                a1re, a1im = M[:, 4 * row:4 * row + 1], M[:, 4 * row + 1:4 * row + 2]
                a2re, a2im = M[:, 4 * row + 2:4 * row + 3], M[:, 4 * row + 3:4 * row + 4]
                pre, pim = v[:, 0:1], v[:, 1:2]
                qre, qim = v[:, 2:3], v[:, 3:4]
                o_re, o_im = tv[:, ob:ob + 1], tv[:, ob + 1:ob + 2]
                s1, s2 = tv[:, 4 + 0:4 + 1], tv[:, 4 + 1:4 + 2]
                if ob >= 2:
                    s1, s2 = tv[:, 4 + 2:4 + 3], tv[:, 4 + 3:4 + 4]
                cfe(eng, lambda e, o=o_re, a=a1re, b=pre: e.tensor_mul(o, a, b), ("cfME", "cfv"), ("cftv",))
                cfe(eng, lambda e, s=s1, a=a1im, b=pim: e.tensor_mul(s, a, b), ("cfME", "cfv"), ("cftv",))
                cfe(eng, lambda e, o=o_re, s=s1: e.tensor_sub(o, o, s), ("cftv",), ("cftv",))
                cfe(eng, lambda e, s=s1, a=a2re, b=qre: e.tensor_mul(s, a, b), ("cfME", "cfv"), ("cftv",))
                cfe(eng, lambda e, o=o_re, s=s1: e.tensor_add(o, o, s), ("cftv",), ("cftv",))
                cfe(eng, lambda e, s=s1, a=a2im, b=qim: e.tensor_mul(s, a, b), ("cfME", "cfv"), ("cftv",))
                cfe(eng, lambda e, o=o_re, s=s1: e.tensor_sub(o, o, s), ("cftv",), ("cftv",))
                cfe(eng, lambda e, o=o_im, a=a1re, b=pim: e.tensor_mul(o, a, b), ("cfME", "cfv"), ("cftv",))
                cfe(eng, lambda e, s=s2, a=a1im, b=pre: e.tensor_mul(s, a, b), ("cfME", "cfv"), ("cftv",))
                cfe(eng, lambda e, o=o_im, s=s2: e.tensor_add(o, o, s), ("cftv",), ("cftv",))
                cfe(eng, lambda e, s=s2, a=a2re, b=qim: e.tensor_mul(s, a, b), ("cfME", "cfv"), ("cftv",))
                cfe(eng, lambda e, o=o_im, s=s2: e.tensor_add(o, o, s), ("cftv",), ("cftv",))
                cfe(eng, lambda e, s=s2, a=a2im, b=qre: e.tensor_mul(s, a, b), ("cfME", "cfv"), ("cftv",))
                cfe(eng, lambda e, o=o_im, s=s2: e.tensor_add(o, o, s), ("cftv",), ("cftv",))
            cfe(eng, lambda e, v=vrow: e.tensor_copy(
                fbuf[0:1, v:v + 4], tv[:, 0:4]), ("cftv",), ("cfv",))
            cfe("v", lambda e, v=vrow, t=tmp: e.tensor_reduce(
                fbuf[0:1, t:t + 1], fbuf[0:1, v:v + 4], AX.X, ALU.max,
                apply_absolute_value=True), ("cfv",), ("cfsc",))
            cfe(eng, lambda e, t=tmp: e.tensor_scalar_add(
                fbuf[0:1, t:t + 1], fbuf[0:1, t:t + 1], 1e-30),
                ("cfsc",), ("cfsc",))
            cfe("v", lambda e, t=tmp: e.reciprocal(
                fbuf[0:1, t + 1:t + 2], fbuf[0:1, t:t + 1]), ("cfsc",), ("cfsc",))
            cfe(eng, lambda e, v=vrow, t=tmp: e.tensor_scalar_mul(
                fbuf[0:1, v:v + 4], fbuf[0:1, v:v + 4],
                fbuf[0:1, t + 1:t + 2]), ("cfv", "cfsc"), ("cfv",))

    def cf_run():
        cfe("v", lambda e: e.memset(fbuf[:, 1464:1472], 0.0), (), ("cfIrow",))
        cfe("v", lambda e: e.memset(fbuf[:, 1464:1465], 1.0), ("cfIrow",), ("cfIrow",))
        cfe("v", lambda e: e.memset(fbuf[:, 1470:1471], 1.0), ("cfIrow",), ("cfIrow",))
        cf_pass1("v", 1096, range(16), (1056, 1060, 1064), 1080, 1084, "cfMf")
        cf_pass1("g", 1128, range(15, -1, -1), (1068, 1072, 1076), 1088, 1092, "cfMb")
        pf = cf_hs("v", 1096, 1104, 1112, 1120, 1080, 1084, False, "cfMf")
        pb = cf_hs("g", 1128, 1136, 1144, 1152, 1088, 1092, True, "cfMb")
        em.dma("q", cf_in[0:1, 0:8], fbuf[127:128, pf:pf + 8],
               reads=(f"cf{pf}", "fC"), writes=("cfin",))
        em.dma("q", cf_in[0:1, 8:16], fbuf[0:1, pb:pb + 8],
               reads=(f"cf{pb}", "fC"), writes=("cfin",))
        em.collective("AllGather", ALU.bypass, cf_in.opt(), cf_out.opt(),
                      reads=("cfin",), writes=("cfout",))
        em.dma("q", fbuf[0:1, 1248:1376], cf_out[:, :],
               reads=("cfout",), writes=("cfg", "fC"))
        shf, shb = (1112 if pf != 1112 else 1120), (1144 if pb != 1144 else 1152)
        em.dma("q", fbuf[1:128, shf:shf + 8], fbuf[0:127, pf:pf + 8],
               reads=(f"cf{pf}", "fC"), writes=(f"cf{shf}", "fC"))
        cf_ident_fill((0, 1), shf)
        em.dma("q", fbuf[0:127, shb:shb + 8], fbuf[1:128, pb:pb + 8],
               reads=(f"cf{pb}", "fC"), writes=(f"cf{shb}", "fC"))
        cf_ident_fill((127, 128), shb)
        cf_vcombine("v", 0, range(0, 7), 1408, 1420)
        cf_mobius_bnd("v", shf, 1408, 1224, 1228, 1232)
        cf_vcombine("v", 1, range(7, 0, -1), 1408, 1420)
        cf_mobius_bnd("v", shb, 1408, 1226, 1228, 1232)
        cf_pass3("v", 1224, range(16), 1160, 1176, 1228)
        cf_pass3("g", 1226, range(15, -1, -1), 1192, 1208, 1232)
        W0 = 1424
        cfe("v", lambda e: e.tensor_add(fbuf[:, W0:W0 + 16],
                                        fbuf[:, 1160:1176],
                                        fbuf[:, 1192:1208]),
            ("cfT",), ("cfW",))
        cfe("v", lambda e: e.tensor_sub(fbuf[:, W0:W0 + 16],
                                        fbuf[:, W0:W0 + 16],
                                        fbuf[:, 1024:1040]),
            ("cfW", "cf_dre"), ("cfW",))
        W1 = 1440
        cfe("v", lambda e: e.tensor_add(fbuf[:, W1:W1 + 16],
                                        fbuf[:, 1176:1192],
                                        fbuf[:, 1208:1224]),
            ("cfT",), ("cfW",))
        cfe("v", lambda e: e.tensor_sub(fbuf[:, W1:W1 + 16],
                                        fbuf[:, W1:W1 + 16],
                                        fbuf[:, 1040:1056]),
            ("cfW", "cf_dim"), ("cfW",))
        DD = 1456
        cfe("v", lambda e: e.tensor_mul(fbuf[:, DD:DD + 16],
                                        fbuf[:, W0:W0 + 16],
                                        fbuf[:, W0:W0 + 16]), ("cfW",), ("cfD",))
        cfe("v", lambda e: e.tensor_mul(fbuf[:, DD + 16:DD + 32],
                                        fbuf[:, W1:W1 + 16],
                                        fbuf[:, W1:W1 + 16]), ("cfW",), ("cfD",))
        cfe("v", lambda e: e.tensor_add(fbuf[:, DD:DD + 16],
                                        fbuf[:, DD:DD + 16],
                                        fbuf[:, DD + 16:DD + 32]), ("cfD",), ("cfD",))
        cfe("v", lambda e: e.reciprocal(fbuf[:, DD:DD + 16],
                                        fbuf[:, DD:DD + 16]), ("cfD",), ("cfD",))
        cfe("v", lambda e: e.tensor_mul(fbuf[:, 1984:2000],
                                        fbuf[:, W0:W0 + 16],
                                        fbuf[:, DD:DD + 16]),
            ("cfW", "cfD"), ("cf_gre", "fC"))
        cfe("v", lambda e: e.tensor_mul(fbuf[:, 2000:2016],
                                        fbuf[:, W1:W1 + 16],
                                        fbuf[:, DD:DD + 16]),
            ("cfW", "cfD"), ("cf_gim", "fC"))
        cfe("v", lambda e: e.tensor_scalar_mul(fbuf[:, 2000:2016],
                                               fbuf[:, 2000:2016], -1.0),
            ("cf_gim",), ("cf_gim", "fC"))
        dbg("gre", fbuf[:, 1984:2000], ("cf_gre",))
        dbg("gim", fbuf[:, 2000:2016], ("cf_gim",))

    def l0_attn():
        em.dma("q", xspill, xbuf[:, :, :], reads=("x",), writes=("xsp",))
        ln_affine(0, composed=True)
        dbg("xn1", xnbuf[:, 0, :], ("xn:0",))
        em.dma("g", scr[0:128, 0:16].rearrange("p (a c) -> p a c", c=2),
               dram["l0a_wpot"].rearrange("(a p) c -> p a c", p=128),
               writes=("scr0",))
        em.dma("q", biasb[0:2, 134:135], dram["l0a_bpot"], writes=("bpot",))
        wpotv = scr[0:128, 0:16].rearrange("p (a c) -> p a c", c=2)
        for ti in range(NTILE):
            c0 = ti * 512
            for d in range(NSLAB):
                em.mm(wpotv[:, d, :], xnbuf[:, d, c0:c0 + 512],
                      banks[3][0:2, :], BK[3], start=(d == 0),
                      stop=(d == NSLAB - 1), reads=("scr0", f"xn:{d}"))
            em.emit("s", lambda e, c0=c0: e.activation(
                    fbuf[0:2, c0:c0 + 512], banks[3][0:2, :], AF.Identity,
                    bias=biasb[0:2, 134:135]),
                    reads=(BK[3], "bpot"), writes=("pot", "fA"))
        dbg("potraw", fbuf[0:2, 0:2048], ("pot",))
        em.emit("s", lambda e: e.activation(fbuf[0:1, 0:2048],
                                            fbuf[0:1, 0:2048], AF.Exp),
                reads=("pot",), writes=("pot", "fA"))
        em.emit("s", lambda e: e.activation(fbuf[0:1, 0:2048],
                                            fbuf[0:1, 0:2048], AF.Ln,
                                            bias=consts[0:1, 1:2]),
                reads=("pot", "c_one"), writes=("pot", "fA"))
        em.emit("v", lambda e: e.tensor_scalar_add(fbuf[0:1, 0:2048],
                                                   fbuf[0:1, 0:2048], 0.1),
                reads=("pot",), writes=("pot", "fA"))

        em.dma("q", gspill, fbuf[0:1, 0:2048], reads=("pot",), writes=("gsp",))
        em.dma("q", vspill, fbuf[1:2, 0:2048], reads=("pot",), writes=("vsp",))
        if KCUT < 2:
            return
        for ch in range(16):
            em.emit("v", lambda e, ch=ch: e.tensor_tensor_scan(
                    fbuf[0:1, ch * 128:(ch + 1) * 128],
                    fbuf[0:1, ch * 128:(ch + 1) * 128], zrow[0:1, :],
                    0.0, ALU.add, ALU.add),
                    reads=("pot", "zrow", "gsp"), writes=("pot", "fA"))
        dbg("crow", fbuf[0:1, 0:2048], ("pot",))
        em.dma("q", crspill, fbuf[0:1, 0:2048], reads=("pot",),
               writes=("crsp",))
        em.dma("q", fbuf[:, 1024:1040], vspill, reads=("vsp",),
               writes=("cf_dre", "fC"))
        em.dma("q", fbuf[:, 1040:1056], gspill, reads=("gsp",),
               writes=("cf_dim", "fC"))
        em.emit("v", lambda e: e.tensor_scalar_mul(fbuf[:, 1040:1056],
                                                   fbuf[:, 1040:1056], -1.0),
                reads=("cf_dim",), writes=("cf_dim", "fC"))
        em.emit("v", lambda e: e.tensor_scalar_add(fbuf[:, 1024:1040],
                                                   fbuf[:, 1024:1040], 2.0),
                reads=("cf_dre",), writes=("cf_dre", "fC"))
        cf_run()
        if KCUT < 3:
            return
        em.dma("g", scr[32:33, 0:2048], fbuf[:, 1984:2000],
               reads=("cf_gre",), writes=("scr0", "scr512", "scr1024", "scr1536"))
        em.dma("g", scr[33:34, 0:2048], fbuf[:, 2000:2016],
               reads=("cf_gim",), writes=("scr0", "scr512", "scr1024", "scr1536"))
        em.dma("g", wb[32:34, 0:1024], dram["l0a_wbk"], writes=wbr(0, 1024))
        em.dma("q", biasb[:, 64:72], dram["l0a_bbk_col"], writes=("bbk",))
        for od in range(8):
            for ti in range(NTILE):
                c0 = ti * 512
                bank = (od % 2) * 2 + ti % 2
                em.mm(wb[32:34, od * 128:(od + 1) * 128],
                      scr[32:34, c0:c0 + 512],
                      banks[bank][:, :], BK[bank], start=True, stop=True,
                      reads=("scr0", "scr512", "scr1024", "scr1536") + wbr(0, 1024))
                em.emit("s", lambda e, bank=bank, od=od:
                        e.activation(scrf[:, 0:512], banks[bank][:, :],
                                     AF.Identity,
                                     bias=biasb[:, 64 + od:65 + od]),
                        reads=(BK[bank], "bbk"), writes=("rf0",))
                em.emit("v", lambda e, od=od, c0=c0:
                        e.tensor_add(xbuf[:, od, c0:c0 + 512],
                                     xbuf[:, od, c0:c0 + 512], scrf[:, 0:512]),
                        reads=("rf0", "x"), writes=("x",))
        dbg("xbk", xbuf[:, :, :], ("x",))
        em.dma("q", fbuf[64:65, 0:2048], crspill,
               reads=("crsp",), writes=("crow64", "fA", "fC"))
        if KCUT < 4:
            return
        ln_affine(1, composed=False)
        qkv_and_hebbian()
        if KCUT < 8:
            return
        ln_affine(2, composed=False)
        inner_ffn()
        for od in range(NSLAB):
            for ti in range(NTILE):
                c0 = ti * 512
                rr = ti % 2
                em.dma("q", scrf[:, rr * 512:rr * 512 + 512],
                       xspill[:, od, c0:c0 + 512],
                       reads=("xsp",), writes=(f"rf{rr}",))
                em.emit("v", lambda e, od=od, c0=c0, rr=rr:
                        e.tensor_add(xbuf[:, od, c0:c0 + 512],
                                     xbuf[:, od, c0:c0 + 512],
                                     scrf[:, rr * 512:rr * 512 + 512]),
                        reads=(f"rf{rr}", "x"), writes=("x",))

    def qkv_and_hebbian():
        em.dma("q", biasb[:, 72:84], dram["l0a_bqkv_col"], writes=("bqkv",))
        for oc in range(8):
            w0 = (oc % 2) * 1024
            wkeys = wbr(w0, w0 + 1024)
            em.dma("g", wb[:, w0:w0 + 1024]
                   .rearrange("p (a c) -> p a c", c=128),
                   dram["l0a_wqkv"].rearrange("(a p) c -> p a c", p=128)
                   [:, :, oc * 128:(oc + 1) * 128], writes=wkeys)
            wv = wb[:, w0:w0 + 1024].rearrange("p (a c) -> p a c", c=128)
            base = QFM0 + oc * 2048 if oc < 4 else KFM0 + (oc - 4) * 2048
            for ti in range(NTILE):
                c0 = ti * 512
                bank = ti % 2 + 2 * (oc % 2)
                for d in range(NSLAB):
                    em.mm(wv[:, d, :], xnbuf[:, d, c0:c0 + 512],
                          banks[bank][:, :], BK[bank], start=(d == 0),
                          stop=(d == NSLAB - 1), reads=wkeys + (f"xn:{d}",))
                em.emit("s", lambda e, base=base, c0=c0, bank=bank, oc=oc:
                        e.activation(bigv[:, base + c0:base + c0 + 512],
                                     banks[bank][:, :], AF.Identity,
                                     bias=biasb[:, 72 + oc:73 + oc]),
                        reads=(BK[bank], "bqkv"),
                        writes=bgr(base + c0, base + c0 + 512))
        em.dma("q", fbuf[32:33, 0:1024], dram["l0a_bkv_row"],
               writes=("bkvrow", "fA"))
        for hh in range(2):
            em.mm(onesrow[32:33, :], fbuf[32:33, hh * 512:hh * 512 + 512],
                  banks[6][:, :], BK[6], start=True, stop=True,
                  reads=("bkvrow", "onesrow"))
            em.emit("v", lambda e, hh=hh: e.tensor_copy(
                    scr[:, hh * 512:hh * 512 + 512], banks[6][:, :]),
                    reads=(BK[6],), writes=(f"scr{hh * 512}",))
        for part in range(2):
            for g in range(2):
                em.dma("g", wb[:, 2048 + g * 2048:2048 + (g + 1) * 2048]
                       .rearrange("p (a c) -> p a c", c=512),
                       dram["l0a_wqkv"].rearrange("(a p) c -> p a c", p=128)
                       [:, g * 4:(g + 1) * 4,
                        512 + part * 512:1024 + part * 512],
                       writes=wbr(2048 + g * 2048, 2048 + (g + 1) * 2048))
            tbase = KTOK0 if part == 0 else VTOK0
            for tc in range(16):
                bank = 4 + tc % 2
                for d in range(NSLAB):
                    wv = wb[:, 2048 + d * 512:2048 + (d + 1) * 512]
                    em.mm(xnbuf[:, d, tc * 128:(tc + 1) * 128], wv,
                          banks[bank][:, :], BK[bank], start=(d == 0),
                          stop=(d == NSLAB - 1),
                          reads=wbr(2048 + d * 512, 2048 + (d + 1) * 512)
                          + (f"xn:{d}",))
                em.emit("v", lambda e, tbase=tbase, tc=tc, bank=bank, part=part:
                        e.tensor_add(bigv[:, tbase + tc * 512:
                                          tbase + (tc + 1) * 512],
                                     banks[bank][:, :],
                                     scr[:, part * 512:(part + 1) * 512]),
                        reads=(BK[bank], f"scr{part * 512}"),
                        writes=bgr(tbase + tc * 512, tbase + (tc + 1) * 512))
        dbg("qfm", bigv[:, QFM0:QFM0 + 2048], bgr(QFM0, QFM0 + 2048))
        dbg("ktok", bigv[:, KTOK0:KTOK0 + 512], bgr(KTOK0, KTOK0 + 512))
        dbg("vtok", bigv[:, VTOK0:VTOK0 + 512], bgr(VTOK0, VTOK0 + 512))
        if KCUT < 5:
            return
        em.emit("v", lambda e: e.tensor_copy(
                fbuf[64:65, 2048:2064],
                fbuf[64:65, 0:2048]
                .rearrange("p (a b) -> p a b", b=128)[:, :, 127:128]
                .rearrange("p a b -> p (a b)")),
                reads=("crow64",), writes=("hctot", "fD"))
        em.emit("v", lambda e: e.tensor_tensor_scan(
                fbuf[64:65, 2064:2080], fbuf[64:65, 2048:2064],
                zrow[64:65, 0:16], 0.0, ALU.add, ALU.add),
                reads=("hctot", "zrow"), writes=("hincl", "fD"))
        em.emit("v", lambda e: e.tensor_sub(
                fbuf[64:65, 2080:2096], fbuf[64:65, 2064:2080],
                fbuf[64:65, 2048:2064]),
                reads=("hincl", "hctot"), writes=("hexcl", "fD"))
        em.emit("g", lambda e: e.memset(fbuf[0:64, 1024:1536], 0.0),
                writes=("hebW", "fC"))
        for c in range(16):
            cc0 = c * 128
            em.transpose(banks[3][0:128, 0:1], fbuf[64:65, cc0:cc0 + 128],
                         identf[64:65, 64:65], BK[3],
                         reads=("crow64", "identf"))
            em.emit("v", lambda e: e.tensor_copy(fbuf[:, 2096:2097],
                                                 banks[3][0:128, 0:1]),
                    reads=(BK[3],), writes=("hcc", "fD"))
            em.mm(onesrow[64:65, :], fbuf[64:65, cc0 + 127:cc0 + 128],
                  banks[3][:, 0:1], BK[3], start=True, stop=True,
                  reads=("crow64", "onesrow"))
            em.emit("v", lambda e: e.tensor_sub(fbuf[:, 2097:2098],
                                                fbuf[:, 2096:2097],
                                                banks[3][:, 0:1]),
                    reads=("hcc", BK[3]), writes=("hccd", "fD"))
            em.emit("s", lambda e: e.activation(fbuf[:, 2098:2099],
                                                fbuf[:, 2097:2098], AF.Exp,
                                                bias=consts[:, 2:3],
                                                scale=DT_),
                    reads=("hccd", "c_lneta"), writes=("hscol", "fD"))
            em.emit("s", lambda e: e.activation(fbuf[:, 2099:2100],
                                                banks[3][:, 0:1], AF.Exp,
                                                scale=-DT_),
                    reads=(BK[3],), writes=("hdecC", "fD"))
            em.emit("v", lambda e, cc0=cc0: e.tensor_scalar_mul(
                    scr[:, 0:512],
                    bigv[:, KTOK0 + cc0 * 4:KTOK0 + cc0 * 4 + 512],
                    fbuf[:, 2098:2099]),
                    reads=bgr(KTOK0 + cc0 * 4, KTOK0 + cc0 * 4 + 512)
                    + ("hscol",), writes=("scr0",))
            em.dma("q", wspill[c, :, :], fbuf[0:64, 1024:1536],
                    reads=("hebW",), writes=(f"wsp:{c}",))
            for h in range(8):
                em.mm(scr[:, h * 64:(h + 1) * 64],
                      bigv[:, VTOK0 + cc0 * 4 + h * 64:
                           VTOK0 + cc0 * 4 + (h + 1) * 64],
                      banks[6][0:64, h * 64:(h + 1) * 64], BK[6],
                      start=(h == 0), stop=(h == 7),
                      reads=("scr0",) + bgr(VTOK0 + cc0 * 4,
                                            VTOK0 + cc0 * 4 + 512))
            em.emit("v", lambda e: e.scalar_tensor_tensor(
                    fbuf[0:64, 1024:1536], fbuf[0:64, 1024:1536],
                    fbuf[0:64, 2099:2100], banks[6][0:64, :],
                    ALU.mult, ALU.add),
                    reads=("hebW", "hdecC", BK[6]), writes=("hebW", "fC"))
        if KCUT < 6:
            return
        em.emit("s", lambda e: e.activation(fbuf[64:65, 2100:2101],
                                            fbuf[64:65, 2079:2080], AF.Exp,
                                            scale=-DT_),
                reads=("hincl",), writes=("hdtot", "fD"))
        em.dma("q", heb_in[0:64, :], fbuf[0:64, 1024:1536],
               reads=("hebW",), writes=("hin",))
        for kk in range(4):
            em.dma("q", heb_in[64:65, kk * 128:(kk + 1) * 128],
                   zrow[64:65, 0:128], reads=("zrow",), writes=("hin",))
        em.dma("q", heb_in[64:65, 0:1], fbuf[64:65, 2100:2101],
               reads=("hdtot",), writes=("hin",))
        em.collective("AllGather", ALU.bypass, heb_in.opt(), heb_out.opt(),
                      reads=("hin",), writes=("hout",))
        em.emit("g", lambda e: e.memset(scrf[0:64, 512:1024], 0.0),
                writes=("rf1",))
        for j in range(7):
            em.dma("q", scrf[0:64, 0:512], heb_out[j * 65:j * 65 + 64, :],
                   reads=("hout",), writes=("rf0",))
            em.dma("q", fbuf[64:65, 2101:2102],
                   heb_out[j * 65 + 64:j * 65 + 65, 0:1],
                   reads=("hout",), writes=("hdj", "fD"))
            em.mm(onesrow[64:65, :], fbuf[64:65, 2101:2102], banks[3][:, 0:1],
                  BK[3], start=True, stop=True, reads=("hdj", "onesrow"))
            em.emit("v", lambda e: e.tensor_scalar_sub(
                    fbuf[:, 2102:2103], banks[3][:, 0:1], 1.0),
                    reads=(BK[3],), writes=("hfac", "fD"))
            em.emit("v", lambda e, j=j: e.tensor_scalar(
                    fbuf[:, 2102:2103], fbuf[:, 2102:2103],
                    cmask[:, j:j + 1], 1.0, ALU.mult, ALU.add),
                    reads=("hfac", "cmask"), writes=("hfac", "fD"))
            em.emit("v", lambda e, j=j: e.tensor_scalar_mul(
                    scrf[0:64, 0:512], scrf[0:64, 0:512],
                    cmask[0:64, j:j + 1]),
                    reads=("rf0", "cmask"), writes=("rf0",))
            em.emit("v", lambda e: e.scalar_tensor_tensor(
                    scrf[0:64, 512:1024], scrf[0:64, 512:1024],
                    fbuf[0:64, 2102:2103], scrf[0:64, 0:512],
                    ALU.mult, ALU.add),
                    reads=("rf1", "hfac", "rf0"), writes=("rf1",))
        if KCUT < 7:
            return
        for c in range(16):
            cc0 = c * 128
            em.dma("q", scrf[0:64, 0:512], wspill[c, :, :],
                   reads=(f"wsp:{c}",), writes=("rf0",))
            em.mm(onesrow[64:65, :], fbuf[64:65, 2080 + c:2081 + c],
                  banks[3][:, 0:1], BK[3], start=True, stop=True,
                  reads=("hexcl", "onesrow"))
            em.emit("s", lambda e: e.activation(fbuf[:, 2105:2106],
                                                banks[3][:, 0:1], AF.Exp,
                                                scale=-DT_),
                    reads=(BK[3],), writes=("hcd", "fD"))
            em.emit("v", lambda e: e.scalar_tensor_tensor(
                    scr[0:64, 1536:2048], scrf[0:64, 512:1024],
                    fbuf[0:64, 2105:2106], scrf[0:64, 0:512],
                    ALU.mult, ALU.add),
                    reads=("rf1", "hcd", "rf0"), writes=("scr1536",))
            em.dma("q", scr[64:128, 1536:2048], scr[0:64, 1536:2048],
                   reads=("scr1536",), writes=("scr1536",))
            if KCUT2 < 2:
                continue
            em.transpose(banks[3][0:128, 0:1], fbuf[64:65, cc0:cc0 + 128],
                         identf[64:65, 64:65], BK[3],
                         reads=("crow64", "identf"))
            em.emit("v", lambda e: e.tensor_copy(fbuf[:, 2096:2097],
                                                 banks[3][0:128, 0:1]),
                    reads=(BK[3],), writes=("hcc", "fD"))
            em.mm(onesrow[64:65, :], fbuf[64:65, cc0:cc0 + 128],
                  banks[3][:, 0:128], BK[3], start=True, stop=True,
                  reads=("crow64", "onesrow"))
            em.emit("v", lambda e: e.tensor_scalar(
                    scrf[:, 0:128], banks[3][:, 0:128],
                    fbuf[:, 2096:2097], -DT_, ALU.subtract, ALU.mult),
                    reads=(BK[3], "hcc"), writes=("rf0",))
            em.emit("s", lambda e: e.activation(scrf[:, 128:256],
                                                scrf[:, 0:128], AF.Exp),
                    reads=("rf0",), writes=("rf0",))
            em.emit("v", lambda e: e.tensor_mul(
                    scr[:, 1024:1152].rearrange("p (o c) -> p o c", o=1),
                    scrf[:, 128:256].rearrange("p (o c) -> p o c", o=1),
                    etam[:, :].rearrange("p (o c) -> p o c", o=1)),
                    reads=("rf0", "etam"), writes=("scr1024",))
            if KCUT2 < 3:
                continue
            for h in range(8):
                bank = h // 4
                kslab = KFM0 + (h // 2) * 2048 + cc0
                qslab = QFM0 + (h // 2) * 2048 + cc0
                rds = bgr(kslab, kslab + 128) + bgr(qslab, qslab + 128)
                pb = (h % 2) * 64
                rhs = bigv[pb:pb + 64, qslab:qslab + 128]
                if h % 2 == 0:
                    em.mm(bigv[0:64, kslab:kslab + 128], rhs,
                          banks[bank][:, (h % 4) * 128:(h % 4) * 128 + 128],
                          BK[bank], start=True, stop=True, reads=rds)
                else:
                    for sh in range(2):
                        em.mm(bigv[64:128, kslab + sh * 64:kslab + sh * 64 + 64],
                              rhs,
                              banks[bank][sh * 64:sh * 64 + 64,
                                          (h % 4) * 128:(h % 4) * 128 + 128],
                              BK[bank], start=True, stop=True, reads=rds)
            for bank in range(2):
                em.emit("v", lambda e, bank=bank: e.tensor_mul(
                        scr[:, bank * 512:bank * 512 + 512]
                        .rearrange("p (a c) -> p a c", c=128),
                        banks[bank][:, :].rearrange("p (a c) -> p a c", c=128),
                        scr[:, 1024:1152].rearrange("p (o c) -> p o c", o=1)
                        .broadcast_to([128, 4, 128])),
                        reads=(BK[bank], "scr1024"),
                        writes=(f"scr{bank * 512}",))
            if KCUT2 < 4:
                continue
            for h in range(8):
                sec = h // 2
                ob = 2 if h % 2 == 0 else 4
                pb = (h % 2) * 64
                em.mm(bigv[:, VTOK0 + cc0 * 4 + h * 64:
                           VTOK0 + cc0 * 4 + (h + 1) * 64],
                      scr[:, (h // 4) * 512 + (h % 4) * 128:
                          (h // 4) * 512 + (h % 4) * 128 + 128],
                      banks[ob][pb:pb + 64, sec * 128:(sec + 1) * 128],
                      BK[ob], start=(h < 2), stop=False,
                      reads=bgr(VTOK0 + cc0 * 4, VTOK0 + cc0 * 4 + 512)
                      + (f"scr{(h // 4) * 512}",))
            if KCUT2 < 5:
                continue
            em.emit("v", lambda e: e.tensor_scalar_mul(
                    scrf[:, 0:128], banks[3][:, 0:128], -DT_),
                    reads=(BK[3],), writes=("rf0",))
            em.emit("s", lambda e: e.activation(scrf[:, 128:256],
                                                scrf[:, 0:128], AF.Exp),
                    reads=("rf0",), writes=("rf0",))
            for q in range(4):
                em.emit("v", lambda e, q=q, cc0=cc0: e.tensor_mul(
                        scr[:, q * 128:(q + 1) * 128],
                        bigv[:, QFM0 + q * 2048 + cc0:
                             QFM0 + q * 2048 + cc0 + 128],
                        scrf[:, 128:256]),
                        reads=bgr(QFM0 + q * 2048 + cc0,
                                  QFM0 + q * 2048 + cc0 + 128) + ("rf0",),
                        writes=("scr0",))
            for h in range(8):
                sec = h // 2
                ob = 2 if h % 2 == 0 else 4
                pb = (h % 2) * 64
                em.mm(scr[(h % 2) * 64:(h % 2) * 64 + 64,
                          1536 + h * 64:1536 + (h + 1) * 64],
                      scr[(h % 2) * 64:(h % 2) * 64 + 64,
                          (h // 2) * 128:(h // 2) * 128 + 128],
                      banks[ob][pb:pb + 64, sec * 128:(sec + 1) * 128],
                      BK[ob], start=False, stop=(h >= 6),
                      reads=("scr1536", "scr0"))
            if KCUT2 < 6:
                continue
            for s4 in range(4):
                em.emit("v", lambda e, s4=s4, cc0=cc0: e.tensor_copy(
                        bigv[0:64, KTOK0 + s4 * 2048 + cc0:
                             KTOK0 + s4 * 2048 + cc0 + 128],
                        banks[2][0:64, s4 * 128:(s4 + 1) * 128]),
                        reads=(BK[2],),
                        writes=bgr(KTOK0 + s4 * 2048 + cc0,
                                   KTOK0 + s4 * 2048 + cc0 + 128))
                em.emit("v", lambda e, s4=s4, cc0=cc0: e.tensor_copy(
                        bigv[64:128, KTOK0 + s4 * 2048 + cc0:
                             KTOK0 + s4 * 2048 + cc0 + 128],
                        banks[4][64:128, s4 * 128:(s4 + 1) * 128]),
                        reads=(BK[4],),
                        writes=bgr(KTOK0 + s4 * 2048 + cc0,
                                   KTOK0 + s4 * 2048 + cc0 + 128))
        for s4 in range(4):
            dbg(f"heb{s4}", bigv[:, KTOK0 + s4 * 2048:KTOK0 + (s4 + 1) * 2048],
                bgr(KTOK0 + s4 * 2048, KTOK0 + (s4 + 1) * 2048))
        if KCUT < 7.5:
            pass
        em.dma("g", wb[:, 0:4096].rearrange("p (a c) -> p a c", c=1024),
               dram["l0a_wout"].rearrange("(a p) c -> p a c", p=128),
               writes=wbr(0, 4096))
        em.dma("q", biasb[:, 84:92], dram["l0a_bout_col"], writes=("bout",))
        for od in range(8):
            for ti in range(NTILE):
                c0 = ti * 512
                bank = 4 + ti % 2
                for sl in range(4):
                    em.mm(wb[:, sl * 1024 + od * 128:sl * 1024 + (od + 1) * 128],
                          bigv[:, KTOK0 + sl * 2048 + c0:
                               KTOK0 + sl * 2048 + c0 + 512],
                          banks[bank][:, :], BK[bank], start=(sl == 0),
                          stop=(sl == 3),
                          reads=wbr(sl * 1024 + od * 128,
                                    sl * 1024 + (od + 1) * 128)
                          + bgr(KTOK0 + sl * 2048 + c0,
                                KTOK0 + sl * 2048 + c0 + 512))
                em.emit("s", lambda e, bank=bank, od=od:
                        e.activation(scrf[:, 0:512], banks[bank][:, :],
                                     AF.Identity,
                                     bias=biasb[:, 84 + od:85 + od]),
                        reads=(BK[bank], "bout"), writes=("rf0",))
                em.emit("v", lambda e, od=od, c0=c0:
                        e.tensor_add(xbuf[:, od, c0:c0 + 512],
                                     xbuf[:, od, c0:c0 + 512], scrf[:, 0:512]),
                        reads=("rf0", "x"), writes=("x",))
        dbg("xwo", xbuf[:, :, :], ("x",))

    def inner_ffn():
        em.dma("q", biasb[:, 92:124], dram["l0a_bff1_col"], writes=("bff1",))
        em.dma("q", biasb[:, 124:132], dram["l0a_bff2_col"], writes=("bff2",))
        for half in range(2):
            h0 = half * 1024
            for c in range(32):
                w0 = (c % 2) * 1024
                wkeys = wbr(w0, w0 + 1024)
                em.dma("g", wb[:, w0:w0 + 1024]
                       .rearrange("p (a c) -> p a c", c=128),
                       dram["l0a_wff1"].rearrange("(a p) c -> p a c", p=128)
                       [:, :, c * 128:(c + 1) * 128], writes=wkeys)
                wv = wb[:, w0:w0 + 1024].rearrange("p (a c) -> p a c", c=128)
                for tt in range(2):
                    c0 = h0 + tt * 512
                    bank = tt + 2 * (c % 2)
                    for d in range(NSLAB):
                        em.mm(wv[:, d, :], xnbuf[:, d, c0:c0 + 512],
                              banks[bank][:, :], BK[bank], start=(d == 0),
                              stop=(d == NSLAB - 1),
                              reads=wkeys + (f"xn:{d}",))
                    em.emit("s", lambda e, c=c, tt=tt, bank=bank:
                            e.activation(big1[:, c, tt * 512:tt * 512 + 512],
                                         banks[bank][:, :],
                                         AF.Gelu_apprx_tanh,
                                         bias=biasb[:, 92 + c:93 + c]),
                            reads=(BK[bank], "bff1"), writes=(f"bg{c}",))
            for od in range(8):
                gslots = [2048 + ((od * 4 + g) % 4) * 1024 for g in range(4)]
                gkeys = [wbr(s, s + 1024) for s in gslots]
                for g in range(4):
                    em.dma("g", wb[:, gslots[g]:gslots[g] + 1024]
                           .rearrange("p (a c) -> p a c", c=128),
                           dram["l0a_wff2"].rearrange("(a p) c -> p a c", p=128)
                           [:, g * 8:(g + 1) * 8, od * 128:(od + 1) * 128],
                           writes=gkeys[g])
                for tt in range(2):
                    c0 = tt * 512
                    bank = tt + 2 * (od % 2)
                    for f in range(32):
                        g = f // 8
                        wv = wb[:, gslots[g]:gslots[g] + 1024].rearrange(
                            "p (a c) -> p a c", c=128)
                        em.mm(wv[:, f % 8, :], big1[:, f, c0:c0 + 512],
                              banks[bank][:, :], BK[bank], start=(f == 0),
                              stop=(f == 31), reads=gkeys[g] + (f"bg{f}",))
                    g0 = h0 + c0
                    em.emit("s", lambda e, bank=bank, od=od, tt=tt:
                            e.activation(scrf[:, tt * 512:tt * 512 + 512],
                                         banks[bank][:, :], AF.Identity,
                                         bias=biasb[:, 124 + od:125 + od]),
                            reads=(BK[bank], "bff2"), writes=(f"rf{tt}",))
                    em.emit("v", lambda e, od=od, g0=g0, tt=tt:
                            e.tensor_add(xbuf[:, od, g0:g0 + 512],
                                         xbuf[:, od, g0:g0 + 512],
                                         scrf[:, tt * 512:tt * 512 + 512]),
                            reads=(f"rf{tt}", "x"), writes=("x",))

    H1 = 16
    KR1, SK1 = 0, 16384

    l1ln_sb = sb("l1ln_sb", (128, 16), F32)
    cm32_sb = sb("cm32_sb", (64, 32), F32)
    segdiag = sb("segdiag", (128, 4), F32)
    betam = sb("betam", (128, 16), F32)
    musg_sb = xnbuf[:, 5, 0:128]
    identb = xnbuf[:, 5, 128:256]
    stage_sb = xnbuf[:, 5, 256:512]
    zt_sb = xnbuf[:, 5, 512:1024]
    cumA_sb = xnbuf[0:64, 6, 0:1024]
    cumAT_sb = xnbuf[0:64, 6, 1024:2048]
    memb_sb = xnbuf[:, 7, 0:1024]
    min_sb = xnbuf[0:64, 7, 1024:2048]
    memst = fbuf[0:64, 1024:2048]
    for _k in ("musg", "identb", "stg0", "stg1", "zt"):
        em.aliases[_k] = ("xn:5",)
    for _k in ("cA", "cAT"):
        em.aliases[_k] = ("xn:6",)
    for _k in ("mbb", "min"):
        em.aliases[_k] = ("xn:7",)
    em.aliases["mst"] = ("fC",)
    xvf = xbuf[:, :, :].rearrange("p a c -> p (a c)")
    wqkv1 = dram["l1a_wqkv"].rearrange("(a p) c -> p a c", p=128)

    def l1_qkv_passes():
        for pp, wcol in (("k", 1024), ("q", 0)):
            for oc in range(8):
                w0 = (oc % 2) * 1024
                wkeys = wbr(w0, w0 + 1024)
                em.dma("g", wb[:, w0:w0 + 1024]
                       .rearrange("p (a c) -> p a c", c=128),
                       wqkv1[:, :, wcol + oc * 128:wcol + (oc + 1) * 128],
                       writes=wkeys)
                wv = wb[:, w0:w0 + 1024].rearrange("p (a c) -> p a c", c=128)
                for ti in range(NTILE):
                    c0 = ti * 512
                    bank = ti % 2 + 2 * (oc % 2)
                    for d in range(NSLAB):
                        em.mm(wv[:, d, :], xnbuf[:, d, c0:c0 + 512],
                              banks[bank][:, :], BK[bank], start=(d == 0),
                              stop=(d == NSLAB - 1), reads=wkeys + (f"xn:{d}",))
                    if pp == "k":
                        a0 = KR1 + oc * 2048 + c0
                        em.emit("s", lambda e, a0=a0, bank=bank: e.activation(
                                bigv[:, a0:a0 + 512], banks[bank][:, :],
                                AF.Copy),
                                reads=(BK[bank],), writes=bgr(a0, a0 + 512))
                    else:
                        rr = ((oc * NTILE + ti) % 4) * 512
                        em.emit("s", lambda e, rr=rr, bank=bank: e.activation(
                                scr[:, rr:rr + 512], banks[bank][:, :],
                                AF.Copy),
                                reads=(BK[bank],), writes=(f"scr{rr}",))
                        em.dma("q", qspill[:, oc, c0:c0 + 512],
                               scr[:, rr:rr + 512],
                               reads=(f"scr{rr}",), writes=("qsp",))
        for d in range(NSLAB):
            ks = bigv[:, KR1 + d * 2048:KR1 + (d + 1) * 2048]
            ss = bigv[:, SK1 + d * 2048:SK1 + (d + 1) * 2048]
            kk = bgr(KR1 + d * 2048, KR1 + (d + 1) * 2048)
            sk = bgr(SK1 + d * 2048, SK1 + (d + 1) * 2048)
            em.emit("s", lambda e, ks=ks: e.activation(
                    fbuf[:, 0:2048], ks, AF.Relu, scale=-1.0),
                    reads=kk, writes=("fA", "fC"))
            em.emit("s", lambda e: e.activation(
                    fbuf[:, 0:2048], fbuf[:, 0:2048], AF.Exp, scale=-1.0),
                    reads=("fA", "fC"), writes=("fA", "fC"))
            em.emit("s", lambda e, ss=ss, ks=ks: e.activation(ss, ks, AF.Relu),
                    reads=kk, writes=sk)
            em.emit("v", lambda e, ss=ss: e.tensor_add(ss, ss,
                                                       fbuf[:, 0:2048]),
                    reads=sk + ("fA", "fC"), writes=sk)
        for hf in range(2):
            wkeys = wbr(0, 4096)
            em.dma("g", wb[:, 0:4096].rearrange("p (a c) -> p a c", c=512),
                   wqkv1[:, :, 2048 + hf * 512:2048 + hf * 512 + 512],
                   writes=wkeys)
            for tc in range(16):
                bank = 4 + tc % 2
                for d in range(NSLAB):
                    em.mm(xnbuf[:, d, tc * 128:(tc + 1) * 128],
                          wb[:, d * 512:(d + 1) * 512],
                          banks[bank][:, :], BK[bank], start=(d == 0),
                          stop=(d == NSLAB - 1), reads=wkeys + (f"xn:{d}",))
                em.emit("v", lambda e, tc=tc, hf=hf, bank=bank: e.tensor_copy(
                        xvf[:, tc * 1024 + hf * 512:tc * 1024 + hf * 512 + 512],
                        banks[bank][:, :]),
                        reads=(BK[bank],), writes=(f"vt{tc}", "x"))
        em.dma("g", wb[:, 2048:4096], dram["l1a_cos"], writes=wbr(2048, 4096))
        em.dma("g", wb[:, 4096:6144], dram["l1a_sin"], writes=wbr(4096, 6144))
        for d in range(NSLAB):
            ks = bigv[:, KR1 + d * 2048:KR1 + (d + 1) * 2048]
            kk = bgr(KR1 + d * 2048, KR1 + (d + 1) * 2048)
            wkeys = wbr(0, 2048)
            for (d0, s0) in ((0, 32), (32, 0), (64, 96), (96, 64)):
                em.dma("q", wb[d0:d0 + 32, 0:2048],
                       bigv[s0:s0 + 32, KR1 + d * 2048:KR1 + (d + 1) * 2048],
                       reads=kk, writes=wkeys)
            em.emit("v", lambda e, ks=ks: e.tensor_mul(ks, ks,
                                                       wb[:, 2048:4096]),
                    reads=kk + wbr(2048, 4096), writes=kk)
            em.emit("v", lambda e: e.tensor_mul(wb[:, 0:2048], wb[:, 0:2048],
                                                wb[:, 4096:6144]),
                    reads=wkeys + wbr(4096, 6144), writes=wkeys)
            em.emit("v", lambda e, ks=ks: e.tensor_add(ks, ks, wb[:, 0:2048]),
                    reads=kk + wkeys, writes=kk)
        dbg("l1kr", bigv[:, KR1:KR1 + 2048], bgr(KR1, KR1 + 2048))
        dbg("l1sk", bigv[:, SK1:SK1 + 2048], bgr(SK1, SK1 + 2048))

    def l1_z():
        em.emit("g", lambda e: e.memset(fbuf[:, 2048:2112], 0.0),
                writes=("fD",))
        for d in range(NSLAB):
            ss = bigv[:, SK1 + d * 2048:SK1 + (d + 1) * 2048]
            em.emit("v", lambda e, ss=ss, d=d: e.tensor_reduce(
                    fbuf[:, d * 64:(d + 1) * 64],
                    ss.rearrange("p (j t) -> p j t", t=32), AX.X, ALU.add),
                    reads=bgr(SK1 + d * 2048, SK1 + (d + 1) * 2048),
                    writes=("fA",))
            em.emit("v", lambda e, d=d: e.tensor_tensor_scan(
                    fbuf[:, 1024 + d * 64:1024 + (d + 1) * 64],
                    fbuf[:, d * 64:(d + 1) * 64], fbuf[:, 2048:2112],
                    0.0, ALU.add, ALU.add),
                    reads=("fA", "fD"), writes=("fC",))
        em.emit("v", lambda e: e.tensor_sub(fbuf[:, 0:512],
                                            fbuf[:, 1024:1536],
                                            fbuf[:, 0:512]),
                reads=("fA", "fC"), writes=("fA",))
        em.emit("v", lambda e: e.tensor_copy(
                fbuf[:, 2048:2056],
                fbuf[:, 1024:1536].rearrange("p (d j) -> p d j", j=64)
                [:, :, 63:64].rearrange("p d j -> p (d j)")),
                reads=("fC",), writes=("fD",))
        em.dma("q", z_in, fbuf[:, 2048:2056], reads=("fD",), writes=("zin",))
        em.collective("AllGather", ALU.bypass, z_in.opt(), z_out.opt(),
                      reads=("zin",), writes=("zout",))
        em.emit("g", lambda e: e.memset(fbuf[:, 2056:2064], 0.0),
                writes=("fD",))
        for j in range(7):
            em.dma("q", fbuf[:, 2064:2072], z_out[j * 128:(j + 1) * 128, :],
                   reads=("zout",), writes=("fD",))
            em.emit("v", lambda e, j=j: e.scalar_tensor_tensor(
                    fbuf[:, 2056:2064], fbuf[:, 2064:2072], cmask[:, j:j + 1],
                    fbuf[:, 2056:2064], ALU.mult, ALU.add),
                    reads=("fD", "cmask"), writes=("fD",))
        em.emit("v", lambda e: e.tensor_add(
                fbuf[:, 0:512].rearrange("p (d j) -> p d j", j=64),
                fbuf[:, 0:512].rearrange("p (d j) -> p d j", j=64),
                fbuf[:, 2056:2064].rearrange("p (d o) -> p d o", o=1)
                .broadcast_to([128, 8, 64])),
                reads=("fA", "fD"), writes=("fA",))
        em.emit("v", lambda e: e.tensor_scalar_add(fbuf[:, 0:512],
                                                   fbuf[:, 0:512], 1e-6),
                reads=("fA",), writes=("fA",))
        em.emit("v", lambda e: e.tensor_copy(zt_sb[:, :], fbuf[:, 0:512]),
                reads=("fA",), writes=("zt",))
        dbg("l1zt", fbuf[:, 0:512], ("fA",))

    def l1_den(c, sq_mode):
        cc0 = c * 128
        dbk = 6 if sq_mode else 4
        for h in range(H1):
            pb = (h % 2) * 64
            zsl = zt_sb[pb:pb + 64,
                        (h // 2) * 64 + c * 4:(h // 2) * 64 + c * 4 + 4]
            if sq_mode:
                l_ap = scr[:, 1024 + (h // 2) * 128:1024 + (h // 2) * 128 + 128]
                lks = ("scr1024", "scr1536")
            else:
                a0 = SK1 + (h // 2) * 2048 + cc0
                l_ap = bigv[:, a0:a0 + 128]
                lks = bgr(a0, a0 + 128)
            if h % 2 == 0:
                em.mm(l_ap[0:64, 0:128], zsl, banks[dbk][:, h * 4:h * 4 + 4],
                      BK[dbk], start=True, stop=True, reads=lks + ("zt",))
            else:
                for sh in range(2):
                    em.mm(l_ap[64:128, sh * 64:sh * 64 + 64], zsl,
                          banks[dbk][sh * 64:sh * 64 + 64, h * 4:h * 4 + 4],
                          BK[dbk], start=True, stop=True, reads=lks + ("zt",))
        em.emit("v", lambda e, dbk=dbk: e.tensor_mul(
                fbuf[:, 2048:2112].rearrange("p (h j) -> p h j", j=4),
                banks[dbk][:, 0:64].rearrange("p (h j) -> p h j", j=4),
                segdiag[:, :].rearrange("p (o j) -> p o j", o=1)
                .broadcast_to([128, 16, 4])),
                reads=(BK[dbk], "sgd"), writes=("fD",))
        em.emit("v", lambda e: e.tensor_reduce(
                biasb[:, 144:160],
                fbuf[:, 2048:2112].rearrange("p (h j) -> p h j", j=4),
                AX.X, ALU.add),
                reads=("fD",), writes=("denP",))
        if sq_mode:
            em.emit("v", lambda e: e.tensor_scalar_add(
                    biasb[:, 144:160], biasb[:, 144:160], 1e-6),
                    reads=("denP",), writes=("denP",))
            em.emit("v", lambda e: e.reciprocal(fbuf[:, 2048:2064],
                                                biasb[:, 144:160]),
                    reads=("denP",), writes=("fD",))
        else:
            em.emit("v", lambda e: e.reciprocal(fbuf[:, 2048:2064],
                                                biasb[:, 144:160]),
                    reads=("denP",), writes=("fD",))

    def l1_p5():
        allscr = ("scr0", "scr512", "scr1024", "scr1536")
        for c in range(16):
            cc0 = c * 128
            for d in range(NSLAB):
                a0 = SK1 + d * 2048 + cc0
                em.transpose(psb[:, d * 128:(d + 1) * 128], bigv[:, a0:a0 + 128],
                             identb[:, :], PSB,
                             reads=bgr(a0, a0 + 128) + ("identb",))
            em.emit("s", lambda e: e.activation(scr[:, 0:1024], psb[:, :],
                                                AF.Copy),
                    reads=(PSB,), writes=("scr0", "scr512"))
            l1_den(c, sq_mode=False)
            em.emit("v", lambda e: e.tensor_mul(
                    scr[:, 1024:2048].rearrange("p (h f) -> p h f", f=64),
                    scr[:, 0:1024].rearrange("p (h f) -> p h f", f=64),
                    fbuf[:, 2048:2064].rearrange("p (h o) -> p h o", o=1)
                    .broadcast_to([128, 16, 64])),
                    reads=("scr0", "scr512", "fD"),
                    writes=("scr1024", "scr1536"))
            em.emit("v", lambda e: e.tensor_copy(
                    fbuf[:, 0:2048].rearrange("p (h f) -> p h f", f=128)
                    [:, :, 0:64],
                    scr[:, 1024:2048].rearrange("p (h f) -> p h f", f=64)),
                    reads=("scr1024", "scr1536"), writes=("fA", "fC"))
            em.emit("v", lambda e, c=c: e.tensor_copy(
                    fbuf[:, 0:2048].rearrange("p (h f) -> p h f", f=128)
                    [:, :, 64:128],
                    xvf[:, c * 1024:(c + 1) * 1024]
                    .rearrange("p (h f) -> p h f", f=64)),
                    reads=(f"vt{c}",), writes=("fA", "fC"))
            for j in range(NSLAB):
                em.transpose(psb[:, j * 128:(j + 1) * 128],
                             scr[:, 1024 + j * 128:1024 + (j + 1) * 128],
                             identb[:, :], PSB,
                             reads=("scr1024", "scr1536", "identb"))
            em.emit("s", lambda e: e.activation(scr[:, 0:1024], psb[:, :],
                                                AF.Copy),
                    reads=(PSB,), writes=("scr0", "scr512"))
            for h in range(H1):
                bank = h // 4
                a0 = SK1 + (h // 2) * 2048 + cc0
                lk = bgr(a0, a0 + 128)
                pb = (h % 2) * 64
                rhs = scr[pb:pb + 64, (h // 2) * 128:(h // 2) * 128 + 128]
                if h % 2 == 0:
                    em.mm(bigv[0:64, a0:a0 + 128], rhs,
                          banks[bank][:, (h % 4) * 128:(h % 4) * 128 + 128],
                          BK[bank], start=True, stop=True,
                          reads=lk + ("scr0", "scr512"))
                else:
                    for sh in range(2):
                        em.mm(bigv[64:128, a0 + sh * 64:a0 + sh * 64 + 64],
                              rhs,
                              banks[bank][sh * 64:sh * 64 + 64,
                                          (h % 4) * 128:(h % 4) * 128 + 128],
                              BK[bank], start=True, stop=True,
                              reads=lk + ("scr0", "scr512"))
            for bank in range(4):
                em.emit("v", lambda e, bank=bank: e.tensor_mul(
                        wb[:, bank * 512:(bank + 1) * 512]
                        .rearrange("p (a f) -> p a f", f=128),
                        banks[bank][:, :].rearrange("p (a f) -> p a f", f=128),
                        musg_sb[:, :].rearrange("p (o f) -> p o f", o=1)
                        .broadcast_to([128, 4, 128])),
                        reads=(BK[bank], "musg"),
                        writes=wbr(bank * 512, (bank + 1) * 512))
            em.emit("s", lambda e: e.activation(scr[:, 0:1024],
                                                fbuf[:, 0:1024], AF.Copy),
                    reads=("fA",), writes=("scr0", "scr512"))
            em.emit("s", lambda e: e.activation(scr[:, 1024:2048],
                                                fbuf[:, 1024:2048], AF.Copy),
                    reads=("fC",), writes=("scr1024", "scr1536"))
            for it in range(3):
                for bank in range(4):
                    fkey = "fA" if bank < 2 else "fC"
                    skey = f"scr{bank * 512}"
                    for hh in range(4):
                        h = bank * 4 + hh
                        em.mm(wb[:, h * 128:(h + 1) * 128],
                              scr[:, h * 128:(h + 1) * 128],
                              banks[bank][:, hh * 128:hh * 128 + 128],
                              BK[bank], start=True, stop=True,
                              reads=wbr(h * 128, (h + 1) * 128) + (skey,))
                    em.emit("v", lambda e, bank=bank: e.tensor_sub(
                            scr[:, bank * 512:(bank + 1) * 512],
                            fbuf[:, bank * 512:(bank + 1) * 512],
                            banks[bank][:, :]),
                            reads=(fkey, BK[bank]), writes=(skey,))
            em.dma("g", yspill[c, :, :], scr[:, 0:2048], reads=allscr,
                   writes=(f"ysp{c}",))
            for d in range(NSLAB):
                a0 = SK1 + d * 2048 + cc0
                em.transpose(psb[:, (d % 2) * 128:(d % 2) * 128 + 128],
                             bigv[:, a0:a0 + 128], identb[:, :], PSB,
                             reads=bgr(a0, a0 + 128) + ("identb",))
                em.emit("v", lambda e, d=d: e.tensor_copy(
                        stage_sb[:, (d % 2) * 128:(d % 2) * 128 + 128],
                        psb[:, (d % 2) * 128:(d % 2) * 128 + 128]),
                        reads=(PSB,), writes=(f"stg{d % 2}",))
                for hh in range(2):
                    h = 2 * d + hh
                    yk = scr[:, h * 128:h * 128 + 64]
                    yv = scr[:, h * 128 + 64:h * 128 + 128]
                    st = stage_sb[:, (d % 2) * 128 + hh * 64:
                                  (d % 2) * 128 + hh * 64 + 64]
                    scrk = (f"scr{(h // 4) * 512}", f"stg{d % 2}")
                    em.mm(yk, st,
                          banks[5 + h // 8][0:64, (h % 8) * 64:(h % 8) * 64 + 64],
                          BK[5 + h // 8], start=True, stop=True, reads=scrk)
                    em.mm(st, yv,
                          banks[h // 8][0:64, (h % 8) * 64:(h % 8) * 64 + 64],
                          BK[h // 8], start=True, stop=True, reads=scrk)
            em.emit("s", lambda e: e.activation(wb[0:64, 0:512],
                                                banks[5][0:64, :], AF.Copy),
                    reads=(BK[5],), writes=wbr(0, 512))
            em.emit("s", lambda e: e.activation(wb[0:64, 512:1024],
                                                banks[6][0:64, :], AF.Copy),
                    reads=(BK[6],), writes=wbr(512, 1024))
            em.emit("s", lambda e: e.activation(wb[0:64, 1024:1536],
                                                banks[0][0:64, :], AF.Copy),
                    reads=(BK[0],), writes=wbr(1024, 1536))
            em.emit("s", lambda e: e.activation(wb[0:64, 1536:2048],
                                                banks[1][0:64, :], AF.Copy),
                    reads=(BK[1],), writes=wbr(1536, 2048))
            em.dma("q", abspill[c, :, :], wb[0:64, 0:2048],
                   reads=wbr(0, 2048), writes=(f"absp{c}",))

    def l1_chain():
        em.emit("g", lambda e: e.memset(memst[:, :], 0.0), writes=("mst",))
        em.emit("v", lambda e: e.tensor_copy(
                cumA_sb[:, :].rearrange("p (h f) -> p h f", f=64),
                identb[0:64, 0:64].rearrange("p (o f) -> p o f", o=1)
                .broadcast_to([64, 16, 64])),
                reads=("identb",), writes=("cA",))
        em.emit("v", lambda e: e.tensor_copy(
                cumAT_sb[:, :].rearrange("p (h f) -> p h f", f=64),
                identb[0:64, 0:64].rearrange("p (o f) -> p o f", o=1)
                .broadcast_to([64, 16, 64])),
                reads=("identb",), writes=("cAT",))
        for c in range(16):
            em.dma("q", mcm[c, :, :], memst[:, :], reads=("mst",),
                   writes=(f"mcm{c}",))
            em.dma("q", mca[c, :, :], cumAT_sb[:, :], reads=("cAT",),
                   writes=(f"mca{c}",))
            em.dma("q", wb[0:64, 0:2048], abspill[c, :, :],
                   reads=(f"absp{c}",), writes=wbr(0, 2048))
            em.emit("s", lambda e: e.activation(memb_sb[0:64, :],
                                                memst[:, :], AF.Copy),
                    reads=("mst",), writes=("mbb",))
            for h in range(H1):
                em.mm(wb[0:64, h * 64:(h + 1) * 64],
                      memb_sb[0:64, h * 64:(h + 1) * 64],
                      banks[h // 8][0:64, (h % 8) * 64:(h % 8) * 64 + 64],
                      BK[h // 8], start=True, stop=True,
                      reads=wbr(h * 64, (h + 1) * 64) + ("mbb",))
            for hb in range(2):
                em.emit("v", lambda e, hb=hb: e.tensor_sub(
                        memst[:, hb * 512:(hb + 1) * 512],
                        memst[:, hb * 512:(hb + 1) * 512],
                        banks[hb][0:64, :]),
                        reads=("mst", BK[hb]), writes=("mst",))
                em.emit("v", lambda e, hb=hb: e.tensor_add(
                        memst[:, hb * 512:(hb + 1) * 512],
                        memst[:, hb * 512:(hb + 1) * 512],
                        wb[0:64, 1024 + hb * 512:1024 + (hb + 1) * 512]),
                        reads=("mst",) + wbr(1024 + hb * 512,
                                             1024 + (hb + 1) * 512),
                        writes=("mst",))
            for h in range(H1):
                em.mm(wb[0:64, h * 64:(h + 1) * 64],
                      cumA_sb[0:64, h * 64:(h + 1) * 64],
                      banks[2 + h // 8][0:64, (h % 8) * 64:(h % 8) * 64 + 64],
                      BK[2 + h // 8], start=True, stop=True,
                      reads=wbr(h * 64, (h + 1) * 64) + ("cA",))
            for hb in range(2):
                em.emit("v", lambda e, hb=hb: e.tensor_sub(
                        cumA_sb[:, hb * 512:(hb + 1) * 512],
                        cumA_sb[:, hb * 512:(hb + 1) * 512],
                        banks[2 + hb][0:64, :]),
                        reads=("cA", BK[2 + hb]), writes=("cA",))
            for h in range(H1):
                em.transpose(psb[0:64, h * 64:(h + 1) * 64],
                             cumA_sb[0:64, h * 64:(h + 1) * 64],
                             identb[0:64, 0:64], PSB, reads=("cA", "identb"))
            em.emit("v", lambda e: e.tensor_copy(cumAT_sb[:, :],
                                                 psb[0:64, 0:1024]),
                    reads=(PSB,), writes=("cAT",))
        dbg("l1mst", memst[:, :], ("mst",))

    def l1_ag():
        em.dma("q", mag_in[0:64, :], memst[:, :], reads=("mst",),
               writes=("magi",))
        em.emit("v", lambda e: e.tensor_copy(scrf[0:64, :], cumA_sb[:, :]),
                reads=("cA",), writes=("rf0", "rf1"))
        em.dma("q", mag_in[64:128, :], scrf[0:64, :], reads=("rf0", "rf1"),
               writes=("magi",))
        em.collective("AllGather", ALU.bypass, mag_in.opt(), mag_out.opt(),
                      reads=("magi",), writes=("mago",))
        em.emit("g", lambda e: e.memset(fbuf[0:64, 0:1024], 0.0),
                writes=("fA",))
        for j in range(7):
            em.dma("q", scrf[0:64, :], mag_out[j * 128:j * 128 + 64, :],
                   reads=("mago",), writes=("rf0", "rf1"))
            em.dma("g", memb_sb[0:64, :], mag_out[j * 128 + 64:(j + 1) * 128, :],
                   reads=("mago",), writes=("mbb",))
            for h in range(H1):
                em.transpose(psb[0:64, h * 64:(h + 1) * 64],
                             memb_sb[0:64, h * 64:(h + 1) * 64],
                             identb[0:64, 0:64], PSB, reads=("mbb", "identb"))
            em.emit("v", lambda e: e.tensor_copy(cumAT_sb[:, :],
                                                 psb[0:64, 0:1024]),
                    reads=(PSB,), writes=("cAT",))
            em.emit("v", lambda e: e.tensor_copy(scr[0:64, 0:1024],
                                                 fbuf[0:64, 0:1024]),
                    reads=("fA",), writes=("scr0", "scr512"))
            for h in range(H1):
                em.mm(cumAT_sb[0:64, h * 64:(h + 1) * 64],
                      scr[0:64, h * 64:(h + 1) * 64],
                      banks[h // 8][0:64, (h % 8) * 64:(h % 8) * 64 + 64],
                      BK[h // 8], start=True, stop=True,
                      reads=("cAT", "scr0", "scr512"))
            for hb in range(2):
                em.emit("v", lambda e, hb=hb: e.tensor_sub(
                        fbuf[0:64, 1024 + hb * 512:1024 + (hb + 1) * 512],
                        banks[hb][0:64, :],
                        fbuf[0:64, hb * 512:(hb + 1) * 512]),
                        reads=(BK[hb], "fA"), writes=("fC",))
                em.emit("v", lambda e, hb=hb: e.tensor_add(
                        fbuf[0:64, 1024 + hb * 512:1024 + (hb + 1) * 512],
                        fbuf[0:64, 1024 + hb * 512:1024 + (hb + 1) * 512],
                        scrf[0:64, hb * 512:(hb + 1) * 512]),
                        reads=("fC", "rf0", "rf1"), writes=("fC",))
                em.emit("v", lambda e, hb=hb, j=j: e.scalar_tensor_tensor(
                        fbuf[0:64, hb * 512:(hb + 1) * 512],
                        fbuf[0:64, 1024 + hb * 512:1024 + (hb + 1) * 512],
                        cmask[0:64, j:j + 1],
                        fbuf[0:64, hb * 512:(hb + 1) * 512],
                        ALU.mult, ALU.add),
                        reads=("fC", "cmask", "fA"), writes=("fA",))
        em.emit("v", lambda e: e.tensor_copy(min_sb[:, :],
                                             fbuf[0:64, 0:1024]),
                reads=("fA",), writes=("min",))
        dbg("l1min", fbuf[0:64, 0:1024], ("fA",))

    def l1_out():
        for c in range(16):
            cc0 = c * 128
            em.dma("g", memb_sb[0:64, :], mca[c, :, :], reads=(f"mca{c}",),
                   writes=("mbb",))
            em.dma("q", fbuf[0:64, 0:1024], mcm[c, :, :], reads=(f"mcm{c}",),
                   writes=("fA",))
            em.dma("g", wb[:, 0:2048], yspill[c, :, :], reads=(f"ysp{c}",),
                   writes=wbr(0, 2048))
            em.dma("q", scr[:, 0:1024].rearrange("p (a f) -> p a f", f=128),
                   qspill[:, :, cc0:cc0 + 128], reads=("qsp",),
                   writes=("scr0", "scr512"))
            for h in range(H1):
                em.mm(memb_sb[0:64, h * 64:(h + 1) * 64],
                      min_sb[0:64, h * 64:(h + 1) * 64],
                      banks[h // 8][0:64, (h % 8) * 64:(h % 8) * 64 + 64],
                      BK[h // 8], start=True, stop=True, reads=("mbb", "min"))
            for hb in range(2):
                em.emit("v", lambda e, hb=hb: e.tensor_add(
                        fbuf[0:64, hb * 512:(hb + 1) * 512],
                        fbuf[0:64, hb * 512:(hb + 1) * 512],
                        banks[hb][0:64, :]),
                        reads=("fA", BK[hb]), writes=("fA",))
            em.emit("s", lambda e: e.activation(memb_sb[0:64, :],
                                                fbuf[0:64, 0:1024], AF.Copy),
                    reads=("fA",), writes=("mbb",))
            em.dma("q", memb_sb[64:128, :], memb_sb[0:64, :], reads=("mbb",),
                   writes=("mbb",))
            em.emit("s", lambda e: e.activation(scrf[:, :], scr[:, 0:1024],
                                                AF.Relu, scale=-1.0),
                    reads=("scr0", "scr512"), writes=("rf0", "rf1"))
            em.emit("s", lambda e: e.activation(scrf[:, :], scrf[:, :],
                                                AF.Exp, scale=-1.0),
                    reads=("rf0", "rf1"), writes=("rf0", "rf1"))
            em.emit("s", lambda e: e.activation(scr[:, 1024:2048],
                                                scr[:, 0:1024], AF.Relu),
                    reads=("scr0", "scr512"),
                    writes=("scr1024", "scr1536"))
            em.emit("v", lambda e: e.tensor_add(scr[:, 1024:2048],
                                                scr[:, 1024:2048],
                                                scrf[:, :]),
                    reads=("scr1024", "scr1536", "rf0", "rf1"),
                    writes=("scr1024", "scr1536"))
            l1_den(c, sq_mode=True)
            for (d0, s0) in ((0, 32), (32, 0), (64, 96), (96, 64)):
                em.dma("g", scrf[d0:d0 + 32, :], scr[s0:s0 + 32, 0:1024],
                       reads=("scr0", "scr512"), writes=("rf0", "rf1"))
            em.emit("v", lambda e, cc0=cc0: e.tensor_mul(
                    scr[:, 0:1024].rearrange("p (a f) -> p a f", f=128),
                    scr[:, 0:1024].rearrange("p (a f) -> p a f", f=128),
                    wb[:, 2048 + cc0:2048 + cc0 + 128]
                    .rearrange("p (o f) -> p o f", o=1)
                    .broadcast_to([128, 8, 128])),
                    reads=("scr0", "scr512") + wbr(2048, 4096),
                    writes=("scr0", "scr512"))
            em.emit("v", lambda e, cc0=cc0: e.tensor_mul(
                    scrf[:, :].rearrange("p (a f) -> p a f", f=128),
                    scrf[:, :].rearrange("p (a f) -> p a f", f=128),
                    wb[:, 4096 + cc0:4096 + cc0 + 128]
                    .rearrange("p (o f) -> p o f", o=1)
                    .broadcast_to([128, 8, 128])),
                    reads=("rf0", "rf1") + wbr(4096, 6144),
                    writes=("rf0", "rf1"))
            em.emit("v", lambda e: e.tensor_add(scr[:, 0:1024],
                                                scr[:, 0:1024], scrf[:, :]),
                    reads=("scr0", "scr512", "rf0", "rf1"),
                    writes=("scr0", "scr512"))
            for h in range(H1):
                pb = (h % 2) * 64
                nb = 4 + h // 8
                nsec = (h % 8) * 64
                sq0 = 1024 + (h // 2) * 128
                em.transpose(psb[:, (h % 2) * 128:(h % 2) * 128 + 128],
                             wb[:, h * 128:(h + 1) * 128], identb[:, :], PSB,
                             reads=wbr(h * 128, (h + 1) * 128) + ("identb",))
                em.emit("s", lambda e, h=h: e.activation(
                        stage_sb[:, (h % 2) * 128:(h % 2) * 128 + 128],
                        psb[:, (h % 2) * 128:(h % 2) * 128 + 128], AF.Copy),
                        reads=(PSB,), writes=(f"stg{h % 2}",))
                em.mm(stage_sb[0:64, (h % 2) * 128:(h % 2) * 128 + 128],
                      memb_sb[0:64, h * 64:(h + 1) * 64],
                      banks[h // 8][:, (h % 8) * 64:(h % 8) * 64 + 64],
                      BK[h // 8], start=True, stop=True,
                      reads=(f"stg{h % 2}", "mbb"))
                em.emit("v", lambda e, h=h: e.tensor_sub(
                        wb[:, h * 128:h * 128 + 64],
                        wb[:, h * 128 + 64:h * 128 + 128],
                        banks[h // 8][:, (h % 8) * 64:(h % 8) * 64 + 64]),
                        reads=wbr(h * 128, (h + 1) * 128) + (BK[h // 8],),
                        writes=wbr(h * 128, (h + 1) * 128))
                a0 = SK1 + (h // 2) * 2048 + cc0
                lk = bgr(a0, a0 + 128)
                gb = 2 + h // 8
                gsec = (h % 4) * 128
                rhs_sq = scr[pb:pb + 64, sq0:sq0 + 128]
                if h % 2 == 0:
                    em.mm(bigv[0:64, a0:a0 + 128], rhs_sq,
                          banks[gb][:, gsec:gsec + 128], BK[gb],
                          start=True, stop=True,
                          reads=lk + ("scr1024", "scr1536"))
                else:
                    for sh in range(2):
                        em.mm(bigv[64:128, a0 + sh * 64:a0 + sh * 64 + 64],
                              rhs_sq,
                              banks[gb][sh * 64:sh * 64 + 64, gsec:gsec + 128],
                              BK[gb], start=True, stop=True,
                              reads=lk + ("scr1024", "scr1536"))
                em.emit("v", lambda e, h=h, gb=gb, gsec=gsec: e.tensor_mul(
                        stage_sb[:, (h % 2) * 128:(h % 2) * 128 + 128],
                        banks[gb][:, gsec:gsec + 128], musg_sb[:, :]),
                        reads=(BK[gb], "musg"), writes=(f"stg{h % 2}",))
                if h % 2 == 0:
                    em.mm(scr[0:64, sq0:sq0 + 128],
                          memb_sb[0:64, h * 64:(h + 1) * 64],
                          banks[nb][:, nsec:nsec + 64], BK[nb],
                          start=True, stop=False,
                          reads=("scr1024", "scr1536", "mbb"))
                    em.mm(stage_sb[:, (h % 2) * 128:(h % 2) * 128 + 128],
                          wb[:, h * 128:h * 128 + 64],
                          banks[nb][:, nsec:nsec + 64], BK[nb],
                          start=False, stop=True,
                          reads=(f"stg{h % 2}",) + wbr(h * 128, (h + 1) * 128))
                else:
                    for sh in range(2):
                        em.mm(scr[64:128, sq0 + sh * 64:sq0 + sh * 64 + 64],
                              memb_sb[64:128, h * 64:(h + 1) * 64],
                              banks[nb][sh * 64:sh * 64 + 64, nsec:nsec + 64],
                              BK[nb], start=True, stop=False,
                              reads=("scr1024", "scr1536", "mbb"))
                        em.mm(stage_sb[:, (h % 2) * 128 + sh * 64:
                                       (h % 2) * 128 + sh * 64 + 64],
                              wb[:, h * 128:h * 128 + 64],
                              banks[nb][sh * 64:sh * 64 + 64, nsec:nsec + 64],
                              BK[nb], start=False, stop=True,
                              reads=(f"stg{h % 2}",) + wbr(h * 128,
                                                           (h + 1) * 128))
            for hb in range(2):
                em.emit("v", lambda e, hb=hb: e.tensor_mul(
                        scrf[:, hb * 512:(hb + 1) * 512]
                        .rearrange("p (h f) -> p h f", f=64),
                        banks[4 + hb][:, :].rearrange("p (h f) -> p h f", f=64),
                        fbuf[:, 2048 + hb * 8:2056 + hb * 8]
                        .rearrange("p (h o) -> p h o", o=1)
                        .broadcast_to([128, 8, 64])),
                        reads=(BK[4 + hb], "fD"), writes=(f"rf{hb}",))
            for s in range(4):
                r0 = (s % 2) * 32
                rr = s * 32
                em.dma("g", wb[r0:r0 + 32, 0:1024],
                       xvf[rr:rr + 32, c * 1024:(c + 1) * 1024],
                       reads=(f"vt{c}",), writes=wbr(0, 1024))
                for h in range(H1):
                    pb = (h % 2) * 64
                    ka = KR1 + (h // 2) * 2048 + cc0 + s * 32
                    em.mm(scr[pb:pb + 64,
                              (h // 2) * 128 + s * 32:(h // 2) * 128 + s * 32 + 32],
                          bigv[pb:pb + 64, ka:ka + 32],
                          banks[6][r0:r0 + 32, h * 32:(h + 1) * 32], BK[6],
                          start=True, stop=True,
                          reads=("scr0", "scr512") + bgr(ka, ka + 32),
                          tile_position=(pb, r0))
                em.emit("v", lambda e, r0=r0: e.scalar_tensor_tensor(
                        fbuf[r0:r0 + 32, 1024:1536]
                        .rearrange("p (h k) -> p h k", k=32),
                        banks[6][r0:r0 + 32, :]
                        .rearrange("p (h k) -> p h k", k=32),
                        0.125,
                        cm32_sb[r0:r0 + 32, :]
                        .rearrange("p (o k) -> p o k", o=1)
                        .broadcast_to([32, 16, 32]),
                        ALU.mult, ALU.add),
                        reads=(BK[6], "cm32"), writes=("fC",))
                em.emit("v", lambda e, r0=r0: e.tensor_reduce(
                        fbuf[r0:r0 + 32, 1536:1552],
                        fbuf[r0:r0 + 32, 1024:1536]
                        .rearrange("p (h k) -> p h k", k=32),
                        AX.X, ALU.max),
                        reads=("fC",), writes=("fC",))
                em.emit("v", lambda e, r0=r0: e.tensor_sub(
                        fbuf[r0:r0 + 32, 1024:1536]
                        .rearrange("p (h k) -> p h k", k=32),
                        fbuf[r0:r0 + 32, 1024:1536]
                        .rearrange("p (h k) -> p h k", k=32),
                        fbuf[r0:r0 + 32, 1536:1552]
                        .rearrange("p (h o) -> p h o", o=1)
                        .broadcast_to([32, 16, 32])),
                        reads=("fC",), writes=("fC",))
                em.emit("s", lambda e, r0=r0: e.activation(
                        scr[r0:r0 + 32, 1024:1536],
                        fbuf[r0:r0 + 32, 1024:1536], AF.Exp),
                        reads=("fC",), writes=("scr1024",))
                em.emit("v", lambda e, r0=r0: e.tensor_reduce(
                        fbuf[r0:r0 + 32, 1552:1568],
                        scr[r0:r0 + 32, 1024:1536]
                        .rearrange("p (h k) -> p h k", k=32),
                        AX.X, ALU.add),
                        reads=("scr1024",), writes=("fC",))
                em.emit("v", lambda e, r0=r0: e.reciprocal(
                        fbuf[r0:r0 + 32, 1568:1584],
                        fbuf[r0:r0 + 32, 1552:1568]),
                        reads=("fC",), writes=("fC",))
                em.emit("v", lambda e, r0=r0: e.tensor_mul(
                        scr[r0:r0 + 32, 1024:1536]
                        .rearrange("p (h k) -> p h k", k=32),
                        scr[r0:r0 + 32, 1024:1536]
                        .rearrange("p (h k) -> p h k", k=32),
                        fbuf[r0:r0 + 32, 1568:1584]
                        .rearrange("p (h o) -> p h o", o=1)
                        .broadcast_to([32, 16, 32])),
                        reads=("scr1024", "fC"), writes=("scr1024",))
                em.emit("v", lambda e, r0=r0: e.transpose(
                        scr[r0:r0 + 32, 1536:2048],
                        scr[r0:r0 + 32, 1024:1536]),
                        reads=("scr1024",), writes=("scr1536",))
                for h in range(H1):
                    em.mm(scr[r0:r0 + 32, 1536 + h * 32:1536 + (h + 1) * 32],
                          wb[r0:r0 + 32, h * 64:(h + 1) * 64],
                          banks[2 + h // 8][rr:rr + 32,
                                            (h % 8) * 64:(h % 8) * 64 + 64],
                          BK[2 + h // 8], start=True, stop=True,
                          reads=("scr1536",) + wbr(h * 64, (h + 1) * 64),
                          tile_position=(r0, rr))
            for hb in range(2):
                em.emit("v", lambda e, hb=hb: e.tensor_sub(
                        scrf[:, hb * 512:(hb + 1) * 512],
                        scrf[:, hb * 512:(hb + 1) * 512],
                        banks[2 + hb][:, :]),
                        reads=(f"rf{hb}", BK[2 + hb]), writes=(f"rf{hb}",))
                em.emit("v", lambda e, hb=hb: e.tensor_mul(
                        scrf[:, hb * 512:(hb + 1) * 512]
                        .rearrange("p (h f) -> p h f", f=64),
                        scrf[:, hb * 512:(hb + 1) * 512]
                        .rearrange("p (h f) -> p h f", f=64),
                        betam[:, hb * 8:(hb + 1) * 8]
                        .rearrange("p (h o) -> p h o", o=1)
                        .broadcast_to([128, 8, 64])),
                        reads=(f"rf{hb}", "beta"), writes=(f"rf{hb}",))
                em.emit("v", lambda e, hb=hb: e.tensor_add(
                        wb[:, 1024 + hb * 512:1024 + (hb + 1) * 512],
                        scrf[:, hb * 512:(hb + 1) * 512],
                        banks[2 + hb][:, :]),
                        reads=(f"rf{hb}", BK[2 + hb]),
                        writes=wbr(1024 + hb * 512, 1024 + (hb + 1) * 512))
            for d in range(NSLAB):
                em.transpose(psb[:, d * 128:(d + 1) * 128],
                             wb[:, 1024 + d * 128:1024 + (d + 1) * 128],
                             identb[:, :], PSB,
                             reads=wbr(1024 + d * 128, 1024 + (d + 1) * 128)
                             + ("identb",))
            okeys = tuple(sorted({bgr(KR1 + d * 2048 + cc0,
                                      KR1 + d * 2048 + cc0 + 128)[0]
                                  for d in range(NSLAB)}))
            em.emit("v", lambda e, cc0=cc0: e.tensor_copy(
                    bigv[:, KR1:KR1 + 16384]
                    .rearrange("p (d t) -> p d t", t=2048)[:, :, cc0:cc0 + 128],
                    psb[:, :].rearrange("p (d t) -> p d t", t=128)),
                    reads=(PSB,), writes=okeys)
        dbg("l1ofm", bigv[:, KR1:KR1 + 2048], bgr(KR1, KR1 + 2048))
        wout1 = dram["l1a_wout"].rearrange("(a p) c -> p a c", p=128)
        for od in range(8):
            s0 = (od % 4) * 1024
            wkeys = wbr(s0, s0 + 1024)
            em.dma("g", wb[:, s0:s0 + 1024]
                   .rearrange("p (a f) -> p a f", f=128),
                   wout1[:, :, od * 128:(od + 1) * 128], writes=wkeys)
            wv = wb[:, s0:s0 + 1024].rearrange("p (a f) -> p a f", f=128)
            for ti in range(NTILE):
                c0 = ti * 512
                bank = 4 + ti % 2
                for sl in range(NSLAB):
                    em.mm(wv[:, sl, :],
                          bigv[:, KR1 + sl * 2048 + c0:KR1 + sl * 2048 + c0 + 512],
                          banks[bank][:, :], BK[bank], start=(sl == 0),
                          stop=(sl == NSLAB - 1),
                          reads=wkeys + bgr(KR1 + sl * 2048 + c0,
                                            KR1 + sl * 2048 + c0 + 512))
                em.emit("s", lambda e, bank=bank: e.activation(
                        scrf[:, 0:512], banks[bank][:, :], AF.Copy),
                        reads=(BK[bank],), writes=("rf0",))
                em.dma("q", scrf[:, 512:1024], xspill[:, od, c0:c0 + 512],
                       reads=("xsp",), writes=("rf1",))
                em.emit("v", lambda e, od=od, c0=c0: e.tensor_add(
                        xbuf[:, od, c0:c0 + 512], scrf[:, 0:512],
                        scrf[:, 512:1024]),
                        reads=("rf0", "rf1"),
                        writes=("x", f"vt{od * 2 + c0 // 1024}"))

    def l1_attn():
        em.dma("q", xspill, xbuf[:, :, :], reads=("x",), writes=("xsp",))
        em.dma("q", l1ln_sb[:, :], dram["l1a_lnr"], writes=("l1ln",))
        em.dma("q", cm32_sb[:, :], dram["l1a_cm32"], writes=("cm32",))
        em.dma("q", segdiag[:, :], dram["l1a_segdiag"], writes=("sgd",))
        em.dma("q", betam[:, :], dram["l1a_beta"], writes=("beta",))
        em.emit("s", lambda e: e.activation(betam[:, :], betam[:, :], AF.Exp,
                                            scale=-1.0),
                reads=("beta",), writes=("beta",))
        em.emit("v", lambda e: e.tensor_scalar_add(betam[:, :], betam[:, :],
                                                   1.0),
                reads=("beta",), writes=("beta",))
        em.emit("v", lambda e: e.reciprocal(betam[:, :], betam[:, :]),
                reads=("beta",), writes=("beta",))
        ln_affine(3, composed=True, write_y=False)
        l1_qkv_passes()
        em.dma("g", musg_sb[:, :], dram["l1a_musg"], writes=("musg",))
        em.emit("g", lambda e: e.memset(identb[:, :], 0.0),
                writes=("identb",))
        em.emit("g", lambda e: e.affine_select(
                out=identb[:, :], in_=identb[:, :], compare_op=ALU.not_equal,
                fill=1.0, base=0, pattern=[[-1, 128]], channel_multiplier=1),
                reads=("identb",), writes=("identb",))
        if KCUT3 < 2:
            return
        l1_z()
        if KCUT3 < 3:
            return
        l1_p5()
        if KCUT3 < 4:
            return
        l1_chain()
        l1_ag()
        if KCUT3 < 5:
            return
        l1_out()

    module_ln()
    l0_ffn(first_call=True)
    if not STUB_L0ATTN:
        l0_attn()
    if KCUT >= 99:
        module_ln()
        l0_ffn(first_call=False)
        module_ln()
        l1_ffn()
        if not STUB_L1ATTN:
            l1_attn()
        module_ln()
        l1_ffn()

    otok = em.dma("q", out_ap, xbuf[:, :, :], reads=("x",))
    em._wait("q", otok)

    em.finalize(ctx.enter_context(nc.Block()))
    return nc, ctx


def host_layout(inputs):
    i = inputs
    x = np.asarray(i["L0_attn_x"], np.float32).reshape(SEQ, D)
    xT = np.ascontiguousarray(x.T).reshape(NSLAB, 128, SEQ)

    def coltab(b):
        return np.ascontiguousarray(
            np.asarray(b, np.float32).reshape(-1, 128).T)

    common = {
        "l0f_wfcT": np.ascontiguousarray(
            np.asarray(i["L0_ffn_W_fc"], np.float32).T),
        "l0f_wprT": np.ascontiguousarray(
            np.asarray(i["L0_ffn_W_proj"], np.float32).T),
        "l0f_afcT": np.ascontiguousarray(
            np.asarray(i["L0_ffn_A_fc"], np.float32).T),
        "l0f_bfcT": np.ascontiguousarray(
            np.asarray(i["L0_ffn_B_fc"], np.float32).T),
        "l0f_aprT": np.ascontiguousarray(
            np.asarray(i["L0_ffn_A_proj"], np.float32).T),
        "l0f_bprT": np.ascontiguousarray(
            np.asarray(i["L0_ffn_B_proj"], np.float32).T),
        "l0f_bfc_col": coltab(i["L0_ffn_b_fc"]),
        "l0f_bpr_col": coltab(i["L0_ffn_b_proj"]),
        "l0a_lng_col": np.concatenate([coltab(i["L0_attn_ln1_g"]),
                                       coltab(i["L0_attn_ln2_g"]),
                                       coltab(i["L0_attn_ln3_g"])], axis=1),
        "l0a_lnb_col": np.concatenate([coltab(i["L0_attn_ln1_b"]),
                                       coltab(i["L0_attn_ln2_b"]),
                                       coltab(i["L0_attn_ln3_b"])], axis=1),
        "l0a_wpot": np.ascontiguousarray(
            np.asarray(i["L0_attn_w_pot"], np.float32)[:, ::-1]),
        "l0a_bpot": np.ascontiguousarray(
            np.asarray(i["L0_attn_b_pot"], np.float32)[::-1].reshape(2, 1)),
        "l0a_wbk": np.ascontiguousarray(np.asarray(i["L0_attn_w_bk"], np.float32)),
        "l0a_bbk_col": coltab(i["L0_attn_b_bk"]),
        "l0a_wqkv": np.ascontiguousarray(np.asarray(i["L0_attn_w_qkv"], np.float32)),
        "l0a_bqkv_col": coltab(i["L0_attn_b_qkv"]),
        "l0a_bkv_row": np.ascontiguousarray(
            np.asarray(i["L0_attn_b_qkv"], np.float32)[512:].reshape(1, 1024)),
        "l0a_wout": np.ascontiguousarray(np.asarray(i["L0_attn_w_out"], np.float32)),
        "l0a_bout_col": coltab(i["L0_attn_b_out"]),
        "l0a_wff1": np.ascontiguousarray(np.asarray(i["L0_attn_w_ff1"], np.float32)),
        "l0a_bff1_col": coltab(i["L0_attn_b_ff1"]),
        "l0a_wff2": np.ascontiguousarray(np.asarray(i["L0_attn_w_ff2"], np.float32)),
        "l0a_bff2_col": coltab(i["L0_attn_b_ff2"]),
        "l1f_wgT": np.ascontiguousarray(
            np.asarray(i["L1_ffn_w_gate"], np.int32).T),
        "l1f_wuT": np.ascontiguousarray(
            np.asarray(i["L1_ffn_w_up"], np.int32).T),
        "l1f_wdT": np.ascontiguousarray(
            np.asarray(i["L1_ffn_w_down"], np.int32).T),
    }
    qkperm = np.concatenate([np.arange(0, 64, 2), np.arange(1, 64, 2)])
    permf = np.concatenate([h * 64 + qkperm for h in range(16)])
    wq1 = np.asarray(i["L1_attn_w_qkv"], np.float32).copy()
    wq1[:, 0:1024] = wq1[:, permf]
    wq1[:, 1024:2048] = wq1[:, 1024 + permf]
    common["l1a_wqkv"] = np.ascontiguousarray(wq1)
    common["l1a_wout"] = np.ascontiguousarray(
        np.asarray(i["L1_attn_w_out"], np.float32))
    common["l1a_lnr"] = np.ascontiguousarray(np.concatenate(
        [coltab(i["L1_attn_ln_g"]), coltab(i["L1_attn_ln_b"])], axis=1))
    common["l1a_musg"] = np.ascontiguousarray(
        (np.arange(128)[:, None] // 32
         < np.arange(128)[None, :] // 32).astype(np.float32))
    cm32 = np.where(np.arange(32)[None, :] <= np.arange(32)[:, None],
                    0.0, -1e30).astype(np.float32)
    common["l1a_cm32"] = np.ascontiguousarray(np.concatenate([cm32, cm32], 0))
    common["l1a_segdiag"] = np.ascontiguousarray(
        (np.arange(128)[:, None] // 32
         == np.arange(4)[None, :]).astype(np.float32))
    beta1 = np.asarray(i["L1_attn_mem_beta"], np.float32).reshape(-1)
    common["l1a_beta"] = np.ascontiguousarray(
        np.broadcast_to(beta1[None, :], (128, 16)).copy())
    rrows = np.arange(128) % 64
    rfreq = 10000.0 ** (-(rrows % 32).astype(np.float64) / 32.0)
    rsgn = np.where(rrows < 32, -1.0, 1.0)

    in_maps = []
    for c in range(NCORES):
        m = dict(common)
        xs = xT[:, :, c * T:(c + 1) * T]
        m["x_in"] = np.ascontiguousarray(xs.transpose(1, 0, 2))
        cm = np.zeros((1, 16), np.float32)
        cm[0, 0:c] = 1.0
        cm[0, 8 + c + 1:16] = 1.0
        m["coremask"] = np.ascontiguousarray(np.repeat(cm, 128, axis=0))
        pos = np.arange(c * T, (c + 1) * T, dtype=np.float64)
        ang = rfreq[:, None] * pos[None, :]
        m["l1a_cos"] = np.ascontiguousarray(np.cos(ang).astype(np.float32))
        m["l1a_sin"] = np.ascontiguousarray(
            (rsgn[:, None] * np.sin(ang)).astype(np.float32))
        in_maps.append(m)
    return in_maps


def kernel(**inputs):
    nc, ctx = build_nc()
    in_maps = host_layout(inputs)
    res = run_bass_kernel_spmd(nc, in_maps, core_ids=list(range(NCORES)))
    global LAST_RES
    LAST_RES = res
    ctx.close()
    shards = [np.asarray(res.results[c]["out"]).transpose(1, 0, 2)
              .reshape(D, T) for c in range(NCORES)]
    full = np.concatenate(shards, axis=1)
    return np.ascontiguousarray(full.T).reshape(1, SEQ, D)


# version 06
# speedup vs baseline: 1.0685x; 1.0685x over previous
import numpy as np
import ml_dtypes
import concourse.bass as bass
import concourse.mybir as mybir
from concourse.bass_utils import run_bass_kernel_spmd
from concourse.bass_isa import ReduceOp

F32 = mybir.dt.float32
BF16 = mybir.dt.bfloat16
I32 = mybir.dt.int32
AF = mybir.ActivationFunctionType
ALU = mybir.AluOpType
AX = mybir.AxisListType

NCORES = 8
SEQ = 16384
D = 1024
T = SEQ // NCORES
NSLAB = D // 128
NTILE = T // 512
MAGIC = 12582912.0

STUB_L0ATTN = False
STUB_L1ATTN = False
import os
KDBG = os.environ.get("KDBG", "") == "1"
KCUT = int(os.environ.get("KCUT", "99"))
KCUT2 = int(os.environ.get("KCUT2", "99"))
KCUT3 = int(os.environ.get("KCUT3", "99"))
LAST_RES = None


class EM:

    def __init__(self, nc, ctx):
        self.nc = nc
        self.eng = {"t": nc.tensor, "v": nc.vector, "s": nc.scalar,
                    "g": nc.gpsimd, "q": nc.sync}
        self.esem = {e: ctx.enter_context(nc.semaphore(f"esem_{e}"))
                     for e in self.eng}
        self.ecnt = {e: 0 for e in self.eng}
        self.dmaq = {}
        for e, n in (("q", 8), ("g", 8), ("s", 4)):
            slots = [{"sem": ctx.enter_context(nc.semaphore(f"ds_{e}{i}")),
                      "key": f"d:{e}{i}", "cnt": 0} for i in range(n)]
            self.dmaq[e] = {"slots": slots, "idx": 0}
        self.ccsem = ctx.enter_context(nc.semaphore("ccsem"))
        self.cccnt = 0
        self.bufs = {}
        self.marks = {e: {} for e in self.eng}
        self.sems = {f"e:{e}": s for e, s in self.esem.items()}
        for e in self.dmaq:
            for sl in self.dmaq[e]["slots"]:
                self.sems[sl["key"]] = sl["sem"]
        self.sems["cc"] = self.ccsem
        self.prog = {e: [] for e in self.eng}
        self.aliases = {}

    def _expand(self, keys):
        out = list(keys)
        for k in keys:
            out.extend(self.aliases.get(k, ()))
        return tuple(out)

    def _buf(self, k):
        b = self.bufs.get(k)
        if b is None:
            b = {"w": None, "r": []}
            self.bufs[k] = b
        return b

    def _wait(self, e, token):
        if token is None:
            return
        semkey, val = token
        m = self.marks[e]
        if m.get(semkey, -1) >= val:
            return
        m[semkey] = val
        self.prog[e].append(("wait", semkey, val))

    def _deps(self, e, reads, writes):
        reads = self._expand(reads)
        writes = self._expand(writes)
        for k in reads:
            self._wait(e, self._buf(k)["w"])
        for k in writes:
            b = self._buf(k)
            self._wait(e, b["w"])
            for t in b["r"]:
                self._wait(e, t)

    def _finish(self, e, reads, writes, token):
        reads = self._expand(reads)
        writes = self._expand(writes)
        for k in reads:
            self._buf(k)["r"].append(token)
        for k in writes:
            b = self._buf(k)
            b["w"] = token
            b["r"] = []

    @staticmethod
    def _site():
        import sys
        f = sys._getframe(2)
        return f"{f.f_code.co_filename}:{f.f_lineno}"

    def emit(self, e, fn, reads=(), writes=()):
        fn._site = self._site()
        self._deps(e, reads, writes)
        self.ecnt[e] += 1
        tok = (f"e:{e}", self.ecnt[e])
        self.prog[e].append(("inst", fn, tok[0], tok[1], 1))
        self._finish(e, reads, writes, tok)
        return tok

    def mm(self, lhsT, rhs, out, bank, start, stop, reads=(),
           tile_position=None):
        site = self._site()
        self._deps("t", reads, (bank,) if start else ())
        self.ecnt["t"] += 1
        tok = ("e:t", self.ecnt["t"])

        def fn(eng, lhsT=lhsT, rhs=rhs, out=out, start=start, stop=stop,
               tp=tile_position):
            return eng.matmul(out, lhsT, rhs, start=start, stop=stop,
                              tile_position=tp)
        fn._site = site
        self.prog["t"].append(("inst", fn, tok[0], tok[1], 1))
        for k in self._expand(reads):
            self._buf(k)["r"].append(tok)
        b = self._buf(bank)
        b["w"] = tok
        if start:
            b["r"] = []
        return tok

    def transpose(self, out, in_, ident, bank, reads=()):
        site = self._site()
        self._deps("t", reads, (bank,))
        self.ecnt["t"] += 1
        tok = ("e:t", self.ecnt["t"])

        def fn(eng, out=out, in_=in_, ident=ident):
            return eng.transpose(out, in_, ident)
        fn._site = site
        self.prog["t"].append(("inst", fn, tok[0], tok[1], 1))
        for k in self._expand(reads):
            self._buf(k)["r"].append(tok)
        b = self._buf(bank)
        b["w"] = tok
        b["r"] = []
        return tok

    def dma(self, e, out_ap, in_ap, reads=(), writes=()):
        q = self.dmaq[e]
        sl = q["slots"][q["idx"]]
        q["idx"] = (q["idx"] + 1) % len(q["slots"])
        if sl["cnt"] > 0:
            self._wait(e, (sl["key"], 16 * sl["cnt"]))
        self._deps(e, reads, writes)
        sl["cnt"] += 1
        tok = (sl["key"], 16 * sl["cnt"])

        def fn(eng, out_ap=out_ap, in_ap=in_ap):
            return eng.dma_start(out=out_ap, in_=in_ap)
        self.prog[e].append(("inst", fn, tok[0], tok[1], 16))
        self._finish(e, reads, writes, tok)
        return tok

    def collective(self, kind, op, in_ap, out_ap, reads=(), writes=()):
        self._deps("g", reads, writes)
        self.cccnt += 1
        tok = ("cc", self.cccnt)

        def fn(eng, in_ap=in_ap, out_ap=out_ap, kind=kind, op=op):
            return eng.collective_compute(
                kind, op, replica_groups=[list(range(NCORES))],
                ins=[in_ap], outs=[out_ap])
        self.prog["g"].append(("inst", fn, "cc", self.cccnt, 1))
        self._finish("g", reads, writes, tok)
        return tok

    def finalize(self, block):
        sems = self.sems

        def mk(ename):
            items = self.prog[ename]

            def emitter(eng):
                for it in items:
                    if it[0] == "wait":
                        eng.wait_ge(sems[it[1]], it[2])
                    else:
                        _, fn, semkey, val, delta = it
                        try:
                            fn(eng).then_inc(sems[semkey], delta)
                        except Exception:
                            print("EMIT-SITE:", getattr(fn, "_site", "?"))
                            raise
            return emitter

        block.gpsimd(mk("g"))
        block.vector(mk("v"))
        block.scalar(mk("s"))
        block.tensor(mk("t"))
        block.sync(mk("q"))


def build_nc():
    nc = bass.Bass("TRN2", target_bir_lowering=False, debug=False)
    from contextlib import ExitStack
    ctx = ExitStack()
    dram = {}

    def din(name, shape, dt=F32):
        dram[name] = nc.dram_tensor(name, list(shape), dt,
                                    kind="ExternalInput").ap()
        return dram[name]

    din("x_in", (128, NSLAB, T))
    din("l0f_wfcT", (D, 4096))
    din("l0f_wprT", (4096, D))
    din("l0f_afcT", (D, 16))
    din("l0f_bfcT", (16, 4096))
    din("l0f_aprT", (4096, 16))
    din("l0f_bprT", (16, D))
    din("l0f_bfc_col", (128, 32))
    din("l0f_bpr_col", (128, 8))
    din("coremask", (128, 16))
    din("l0a_lng_col", (128, 24))
    din("l0a_lnb_col", (128, 24))
    din("l0a_wpot", (D, 2))
    din("l0a_bpot", (2, 1))
    din("l0a_wbk", (2, D))
    din("l0a_bbk_col", (128, 8))
    din("l0a_wqkv", (D, 1536))
    din("l0a_bqkv_col", (128, 12))
    din("l0a_bkv_row", (1, 1024))
    din("l0a_wout", (512, D))
    din("l0a_bout_col", (128, 8))
    din("l0a_wff1", (D, 4096))
    din("l0a_bff1_col", (128, 32))
    din("l0a_wff2", (4096, D))
    din("l0a_bff2_col", (128, 8))
    din("l1f_wgT", (D, 3072), I32)
    din("l1f_wuT", (D, 3072), I32)
    din("l1f_wdT", (3072, D), I32)
    din("l1a_wqkv", (D, 3072))
    din("l1a_wout", (D, D))
    din("l1a_lnr", (128, 16))
    din("l1a_cos", (128, T))
    din("l1a_sin", (128, T))
    din("l1a_musg", (128, 128))
    din("l1a_cm32", (64, 32))
    din("l1a_cmfull", (128, 128))
    din("l1a_segdiag", (128, 4))
    din("l1a_beta", (128, 16))

    out_ap = nc.dram_tensor("out", [128, NSLAB, T], F32,
                            kind="ExternalOutput").ap()

    def dbg(name, src_ap, reads):
        if not KDBG:
            return
        shp = list(src_ap.shape)
        t = nc.dram_tensor("dbg_" + name, shp, F32, kind="ExternalOutput").ap()
        eng = "g" if src_ap.dtype == BF16 else "q"
        em.dma(eng, t, src_ap, reads=reads)

    wq_fc = nc.dram_tensor("wq_fc", [128, NSLAB, 4096], BF16,
                           kind="Internal").ap()
    cf_in = nc.dram_tensor("cf_in", [1, 16], F32, kind="Internal").ap()
    cf_out = nc.dram_tensor("cf_out", [NCORES, 16], F32, kind="Internal").ap()
    heb_in = nc.dram_tensor("heb_in", [65, 512], F32, kind="Internal").ap()
    heb_out = nc.dram_tensor("heb_out", [NCORES * 65, 512], F32,
                             kind="Internal").ap()
    wspill = nc.dram_tensor("wspill", [16, 64, 512], F32, kind="Internal").ap()
    crspill = nc.dram_tensor("crspill", [1, 2048], F32, kind="Internal").ap()
    gspill = nc.dram_tensor("gspill", [1, 2048], F32, kind="Internal").ap()
    xspill = nc.dram_tensor("xspill", [128, NSLAB, T], F32, kind="Internal").ap()
    vspill = nc.dram_tensor("vspill", [1, 2048], F32, kind="Internal").ap()
    wq_pr = nc.dram_tensor("wq_pr", [128, 32, D], BF16, kind="Internal").ap()
    qspill = nc.dram_tensor("qspill", [128, NSLAB, T], BF16,
                            kind="Internal").ap()
    yspill = nc.dram_tensor("yspill", [16, 128, 2048], BF16,
                            kind="Internal").ap()
    abspill = nc.dram_tensor("abspill", [16, 64, 2048], BF16,
                             kind="Internal").ap()
    mcm = nc.dram_tensor("mcm", [16, 64, 1024], F32, kind="Internal").ap()
    mca = nc.dram_tensor("mca", [16, 64, 1024], BF16, kind="Internal").ap()
    z_in = nc.dram_tensor("z_in", [128, 8], F32, kind="Internal").ap()
    z_out = nc.dram_tensor("z_out", [NCORES * 128, 8], F32,
                           kind="Internal").ap()
    mag_in = nc.dram_tensor("mag_in", [128, 1024], F32, kind="Internal").ap()
    mag_out = nc.dram_tensor("mag_out", [NCORES * 128, 1024], F32,
                             kind="Internal").ap()

    em = EM(nc, ctx)
    for _k in ("st_m", "st_r", "lngrow", "bkvrow"):
        em.aliases[_k] = ("fA",)
    for _k in ("st_q", "st_m2", "st_v", "st_sd", "st_t", "hebW",
               "cf_dre", "cf_dim", "cf_gre", "cf_gim", "cfMf", "cfMb",
               "cfg", "cfIrow", "cfI", "cfv", "cfvc", "cfME", "cftv",
               "cfsc", "cfT", "cfW", "cfD"):
        em.aliases[_k] = ("fC",)
    for _k in ("pot", "crow64"):
        em.aliases[_k] = ("fA", "fC")
    em.aliases["crsp"] = ()
    for _k in ("hctot", "hincl", "hexcl", "hcc", "hccd", "hscol", "hdecC",
               "hdtot", "hdj", "hfac", "hcd", "fD0", "fD1", "fD2", "fD3",
               "fD4"):
        em.aliases[_k] = ("fD",)
    for _c in range(1024, 1480, 4):
        em.aliases[f"cf{_c}"] = ("fC",)

    def sb(name, shape, dt):
        return ctx.enter_context(nc.sbuf_tensor(name, list(shape), dt))

    xbuf = sb("xbuf", (128, NSLAB, T), F32)
    xnbuf = sb("xnbuf", (128, NSLAB, T), BF16)
    big1 = sb("big1", (128, 32, 1024), BF16)
    bigv = big1[:, :, :].rearrange("p a c -> p (a c)")

    def bgr(c0, c1):
        return tuple(f"bg{k}" for k in range(c0 // 1024, (c1 + 1023) // 1024))
    fbuf = sb("fbuf", (128, 2112), F32)
    wb = sb("wb", (128, 6144), BF16)

    def wbr(c0, c1):
        return tuple(f"w{k}" for k in range(c0 // 512, (c1 + 511) // 512))
    scr = sb("scr", (128, 2048), BF16)
    scrf = sb("scrf", (128, 1024), F32)
    biasb = sb("biasb", (128, 168), F32)
    consts = sb("consts", (128, 16), F32)
    onesb = sb("onesb", (128, 8), BF16)
    onesrow = sb("onesrow", (65, 128), F32)
    identf = sb("identf", (128, 128), F32)
    qround = sb("qround", (128, 256), I32)
    zrow = sb("zrow", (65, 128), F32)
    etam = sb("etam", (128, 128), BF16)
    afcT_sb = sb("afcT_sb", (128, NSLAB, 16), BF16)
    loraB = sb("loraB", (16, 512), BF16)
    aprT_sb = sb("aprT_sb", (128, 32, 16), BF16)
    l1_sb = sb("l1_sb", (16, T), BF16)
    lp_sb = sb("lp_sb", (16, 1024), BF16)

    banks = [ctx.enter_context(nc.psum_tensor(f"pb{i}", [128, 512], F32))
             for i in range(7)]
    BK = [f"ps{i}" for i in range(7)]
    psb = ctx.enter_context(nc.psum_tensor("psb", [128, 1024], BF16))
    PSB = "psb"

    em.emit("g", lambda e: e.memset(onesb[:, :], 1.0), writes=("ones",))
    em.emit("g", lambda e: e.memset(consts[:, 0:1], 1e-5), writes=("c_eps",))
    em.emit("g", lambda e: e.memset(onesrow[:, :], 1.0), writes=("onesrow",))
    em.emit("g", lambda e: e.memset(identf[:, :], 0.0), writes=("identf",))
    em.emit("g", lambda e: e.affine_select(
            out=identf[:, :], in_=identf[:, :], compare_op=ALU.not_equal,
            fill=1.0, base=0, pattern=[[-1, 128]], channel_multiplier=1),
            reads=("identf",), writes=("identf",))
    em.emit("g", lambda e: e.memset(zrow[:, :], 0.0), writes=("zrow",))
    em.emit("g", lambda e: e.memset(consts[:, 1:2], 1.0), writes=("c_one",))
    em.emit("g", lambda e: e.memset(consts[:, 2:3], float(np.log(0.1))),
            writes=("c_lneta",))
    em.emit("g", lambda e: e.memset(etam[:, :], 0.1), writes=("etam",))
    em.emit("g", lambda e: e.affine_select(
            out=etam[:, :], in_=etam[:, :], compare_op=ALU.is_ge, fill=0.0,
            base=0, pattern=[[1, 128]], channel_multiplier=-1),
            reads=("etam",), writes=("etam",))
    em.dma("q", biasb[:, 40:64], dram["l0a_lnb_col"], writes=("lnb",))
    em.dma("q", biasb[:, 0:32], dram["l0f_bfc_col"], writes=("b_fc",))
    em.dma("q", biasb[:, 32:40], dram["l0f_bpr_col"], writes=("b_pr",))
    em.dma("q", xbuf[:, :, :], dram["x_in"], writes=("x",))
    em.dma("g", afcT_sb[:, :, :],
           dram["l0f_afcT"].rearrange("(a p) r -> p a r", p=128),
           writes=("afcT",))
    em.dma("g", aprT_sb[:, :, :],
           dram["l0f_aprT"].rearrange("(a p) r -> p a r", p=128),
           writes=("aprT",))

    def module_ln():
        for ti in range(NTILE):
            c0 = ti * 512
            for d in range(NSLAB):
                ra = (d % 2) * 1024
                ka, kb = f"scr{ra}", f"scr{ra + 512}"
                em.emit("v", lambda e, d=d, c0=c0, ra=ra:
                        e.tensor_copy(scr[:, ra:ra + 512],
                                      xbuf[:, d, c0:c0 + 512]),
                        reads=("x",), writes=(ka,))
                em.emit("s", lambda e, d=d, c0=c0, ra=ra:
                        e.activation(scr[:, ra + 512:ra + 1024],
                                     xbuf[:, d, c0:c0 + 512], AF.Square),
                        reads=("x",), writes=(kb,))
                em.mm(onesb[:, 0:1], scr[:, ra:ra + 512], banks[4][0:1, :],
                      BK[4], start=(d == 0), stop=(d == NSLAB - 1),
                      reads=(ka, "ones"))
                em.mm(onesb[:, 1:2], scr[:, ra + 512:ra + 1024],
                      banks[5][0:1, :], BK[5], start=(d == 0),
                      stop=(d == NSLAB - 1), reads=(kb, "ones"))
            em.emit("v", lambda e: e.tensor_scalar_mul(
                    fbuf[0:1, 0:512], banks[4][0:1, :], 1.0 / D),
                    reads=(BK[4],), writes=("st_m", "fA"))
            em.emit("v", lambda e: e.tensor_scalar_mul(
                    fbuf[0:1, 1024:1536], banks[5][0:1, :], 1.0 / D),
                    reads=(BK[5],), writes=("st_q", "fC"))
            em.emit("v", lambda e: e.tensor_mul(
                    fbuf[0:1, 1536:2048], fbuf[0:1, 0:512], fbuf[0:1, 0:512]),
                    reads=("st_m",), writes=("st_m2", "fC"))
            em.emit("v", lambda e: e.tensor_sub(
                    fbuf[0:1, 1024:1536], fbuf[0:1, 1024:1536],
                    fbuf[0:1, 1536:2048]),
                    reads=("st_q", "st_m2"), writes=("st_v", "fC"))
            em.emit("s", lambda e: e.activation(
                    fbuf[0:1, 1024:1536], fbuf[0:1, 1024:1536], AF.Sqrt,
                    bias=consts[0:1, 0:1]),
                    reads=("st_v", "c_eps"), writes=("st_sd", "fC"))
            em.emit("v", lambda e: e.reciprocal(
                    fbuf[0:1, 512:1024], fbuf[0:1, 1024:1536]),
                    reads=("st_sd",), writes=("st_r", "fA"))
            em.mm(onesrow[0:1, :], fbuf[0:1, 0:512], banks[6][:, :],
                  BK[6], start=True, stop=True, reads=("st_m", "onesrow"))
            em.mm(onesrow[0:1, :], fbuf[0:1, 512:1024], banks[3][:, :],
                  BK[3], start=True, stop=True, reads=("st_r", "onesrow"))
            for d in range(NSLAB):
                em.emit("v", lambda e, d=d, c0=c0:
                        e.tensor_sub(scrf[:, 0:512], xbuf[:, d, c0:c0 + 512],
                                     banks[6][:, :]),
                        reads=("x", BK[6]), writes=("rf0",))
                em.emit("v", lambda e, d=d, c0=c0:
                        e.tensor_mul(xnbuf[:, d, c0:c0 + 512], scrf[:, 0:512],
                                     banks[3][:, :]),
                        reads=("rf0", BK[3]), writes=(f"xn:{d}",))

    def weight_maxabs(name, srcT, rows, cols, qinv_key, qs_key):
        view = srcT.rearrange("(a p) c -> a p c", p=128)
        em.emit("g", lambda e: e.memset(fbuf[:, 2048:2049], 0.0),
                writes=("fD0",))
        for a in range(rows // 128):
            for cc in range(cols // 1024):
                reg = (a * (cols // 1024) + cc) % 2
                fo = reg * 1024
                wkey = "fA" if reg == 0 else "fC"
                em.dma("q", fbuf[:, fo:fo + 1024],
                       view[a, :, cc * 1024:(cc + 1) * 1024], writes=(wkey,))
                em.emit("v", lambda e, fo=fo, reg=reg: e.tensor_reduce(
                        fbuf[:, 2050 + reg:2051 + reg], fbuf[:, fo:fo + 1024],
                        AX.X, ALU.max, apply_absolute_value=True),
                        reads=(wkey,), writes=(f"fD{2 + reg}",))
                em.emit("v", lambda e, reg=reg: e.tensor_max(
                        fbuf[:, 2048:2049], fbuf[:, 2048:2049],
                        fbuf[:, 2050 + reg:2051 + reg]),
                        reads=(f"fD{2 + reg}", "fD0"), writes=("fD0",))
        em.transpose(banks[6][0:1, 0:128], fbuf[:, 2048:2049], identf[:, :],
                     BK[6], reads=("fD0", "identf"))
        em.emit("v", lambda e: e.reduce_max(
                fbuf[0:1, 2052:2053], banks[6][0:1, 0:128], AX.X),
                reads=(BK[6],), writes=("fD4",))
        em.mm(onesrow[0:1, :], fbuf[0:1, 2052:2053], banks[6][:, 0:1], BK[6],
              start=True, stop=True, reads=("fD4", "onesrow"))
        em.emit("v", lambda e: e.tensor_copy(fbuf[:, 2048:2049],
                                             banks[6][:, 0:1]),
                reads=(BK[6],), writes=("fD0",))
        em.emit("v", lambda e: e.reciprocal(fbuf[:, 2049:2050],
                                            fbuf[:, 2048:2049]),
                reads=("fD0",), writes=("fD1",))
        ic, sc = QCOL[qinv_key], QCOL[qs_key]
        em.emit("v", lambda e, ic=ic: e.tensor_scalar_mul(
                consts[:, ic:ic + 1], fbuf[:, 2049:2050], 127.0),
                reads=("fD1",), writes=(qinv_key,))
        em.emit("v", lambda e, sc=sc: e.tensor_scalar_mul(
                consts[:, sc:sc + 1], fbuf[:, 2048:2049], 1.0 / 127.0),
                reads=("fD0",), writes=(qs_key,))

    QCOL = {"qinv_fc": 4, "qs_fc": 5, "qinv_pr": 6, "qs_pr": 7}

    def quant_chunk(src_key, out_bf_ap, qinv_key, qs_key, wkeys, eng="v"):
        ic, sc = QCOL[qinv_key], QCOL[qs_key]
        em.emit(eng, lambda e, ic=ic: e.tensor_scalar(
                fbuf[:, 1024:2048], fbuf[:, 0:1024],
                consts[:, ic:ic + 1], MAGIC, ALU.mult, ALU.add),
                reads=(src_key, qinv_key), writes=("fC",))
        em.emit(eng, lambda e: e.tensor_scalar_add(
                fbuf[:, 1024:2048], fbuf[:, 1024:2048], -MAGIC),
                reads=("fC",), writes=("fC",))
        for hh in range(4):
            o0 = hh * 256
            em.emit(eng, lambda e, o0=o0: e.tensor_copy(
                    qround[:, :], fbuf[:, 1024 + o0:1280 + o0]),
                    reads=("fC",), writes=("qint",))
            em.emit(eng, lambda e, sc=sc, o0=o0, out_bf_ap=out_bf_ap:
                    e.tensor_scalar_mul(out_bf_ap[:, o0:o0 + 256],
                                        qround[:, :], consts[:, sc:sc + 1]),
                    reads=("qint", qs_key), writes=wkeys)

    wfcT_pac = dram["l0f_wfcT"].rearrange("(a p) c -> p a c", p=128)
    wprT_pac = dram["l0f_wprT"].rearrange("(a p) c -> p a c", p=128)

    def l0_ffn(first_call):
        if first_call:
            weight_maxabs("fc", dram["l0f_wfcT"], D, 4096, "qinv_fc", "qs_fc")
            weight_maxabs("pr", dram["l0f_wprT"], 4096, D, "qinv_pr", "qs_pr")
        for ti in range(NTILE):
            c0 = ti * 512
            for d in range(NSLAB):
                em.mm(afcT_sb[:, d, :], xnbuf[:, d, c0:c0 + 512],
                      banks[6][0:16, :], BK[6], start=(d == 0),
                      stop=(d == NSLAB - 1), reads=("afcT", f"xn:{d}"))
            em.emit("s", lambda e, c0=c0: e.activation(
                    l1_sb[:, c0:c0 + 512], banks[6][0:16, :], AF.Copy,
                    scale=2.0),
                    reads=(BK[6],), writes=("l1",))
        for half in range(2):
            h0 = half * 1024
            for c in range(32):
                w0 = (c % 2) * 1024
                wkeys = wbr(w0, w0 + 1024)
                if first_call and half == 0:
                    em.dma("q", fbuf[:, 0:1024]
                           .rearrange("p (a c) -> p a c", c=128),
                           wfcT_pac[:, :, c * 128:(c + 1) * 128],
                           writes=("fA",))
                    quant_chunk("fA", wb[:, w0:w0 + 1024], "qinv_fc", "qs_fc",
                                wkeys)
                    em.dma("q", wq_fc[:, :, c * 128:(c + 1) * 128],
                           wb[:, w0:w0 + 1024]
                           .rearrange("p (a c) -> p a c", c=128),
                           reads=wkeys, writes=(f"wqfcd:{c}",))
                else:
                    em.dma("q", wb[:, w0:w0 + 1024]
                           .rearrange("p (a c) -> p a c", c=128),
                           wq_fc[:, :, c * 128:(c + 1) * 128],
                           reads=(f"wqfcd:{c}",), writes=wkeys)
                bslot = c % 4
                bkey = f"lB{bslot}"
                em.dma("g", loraB[:, bslot * 128:(bslot + 1) * 128],
                       dram["l0f_bfcT"][:, c * 128:(c + 1) * 128],
                       writes=(bkey,))
                wv = wb[:, w0:w0 + 1024].rearrange("p (a c) -> p a c", c=128)
                for tt in range(2):
                    c0 = h0 + tt * 512
                    bank = tt + 2 * (c % 2)
                    for d in range(NSLAB):
                        em.mm(wv[:, d, :], xnbuf[:, d, c0:c0 + 512],
                              banks[bank][:, :], BK[bank], start=(d == 0),
                              stop=False, reads=wkeys + (f"xn:{d}",))
                    em.mm(loraB[:, bslot * 128:(bslot + 1) * 128],
                          l1_sb[:, c0:c0 + 512], banks[bank][:, :], BK[bank],
                          start=False, stop=True, reads=(bkey, "l1"))
                    em.emit("s", lambda e, c=c, tt=tt, bank=bank:
                            e.activation(big1[:, c, tt * 512:tt * 512 + 512],
                                         banks[bank][:, :], AF.Gelu,
                                         bias=biasb[:, c:c + 1]),
                            reads=(BK[bank], "b_fc"), writes=(f"bg{c}",))
            for tt in range(2):
                c0 = tt * 512
                for f in range(32):
                    em.mm(aprT_sb[:, f, :], big1[:, f, c0:c0 + 512],
                          banks[6][0:16, :], BK[6], start=(f == 0),
                          stop=(f == 31), reads=("aprT", f"bg{f}"))
                em.emit("s", lambda e, c0=c0: e.activation(
                        lp_sb[:, c0:c0 + 512],
                        banks[6][0:16, :], AF.Copy, scale=2.0),
                        reads=(BK[6],), writes=("lp",))
            for od in range(8):
                bslot = od % 4
                bkey = f"lB{bslot}"
                em.dma("g", loraB[:, bslot * 128:(bslot + 1) * 128],
                       dram["l0f_bprT"][:, od * 128:(od + 1) * 128],
                       writes=(bkey,))
                gslots = [2048 + ((od * 4 + g) % 4) * 1024 for g in range(4)]
                gkeys = [wbr(s, s + 1024) for s in gslots]
                if first_call and half == 0:
                    for g in range(4):
                        em.dma("q", fbuf[:, 0:1024]
                               .rearrange("p (a c) -> p a c", c=128),
                               wprT_pac[:, g * 8:(g + 1) * 8,
                                        od * 128:(od + 1) * 128],
                               writes=("fA",))
                        quant_chunk("fA",
                                    wb[:, gslots[g]:gslots[g] + 1024],
                                    "qinv_pr", "qs_pr", gkeys[g])
                        em.dma("q", wq_pr[:, g * 8:(g + 1) * 8,
                                          od * 128:(od + 1) * 128],
                               wb[:, gslots[g]:gslots[g] + 1024]
                               .rearrange("p (a c) -> p a c", c=128),
                               reads=gkeys[g], writes=(f"wqprd:{od}:{g}",))
                else:
                    for g in range(4):
                        em.dma("q", wb[:, gslots[g]:gslots[g] + 1024]
                               .rearrange("p (a c) -> p a c", c=128),
                               wq_pr[:, g * 8:(g + 1) * 8,
                                     od * 128:(od + 1) * 128],
                               reads=(f"wqprd:{od}:{g}",),
                               writes=gkeys[g])
                for tt in range(2):
                    c0 = tt * 512
                    bank = tt + 2 * (od % 2)
                    for f in range(32):
                        g = f // 8
                        wv = wb[:, gslots[g]:gslots[g] + 1024].rearrange(
                            "p (a c) -> p a c", c=128)
                        em.mm(wv[:, f % 8, :], big1[:, f, c0:c0 + 512],
                              banks[bank][:, :], BK[bank], start=(f == 0),
                              stop=False, reads=gkeys[g] + (f"bg{f}",))
                    em.mm(loraB[:, bslot * 128:(bslot + 1) * 128],
                          lp_sb[:, c0:c0 + 512],
                          banks[bank][:, :], BK[bank],
                          start=False, stop=True, reads=(bkey, "lp"))
                    g0 = h0 + c0
                    em.emit("s", lambda e, bank=bank, od=od, tt=tt:
                            e.activation(scrf[:, tt * 512:tt * 512 + 512],
                                         banks[bank][:, :], AF.Identity,
                                         bias=biasb[:, 32 + od:33 + od]),
                            reads=(BK[bank], "b_pr"), writes=(f"rf{tt}",))
                    em.emit("v", lambda e, od=od, g0=g0, tt=tt:
                            e.scalar_tensor_tensor(
                                xbuf[:, od, g0:g0 + 512],
                                scrf[:, tt * 512:tt * 512 + 512], 0.5,
                                xbuf[:, od, g0:g0 + 512], ALU.mult, ALU.add),
                            reads=(f"rf{tt}", "x"), writes=("x",))

    wg_pac = dram["l1f_wgT"].rearrange("(a p) c -> p a c", p=128)
    wu_pac = dram["l1f_wuT"].rearrange("(a p) c -> p a c", p=128)
    wd_pac = dram["l1f_wdT"].rearrange("(a p) c -> p a c", p=128)

    def l1_ffn():
        for half in range(2):
            h0 = half * 1024
            for c in range(24):
                g0 = (c % 2) * 2048
                gk, uk = wbr(g0, g0 + 1024), wbr(g0 + 1024, g0 + 2048)
                em.dma("g", wb[:, g0:g0 + 1024]
                       .rearrange("p (a c) -> p a c", c=128),
                       wg_pac[:, :, c * 128:(c + 1) * 128], writes=gk)
                em.dma("g", wb[:, g0 + 1024:g0 + 2048]
                       .rearrange("p (a c) -> p a c", c=128),
                       wu_pac[:, :, c * 128:(c + 1) * 128], writes=uk)
                gv = wb[:, g0:g0 + 1024].rearrange("p (a c) -> p a c", c=128)
                uv = wb[:, g0 + 1024:g0 + 2048].rearrange("p (a c) -> p a c", c=128)
                for tt in range(2):
                    c0 = h0 + tt * 512
                    bg = tt + 2 * (c % 2)
                    bu = 4 + tt
                    for d in range(NSLAB):
                        em.mm(gv[:, d, :], xnbuf[:, d, c0:c0 + 512],
                              banks[bg][:, :], BK[bg], start=(d == 0),
                              stop=(d == NSLAB - 1), reads=gk + (f"xn:{d}",))
                    for d in range(NSLAB):
                        em.mm(uv[:, d, :], xnbuf[:, d, c0:c0 + 512],
                              banks[bu][:, :], BK[bu], start=(d == 0),
                              stop=(d == NSLAB - 1), reads=uk + (f"xn:{d}",))
                    ra = (c % 2) * 1024 + tt * 512
                    em.emit("s", lambda e, bg=bg, ra=ra: e.activation(
                            scr[:, ra:ra + 512], banks[bg][:, :], AF.Silu,
                            scale=1.0 / 127.0),
                            reads=(BK[bg],), writes=(f"scr{ra}",))
                    em.emit("v", lambda e, c=c, tt=tt, bu=bu, ra=ra:
                            e.tensor_mul(big1[:, c, tt * 512:tt * 512 + 512],
                                         scr[:, ra:ra + 512],
                                         banks[bu][:, :]),
                            reads=(f"scr{ra}", BK[bu]), writes=(f"bg{c}",))
            for od in range(8):
                d0 = (od % 2) * 3072
                dk = wbr(d0, d0 + 3072)
                em.dma("g", wb[:, d0:d0 + 3072]
                       .rearrange("p (a c) -> p a c", c=128),
                       wd_pac[:, :, od * 128:(od + 1) * 128], writes=dk)
                dv = wb[:, d0:d0 + 3072].rearrange("p (a c) -> p a c", c=128)
                for tt in range(2):
                    c0 = tt * 512
                    bank = tt + 2 * (od % 2)
                    for f in range(24):
                        em.mm(dv[:, f, :], big1[:, f, c0:c0 + 512],
                              banks[bank][:, :], BK[bank], start=(f == 0),
                              stop=(f == 23), reads=dk + (f"bg{f}",))
                    g0 = h0 + c0
                    em.emit("s", lambda e, bank=bank, tt=tt: e.activation(
                            scrf[:, tt * 512:tt * 512 + 512],
                            banks[bank][:, :], AF.Copy,
                            scale=1.0 / (127.0 * 127.0)),
                            reads=(BK[bank],), writes=(f"rf{tt}",))
                    em.emit("v", lambda e, od=od, g0=g0, tt=tt:
                            e.scalar_tensor_tensor(
                                xbuf[:, od, g0:g0 + 512],
                                scrf[:, tt * 512:tt * 512 + 512], 0.5,
                                xbuf[:, od, g0:g0 + 512], ALU.mult, ALU.add),
                            reads=(f"rf{tt}", "x"), writes=("x",))

    DT_ = 0.01
    QFM0, KFM0, KTOK0, VTOK0 = 0, 8192, 16384, 24576

    cmask = sb("cmask", (128, 16), F32)
    lngcol = sb("lngcol", (128, 24), F32)
    em.dma("q", lngcol[:, :], dram["l0a_lng_col"], writes=("lngcol",))
    em.dma("q", cmask[:, :], dram["coremask"], writes=("cmask",))

    def ln_affine(which, composed, write_y=True):
        b0 = 40 + which * 8
        g0 = which * 8
        for ti in range(NTILE):
            c0 = ti * 512
            for d in range(NSLAB):
                ra = (d % 2) * 1024
                ka, kb = f"scr{ra}", f"scr{ra + 512}"
                em.emit("v", lambda e, d=d, c0=c0, ra=ra:
                        e.tensor_copy(scr[:, ra:ra + 512],
                                      xbuf[:, d, c0:c0 + 512]),
                        reads=("x",), writes=(ka,))
                em.emit("s", lambda e, d=d, c0=c0, ra=ra:
                        e.activation(scr[:, ra + 512:ra + 1024],
                                     xbuf[:, d, c0:c0 + 512], AF.Square),
                        reads=("x",), writes=(kb,))
                em.mm(onesb[:, 0:1], scr[:, ra:ra + 512], banks[4][0:1, :],
                      BK[4], start=(d == 0), stop=(d == NSLAB - 1),
                      reads=(ka, "ones"))
                em.mm(onesb[:, 1:2], scr[:, ra + 512:ra + 1024],
                      banks[5][0:1, :], BK[5], start=(d == 0),
                      stop=(d == NSLAB - 1), reads=(kb, "ones"))
            em.emit("v", lambda e: e.tensor_scalar_mul(
                    fbuf[0:1, 0:512], banks[4][0:1, :], 1.0 / D),
                    reads=(BK[4],), writes=("st_m", "fA"))
            em.emit("v", lambda e: e.tensor_scalar_mul(
                    fbuf[0:1, 1024:1536], banks[5][0:1, :], 1.0 / D),
                    reads=(BK[5],), writes=("st_q", "fC"))
            em.emit("v", lambda e: e.tensor_mul(
                    fbuf[0:1, 1536:2048], fbuf[0:1, 0:512], fbuf[0:1, 0:512]),
                    reads=("st_m",), writes=("st_m2", "fC"))
            em.emit("v", lambda e: e.tensor_sub(
                    fbuf[0:1, 1024:1536], fbuf[0:1, 1024:1536],
                    fbuf[0:1, 1536:2048]),
                    reads=("st_q", "st_m2"), writes=("st_v", "fC"))
            em.emit("s", lambda e: e.activation(
                    fbuf[0:1, 1536:2048], fbuf[0:1, 1024:1536], AF.Sqrt,
                    bias=consts[0:1, 0:1]),
                    reads=("st_v", "c_eps"), writes=("st_sd", "fC"))
            em.emit("v", lambda e: e.reciprocal(
                    fbuf[0:1, 512:1024], fbuf[0:1, 1536:2048]),
                    reads=("st_sd",), writes=("st_r", "fA"))
            if composed:
                em.emit("v", lambda e: e.tensor_mul(
                        fbuf[0:1, 1536:2048], fbuf[0:1, 1024:1536],
                        fbuf[0:1, 512:1024]),
                        reads=("st_v", "st_r"), writes=("st_t", "fC"))
                em.emit("v", lambda e: e.tensor_mul(
                        fbuf[0:1, 1536:2048], fbuf[0:1, 1536:2048],
                        fbuf[0:1, 512:1024]),
                        reads=("st_t", "st_r"), writes=("st_t", "fC"))
                em.emit("s", lambda e: e.activation(
                        fbuf[0:1, 1536:2048], fbuf[0:1, 1536:2048], AF.Sqrt,
                        bias=consts[0:1, 0:1]),
                        reads=("st_t", "c_eps"), writes=("st_t", "fC"))
                em.emit("v", lambda e: e.reciprocal(
                        fbuf[0:1, 1536:2048], fbuf[0:1, 1536:2048]),
                        reads=("st_t",), writes=("st_t", "fC"))
                if write_y:
                    em.mm(onesrow[0:1, :], fbuf[0:1, 512:1024], banks[5][:, :],
                          BK[5], start=True, stop=True,
                          reads=("st_r", "onesrow"))
                em.emit("v", lambda e: e.tensor_mul(
                        fbuf[0:1, 512:1024], fbuf[0:1, 512:1024],
                        fbuf[0:1, 1536:2048]),
                        reads=("st_r", "st_t"), writes=("st_r", "fA"))
            em.mm(onesrow[0:1, :], fbuf[0:1, 0:512], banks[6][:, :],
                  BK[6], start=True, stop=True, reads=("st_m", "onesrow"))
            em.mm(onesrow[0:1, :], fbuf[0:1, 512:1024], banks[3][:, :],
                  BK[3], start=True, stop=True, reads=("st_r", "onesrow"))
            for d in range(NSLAB):
                em.emit("v", lambda e, d=d, c0=c0:
                        e.tensor_sub(scrf[:, 0:512], xbuf[:, d, c0:c0 + 512],
                                     banks[6][:, :]),
                        reads=("x", BK[6]), writes=("rf0",))
                em.emit("v", lambda e, d=d, c0=c0:
                        e.tensor_mul(scrf[:, 512:1024], scrf[:, 0:512],
                                     banks[3][:, :]),
                        reads=("rf0", BK[3]), writes=("rf1",))
                if which == 3:
                    em.emit("s", lambda e, d=d, c0=c0:
                            e.activation(xnbuf[:, d, c0:c0 + 512],
                                         scrf[:, 512:1024], AF.Identity,
                                         bias=l1ln_sb[:, 8 + d:9 + d],
                                         scale=l1ln_sb[:, d:d + 1]),
                            reads=("rf1", "l1ln"), writes=(f"xn:{d}",))
                else:
                    em.emit("s", lambda e, d=d, c0=c0:
                            e.activation(xnbuf[:, d, c0:c0 + 512],
                                         scrf[:, 512:1024], AF.Identity,
                                         bias=biasb[:, b0 + d:b0 + d + 1],
                                         scale=lngcol[:, g0 + d:g0 + d + 1]),
                            reads=("rf1", "lnb", "lngcol"), writes=(f"xn:{d}",))
                if composed and write_y:
                    em.emit("v", lambda e, d=d, c0=c0:
                            e.tensor_mul(xbuf[:, d, c0:c0 + 512],
                                         scrf[:, 0:512], banks[5][:, :]),
                            reads=("rf0", BK[5]), writes=("x",))

    CFK = ("fC",)

    def cfe(eng, fn, reads, writes):
        em.emit(eng, fn, reads=tuple(reads) + CFK, writes=tuple(writes) + CFK)

    def cvw(base, part=(0, 128)):
        g = fbuf[part[0]:part[1], base:base + 4].rearrange(
            "p (c two) -> p c two", two=2)
        return g[:, :, 0:1], g[:, :, 1:2]

    def cf_pass1(eng, Mcols, s_order, rpool, tmp1, tmp2, mkey):
        k1, k2, kf = [f"cf{b}" for b in rpool]
        cur1, cur2, free = rpool
        cfe(eng, lambda e, c=cur1: e.memset(fbuf[:, c:c + 4], 0.0), (), (f"cf{cur1}",))
        cfe(eng, lambda e, c=cur1: e.memset(fbuf[:, c:c + 1], 1.0),
            (f"cf{cur1}",), (f"cf{cur1}",))
        cfe(eng, lambda e, c=cur2: e.memset(fbuf[:, c:c + 4], 0.0), (), (f"cf{cur2}",))
        cfe(eng, lambda e, c=cur2: e.memset(fbuf[:, c + 2:c + 3], 1.0),
            (f"cf{cur2}",), (f"cf{cur2}",))
        for s in s_order:
            dsre = fbuf[:, 1024 + s:1025 + s]
            dsim = fbuf[:, 1040 + s:1041 + s]
            c1re, c1im = cvw(cur1)
            c2re, c2im = cvw(cur2)
            nre, nim = cvw(free)
            t1re, t1im = cvw(tmp1)
            kc1, kc2, kn = f"cf{cur1}", f"cf{cur2}", f"cf{free}"
            kt = f"cf{tmp1}"
            cfe(eng, lambda e, a=t1re, b=c1re, d=dsre: e.tensor_scalar_mul(a, b, d),
                (kc1, "cf_dre"), (kt,))
            cfe(eng, lambda e, a=t1im, b=c1im, d=dsim: e.tensor_scalar_mul(a, b, d),
                (kc1, "cf_dim"), (kt,))
            cfe(eng, lambda e, a=t1re, b=t1im: e.tensor_sub(a, a, b), (kt,), (kt,))
            cfe(eng, lambda e, a=nre, b=t1re, c=c2re: e.tensor_sub(a, b, c),
                (kt, kc2), (kn,))
            cfe(eng, lambda e, a=t1re, b=c1re, d=dsim: e.tensor_scalar_mul(a, b, d),
                (kc1, "cf_dim"), (kt,))
            cfe(eng, lambda e, a=t1im, b=c1im, d=dsre: e.tensor_scalar_mul(a, b, d),
                (kc1, "cf_dre"), (kt,))
            cfe(eng, lambda e, a=t1re, b=t1im: e.tensor_add(a, a, b), (kt,), (kt,))
            cfe(eng, lambda e, a=nim, b=t1re, c=c2im: e.tensor_sub(a, b, c),
                (kt, kc2), (kn,))
            cur1, cur2, free = free, cur1, cur2
        kc1, kc2 = f"cf{cur1}", f"cf{cur2}"
        cfe(eng, lambda e, c=cur1, M=Mcols: e.tensor_copy(
            fbuf[:, M:M + 4], fbuf[:, c:c + 4]), (kc1,), (mkey,))
        cfe(eng, lambda e, c=cur2, M=Mcols: e.tensor_copy(
            fbuf[:, M + 4:M + 8], fbuf[:, c:c + 4]), (kc2,), (mkey,))
        cfe("v", lambda e, M=Mcols, t=tmp1: e.tensor_reduce(
            fbuf[:, t:t + 1], fbuf[:, M:M + 8], AX.X, ALU.max,
            apply_absolute_value=True), (mkey,), (f"cf{tmp1}",))
        cfe("v", lambda e, t=tmp1: e.reciprocal(
            fbuf[:, t + 1:t + 2], fbuf[:, t:t + 1]),
            (f"cf{tmp1}",), (f"cf{tmp1}",))
        cfe(eng, lambda e, M=Mcols, t=tmp1: e.tensor_scalar_mul(
            fbuf[:, M:M + 8], fbuf[:, M:M + 8], fbuf[:, t + 1:t + 2]),
            (mkey, f"cf{tmp1}"), (mkey,))

    def cf_compose(eng, Pc, Sc, Cc, tmp1, tmp2):
        kP, kS, kC = f"cf{Pc}", f"cf{Sc}", f"cf{Cc}"
        kt = f"cf{tmp1}"
        P = fbuf[:, Pc:Pc + 8]
        for row in range(2):
            a1re = P[:, 4 * row + 0:4 * row + 1]
            a1im = P[:, 4 * row + 1:4 * row + 2]
            a2re = P[:, 4 * row + 2:4 * row + 3]
            a2im = P[:, 4 * row + 3:4 * row + 4]
            Sg = fbuf[:, Sc:Sc + 8].rearrange("p (r c two) -> p r c two",
                                              r=2, two=2)
            b1re, b1im = Sg[:, 0, :, 0:1], Sg[:, 0, :, 1:2]
            b2re, b2im = Sg[:, 1, :, 0:1], Sg[:, 1, :, 1:2]
            Cg = fbuf[:, Cc:Cc + 8].rearrange("p (r c two) -> p r c two",
                                              r=2, two=2)
            cre, cim = Cg[:, row, :, 0:1], Cg[:, row, :, 1:2]
            t1re, t1im = cvw(tmp1)
            t2re, t2im = cvw(tmp2)
            cfe(eng, lambda e, o=t1re, x=b1re, s=a1re: e.tensor_scalar_mul(o, x, s), (kP, kS), (kt,))
            cfe(eng, lambda e, o=t2re, x=b1im, s=a1im: e.tensor_scalar_mul(o, x, s), (kP, kS), (f"cf{tmp2}",))
            cfe(eng, lambda e, o=t1re, x=t2re: e.tensor_sub(o, o, x), (kt, f"cf{tmp2}"), (kt,))
            cfe(eng, lambda e, o=t2re, x=b2re, s=a2re: e.tensor_scalar_mul(o, x, s), (kP, kS), (f"cf{tmp2}",))
            cfe(eng, lambda e, o=t1re, x=t2re: e.tensor_add(o, o, x), (kt, f"cf{tmp2}"), (kt,))
            cfe(eng, lambda e, o=t2re, x=b2im, s=a2im: e.tensor_scalar_mul(o, x, s), (kP, kS), (f"cf{tmp2}",))
            cfe(eng, lambda e, o=cre, x=t1re, y=t2re: e.tensor_sub(o, x, y), (kt, f"cf{tmp2}"), (kC,))
            cfe(eng, lambda e, o=t1re, x=b1im, s=a1re: e.tensor_scalar_mul(o, x, s), (kP, kS), (kt,))
            cfe(eng, lambda e, o=t2re, x=b1re, s=a1im: e.tensor_scalar_mul(o, x, s), (kP, kS), (f"cf{tmp2}",))
            cfe(eng, lambda e, o=t1re, x=t2re: e.tensor_add(o, o, x), (kt, f"cf{tmp2}"), (kt,))
            cfe(eng, lambda e, o=t2re, x=b2im, s=a2re: e.tensor_scalar_mul(o, x, s), (kP, kS), (f"cf{tmp2}",))
            cfe(eng, lambda e, o=t1re, x=t2re: e.tensor_add(o, o, x), (kt, f"cf{tmp2}"), (kt,))
            cfe(eng, lambda e, o=t2re, x=b2re, s=a2im: e.tensor_scalar_mul(o, x, s), (kP, kS), (f"cf{tmp2}",))
            cfe(eng, lambda e, o=cim, x=t1re, y=t2re: e.tensor_add(o, x, y), (kt, f"cf{tmp2}"), (kC,))

    def cf_rescale(eng, Pc, tmp1):
        cfe("v", lambda e, P=Pc, t=tmp1: e.tensor_reduce(
            fbuf[:, t:t + 1], fbuf[:, P:P + 8], AX.X, ALU.max,
            apply_absolute_value=True), (f"cf{Pc}",), (f"cf{tmp1}",))
        cfe("v", lambda e, t=tmp1: e.reciprocal(
            fbuf[:, t + 1:t + 2], fbuf[:, t:t + 1]),
            (f"cf{tmp1}",), (f"cf{tmp1}",))
        cfe(eng, lambda e, P=Pc, t=tmp1: e.tensor_scalar_mul(
            fbuf[:, P:P + 8], fbuf[:, P:P + 8], fbuf[:, t + 1:t + 2]),
            (f"cf{Pc}", f"cf{tmp1}"), (f"cf{Pc}",))

    def cf_ident_fill(rows, Sc):
        r0, r1 = rows
        if r1 <= r0:
            return
        em.dma("q", fbuf[r0:r1, Sc:Sc + 8], fbuf[0:r1 - r0, 1464:1472],
               reads=("cfIrow", "fC"), writes=(f"cf{Sc}", "fC"))

    def cf_hs(eng, Mcols, Pc, Shc, P2c, tmp1, tmp2, up, mkey):
        cfe(eng, lambda e, M=Mcols, P=Pc: e.tensor_copy(
            fbuf[:, P:P + 8], fbuf[:, M:M + 8]), (mkey,), (f"cf{Pc}",))
        cur, alt = Pc, P2c
        for k in (1, 2, 4, 8, 16, 32, 64):
            if not up:
                em.dma("q", fbuf[k:128, Shc:Shc + 8], fbuf[0:128 - k, cur:cur + 8],
                       reads=(f"cf{cur}", "fC"), writes=(f"cf{Shc}", "fC"))
                cf_ident_fill((0, k), Shc)
            else:
                em.dma("q", fbuf[0:128 - k, Shc:Shc + 8], fbuf[k:128, cur:cur + 8],
                       reads=(f"cf{cur}", "fC"), writes=(f"cf{Shc}", "fC"))
                cf_ident_fill((128 - k, 128), Shc)
            cf_compose(eng, cur, Shc, alt, tmp1, tmp2)
            cf_rescale(eng, alt, tmp1)
            cur, alt = alt, cur
        return cur

    def cf_mobius_bnd(eng, Pexc, vrow, outc, tmp1, tmp2):
        em.mm(onesrow[0:1, :], fbuf[0:1, vrow:vrow + 4], banks[3][:, 0:4],
              BK[3], start=True, stop=True, reads=("cfv", "onesrow", "fC"))
        cfe(eng, lambda e: e.tensor_copy(fbuf[:, 1412:1416], banks[3][:, 0:4]),
            (BK[3],), ("cfvc",))
        pre = fbuf[:, 1412:1413]
        pim = fbuf[:, 1413:1414]
        qre = fbuf[:, 1414:1415]
        qim = fbuf[:, 1415:1416]
        P = fbuf[:, Pexc:Pexc + 8]
        t = fbuf[:, tmp1:tmp1 + 4]
        u = fbuf[:, tmp2:tmp2 + 4]
        kP, kt, ku = f"cf{Pexc}", f"cf{tmp1}", f"cf{tmp2}"
        for half, oc in ((0, tmp1), (1, tmp2)):
            kk = f"cf{oc}"
            e1re = P[:, 4 * half + 0:4 * half + 1]
            e1im = P[:, 4 * half + 1:4 * half + 2]
            e2re = P[:, 4 * half + 2:4 * half + 3]
            e2im = P[:, 4 * half + 3:4 * half + 4]
            o = fbuf[:, oc:oc + 4]
            cfe(eng, lambda e, o=o, a=e1re, b=pre: e.tensor_mul(o[:, 0:1], a, b), (kP, "cfvc"), (kk,))
            cfe(eng, lambda e, o=o, a=e1im, b=pim: e.tensor_mul(o[:, 2:3], a, b), (kP, "cfvc"), (kk,))
            cfe(eng, lambda e, o=o: e.tensor_sub(o[:, 0:1], o[:, 0:1], o[:, 2:3]), (kk,), (kk,))
            cfe(eng, lambda e, o=o, a=e2re, b=qre: e.tensor_mul(o[:, 2:3], a, b), (kP, "cfvc"), (kk,))
            cfe(eng, lambda e, o=o: e.tensor_add(o[:, 0:1], o[:, 0:1], o[:, 2:3]), (kk,), (kk,))
            cfe(eng, lambda e, o=o, a=e2im, b=qim: e.tensor_mul(o[:, 2:3], a, b), (kP, "cfvc"), (kk,))
            cfe(eng, lambda e, o=o: e.tensor_sub(o[:, 0:1], o[:, 0:1], o[:, 2:3]), (kk,), (kk,))
            cfe(eng, lambda e, o=o, a=e1re, b=pim: e.tensor_mul(o[:, 1:2], a, b), (kP, "cfvc"), (kk,))
            cfe(eng, lambda e, o=o, a=e1im, b=pre: e.tensor_mul(o[:, 2:3], a, b), (kP, "cfvc"), (kk,))
            cfe(eng, lambda e, o=o: e.tensor_add(o[:, 1:2], o[:, 1:2], o[:, 2:3]), (kk,), (kk,))
            cfe(eng, lambda e, o=o, a=e2re, b=qim: e.tensor_mul(o[:, 2:3], a, b), (kP, "cfvc"), (kk,))
            cfe(eng, lambda e, o=o: e.tensor_add(o[:, 1:2], o[:, 1:2], o[:, 2:3]), (kk,), (kk,))
            cfe(eng, lambda e, o=o, a=e2im, b=qre: e.tensor_mul(o[:, 2:3], a, b), (kP, "cfvc"), (kk,))
            cfe(eng, lambda e, o=o: e.tensor_add(o[:, 1:2], o[:, 1:2], o[:, 2:3]), (kk,), (kk,))
        cfe(eng, lambda e: e.tensor_mul(u[:, 2:3], u[:, 0:1], u[:, 0:1]), (ku,), (ku,))
        cfe(eng, lambda e: e.tensor_mul(u[:, 3:4], u[:, 1:2], u[:, 1:2]), (ku,), (ku,))
        cfe(eng, lambda e: e.tensor_add(u[:, 2:3], u[:, 2:3], u[:, 3:4]), (ku,), (ku,))
        cfe(eng, lambda e: e.tensor_scalar_add(u[:, 2:3], u[:, 2:3], 1e-30), (ku,), (ku,))
        cfe("v", lambda e: e.reciprocal(u[:, 3:4], u[:, 2:3]), (ku,), (ku,))
        o = fbuf[:, outc:outc + 2]
        ko = f"cf{outc}"
        cfe(eng, lambda e: e.tensor_mul(t[:, 2:3], t[:, 0:1], u[:, 0:1]), (kt, ku), (kt,))
        cfe(eng, lambda e: e.tensor_mul(t[:, 3:4], t[:, 1:2], u[:, 1:2]), (kt, ku), (kt,))
        cfe(eng, lambda e: e.tensor_add(t[:, 2:3], t[:, 2:3], t[:, 3:4]), (kt,), (kt,))
        cfe(eng, lambda e, o=o: e.tensor_mul(o[:, 0:1], t[:, 2:3], u[:, 3:4]), (kt, ku), (ko,))
        cfe(eng, lambda e: e.tensor_mul(t[:, 2:3], t[:, 1:2], u[:, 0:1]), (kt, ku), (kt,))
        cfe(eng, lambda e: e.tensor_mul(t[:, 3:4], t[:, 0:1], u[:, 1:2]), (kt, ku), (kt,))
        cfe(eng, lambda e: e.tensor_sub(t[:, 2:3], t[:, 2:3], t[:, 3:4]), (kt,), (kt,))
        cfe(eng, lambda e, o=o: e.tensor_mul(o[:, 1:2], t[:, 2:3], u[:, 3:4]), (kt, ku), (ko,))
        cfe(eng, lambda e, o=o: e.tensor_scalar_min(o, o, 1.0e5), (ko,), (ko,))
        cfe(eng, lambda e, o=o: e.tensor_scalar_max(o, o, -1.0e5), (ko,), (ko,))

    def cf_pass3(eng, bndc, s_order, Tre, Tim, tmp1):
        stre, stim = fbuf[:, bndc:bndc + 1], fbuf[:, bndc + 1:bndc + 2]
        kst = f"cf{bndc}"
        t = fbuf[:, tmp1:tmp1 + 4]
        kt = f"cf{tmp1}"
        for s in s_order:
            dsre = fbuf[:, 1024 + s:1025 + s]
            dsim = fbuf[:, 1040 + s:1041 + s]
            ore = fbuf[:, Tre + s:Tre + s + 1]
            oim = fbuf[:, Tim + s:Tim + s + 1]
            cfe(eng, lambda e, a=stre: e.tensor_mul(t[:, 0:1], a, a), (kst,), (kt,))
            cfe(eng, lambda e, a=stim: e.tensor_mul(t[:, 1:2], a, a), (kst,), (kt,))
            cfe(eng, lambda e: e.tensor_add(t[:, 0:1], t[:, 0:1], t[:, 1:2]), (kt,), (kt,))
            cfe(eng, lambda e: e.tensor_scalar_add(t[:, 0:1], t[:, 0:1], 1e-20), (kt,), (kt,))
            cfe("v", lambda e: e.reciprocal(t[:, 1:2], t[:, 0:1]), (kt,), (kt,))
            cfe(eng, lambda e, a=stre: e.tensor_mul(t[:, 2:3], a, t[:, 1:2]), (kst, kt), (kt,))
            cfe(eng, lambda e, a=stim: e.tensor_mul(t[:, 3:4], a, t[:, 1:2]), (kst, kt), (kt,))
            cfe(eng, lambda e, o=ore, d=dsre: e.tensor_scalar(
                o, t[:, 2:3], -1.0, d, ALU.mult, ALU.add),
                (kt, "cf_dre"), ("cfT",))
            cfe(eng, lambda e, o=oim, d=dsim: e.tensor_scalar(
                o, t[:, 3:4], 1.0, d, ALU.mult, ALU.add),
                (kt, "cf_dim"), ("cfT",))
            stre, stim = ore, oim
            kst = "cfT"

    def cf_vcombine(eng, half, j_order, vrow, tmp):
        cfe(eng, lambda e: e.memset(fbuf[0:1, 1376:1384], 0.0), (), ("cfI",))
        cfe(eng, lambda e: e.memset(fbuf[0:1, 1376:1377], 1.0), ("cfI",), ("cfI",))
        cfe(eng, lambda e: e.memset(fbuf[0:1, 1382:1383], 1.0), ("cfI",), ("cfI",))
        cfe(eng, lambda e, v=vrow: e.memset(fbuf[0:1, v:v + 4], 0.0), (), ("cfv",))
        cfe(eng, lambda e, v=vrow: e.memset(fbuf[0:1, v:v + 1], 1.0), ("cfv",), ("cfv",))
        ME, TM = 1392, 1400
        for j in j_order:
            g0 = 1248 + j * 16 + half * 8
            sel = cmask[0:1, half * 8 + j:half * 8 + j + 1]
            cfe(eng, lambda e, g0=g0: e.tensor_sub(
                fbuf[0:1, ME:ME + 8], fbuf[0:1, g0:g0 + 8],
                fbuf[0:1, 1376:1384]), ("cfg", "cfI"), ("cfME",))
            cfe(eng, lambda e, sel=sel: e.tensor_scalar_mul(
                fbuf[0:1, ME:ME + 8], fbuf[0:1, ME:ME + 8], sel),
                ("cfME", "cmask"), ("cfME",))
            cfe(eng, lambda e: e.tensor_add(
                fbuf[0:1, ME:ME + 8], fbuf[0:1, ME:ME + 8],
                fbuf[0:1, 1376:1384]), ("cfME", "cfI"), ("cfME",))
            M = fbuf[0:1, ME:ME + 8]
            v = fbuf[0:1, vrow:vrow + 4]
            tv = fbuf[0:1, TM:TM + 8]
            for row, ob in ((0, 0), (1, 2)):
                a1re, a1im = M[:, 4 * row:4 * row + 1], M[:, 4 * row + 1:4 * row + 2]
                a2re, a2im = M[:, 4 * row + 2:4 * row + 3], M[:, 4 * row + 3:4 * row + 4]
                pre, pim = v[:, 0:1], v[:, 1:2]
                qre, qim = v[:, 2:3], v[:, 3:4]
                o_re, o_im = tv[:, ob:ob + 1], tv[:, ob + 1:ob + 2]
                s1, s2 = tv[:, 4 + 0:4 + 1], tv[:, 4 + 1:4 + 2]
                if ob >= 2:
                    s1, s2 = tv[:, 4 + 2:4 + 3], tv[:, 4 + 3:4 + 4]
                cfe(eng, lambda e, o=o_re, a=a1re, b=pre: e.tensor_mul(o, a, b), ("cfME", "cfv"), ("cftv",))
                cfe(eng, lambda e, s=s1, a=a1im, b=pim: e.tensor_mul(s, a, b), ("cfME", "cfv"), ("cftv",))
                cfe(eng, lambda e, o=o_re, s=s1: e.tensor_sub(o, o, s), ("cftv",), ("cftv",))
                cfe(eng, lambda e, s=s1, a=a2re, b=qre: e.tensor_mul(s, a, b), ("cfME", "cfv"), ("cftv",))
                cfe(eng, lambda e, o=o_re, s=s1: e.tensor_add(o, o, s), ("cftv",), ("cftv",))
                cfe(eng, lambda e, s=s1, a=a2im, b=qim: e.tensor_mul(s, a, b), ("cfME", "cfv"), ("cftv",))
                cfe(eng, lambda e, o=o_re, s=s1: e.tensor_sub(o, o, s), ("cftv",), ("cftv",))
                cfe(eng, lambda e, o=o_im, a=a1re, b=pim: e.tensor_mul(o, a, b), ("cfME", "cfv"), ("cftv",))
                cfe(eng, lambda e, s=s2, a=a1im, b=pre: e.tensor_mul(s, a, b), ("cfME", "cfv"), ("cftv",))
                cfe(eng, lambda e, o=o_im, s=s2: e.tensor_add(o, o, s), ("cftv",), ("cftv",))
                cfe(eng, lambda e, s=s2, a=a2re, b=qim: e.tensor_mul(s, a, b), ("cfME", "cfv"), ("cftv",))
                cfe(eng, lambda e, o=o_im, s=s2: e.tensor_add(o, o, s), ("cftv",), ("cftv",))
                cfe(eng, lambda e, s=s2, a=a2im, b=qre: e.tensor_mul(s, a, b), ("cfME", "cfv"), ("cftv",))
                cfe(eng, lambda e, o=o_im, s=s2: e.tensor_add(o, o, s), ("cftv",), ("cftv",))
            cfe(eng, lambda e, v=vrow: e.tensor_copy(
                fbuf[0:1, v:v + 4], tv[:, 0:4]), ("cftv",), ("cfv",))
            cfe("v", lambda e, v=vrow, t=tmp: e.tensor_reduce(
                fbuf[0:1, t:t + 1], fbuf[0:1, v:v + 4], AX.X, ALU.max,
                apply_absolute_value=True), ("cfv",), ("cfsc",))
            cfe(eng, lambda e, t=tmp: e.tensor_scalar_add(
                fbuf[0:1, t:t + 1], fbuf[0:1, t:t + 1], 1e-30),
                ("cfsc",), ("cfsc",))
            cfe("v", lambda e, t=tmp: e.reciprocal(
                fbuf[0:1, t + 1:t + 2], fbuf[0:1, t:t + 1]), ("cfsc",), ("cfsc",))
            cfe(eng, lambda e, v=vrow, t=tmp: e.tensor_scalar_mul(
                fbuf[0:1, v:v + 4], fbuf[0:1, v:v + 4],
                fbuf[0:1, t + 1:t + 2]), ("cfv", "cfsc"), ("cfv",))

    def cf_run():
        cfe("v", lambda e: e.memset(fbuf[:, 1464:1472], 0.0), (), ("cfIrow",))
        cfe("v", lambda e: e.memset(fbuf[:, 1464:1465], 1.0), ("cfIrow",), ("cfIrow",))
        cfe("v", lambda e: e.memset(fbuf[:, 1470:1471], 1.0), ("cfIrow",), ("cfIrow",))
        cf_pass1("v", 1096, range(16), (1056, 1060, 1064), 1080, 1084, "cfMf")
        cf_pass1("g", 1128, range(15, -1, -1), (1068, 1072, 1076), 1088, 1092, "cfMb")
        pf = cf_hs("v", 1096, 1104, 1112, 1120, 1080, 1084, False, "cfMf")
        pb = cf_hs("g", 1128, 1136, 1144, 1152, 1088, 1092, True, "cfMb")
        em.dma("q", cf_in[0:1, 0:8], fbuf[127:128, pf:pf + 8],
               reads=(f"cf{pf}", "fC"), writes=("cfin",))
        em.dma("q", cf_in[0:1, 8:16], fbuf[0:1, pb:pb + 8],
               reads=(f"cf{pb}", "fC"), writes=("cfin",))
        em.collective("AllGather", ALU.bypass, cf_in.opt(), cf_out.opt(),
                      reads=("cfin",), writes=("cfout",))
        em.dma("q", fbuf[0:1, 1248:1376], cf_out[:, :],
               reads=("cfout",), writes=("cfg", "fC"))
        shf, shb = (1112 if pf != 1112 else 1120), (1144 if pb != 1144 else 1152)
        em.dma("q", fbuf[1:128, shf:shf + 8], fbuf[0:127, pf:pf + 8],
               reads=(f"cf{pf}", "fC"), writes=(f"cf{shf}", "fC"))
        cf_ident_fill((0, 1), shf)
        em.dma("q", fbuf[0:127, shb:shb + 8], fbuf[1:128, pb:pb + 8],
               reads=(f"cf{pb}", "fC"), writes=(f"cf{shb}", "fC"))
        cf_ident_fill((127, 128), shb)
        cf_vcombine("v", 0, range(0, 7), 1408, 1420)
        cf_mobius_bnd("v", shf, 1408, 1224, 1228, 1232)
        cf_vcombine("v", 1, range(7, 0, -1), 1408, 1420)
        cf_mobius_bnd("v", shb, 1408, 1226, 1228, 1232)
        cf_pass3("v", 1224, range(16), 1160, 1176, 1228)
        cf_pass3("g", 1226, range(15, -1, -1), 1192, 1208, 1232)
        W0 = 1424
        cfe("v", lambda e: e.tensor_add(fbuf[:, W0:W0 + 16],
                                        fbuf[:, 1160:1176],
                                        fbuf[:, 1192:1208]),
            ("cfT",), ("cfW",))
        cfe("v", lambda e: e.tensor_sub(fbuf[:, W0:W0 + 16],
                                        fbuf[:, W0:W0 + 16],
                                        fbuf[:, 1024:1040]),
            ("cfW", "cf_dre"), ("cfW",))
        W1 = 1440
        cfe("v", lambda e: e.tensor_add(fbuf[:, W1:W1 + 16],
                                        fbuf[:, 1176:1192],
                                        fbuf[:, 1208:1224]),
            ("cfT",), ("cfW",))
        cfe("v", lambda e: e.tensor_sub(fbuf[:, W1:W1 + 16],
                                        fbuf[:, W1:W1 + 16],
                                        fbuf[:, 1040:1056]),
            ("cfW", "cf_dim"), ("cfW",))
        DD = 1456
        cfe("v", lambda e: e.tensor_mul(fbuf[:, DD:DD + 16],
                                        fbuf[:, W0:W0 + 16],
                                        fbuf[:, W0:W0 + 16]), ("cfW",), ("cfD",))
        cfe("v", lambda e: e.tensor_mul(fbuf[:, DD + 16:DD + 32],
                                        fbuf[:, W1:W1 + 16],
                                        fbuf[:, W1:W1 + 16]), ("cfW",), ("cfD",))
        cfe("v", lambda e: e.tensor_add(fbuf[:, DD:DD + 16],
                                        fbuf[:, DD:DD + 16],
                                        fbuf[:, DD + 16:DD + 32]), ("cfD",), ("cfD",))
        cfe("v", lambda e: e.reciprocal(fbuf[:, DD:DD + 16],
                                        fbuf[:, DD:DD + 16]), ("cfD",), ("cfD",))
        cfe("v", lambda e: e.tensor_mul(fbuf[:, 1984:2000],
                                        fbuf[:, W0:W0 + 16],
                                        fbuf[:, DD:DD + 16]),
            ("cfW", "cfD"), ("cf_gre", "fC"))
        cfe("v", lambda e: e.tensor_mul(fbuf[:, 2000:2016],
                                        fbuf[:, W1:W1 + 16],
                                        fbuf[:, DD:DD + 16]),
            ("cfW", "cfD"), ("cf_gim", "fC"))
        cfe("v", lambda e: e.tensor_scalar_mul(fbuf[:, 2000:2016],
                                               fbuf[:, 2000:2016], -1.0),
            ("cf_gim",), ("cf_gim", "fC"))
        dbg("gre", fbuf[:, 1984:2000], ("cf_gre",))
        dbg("gim", fbuf[:, 2000:2016], ("cf_gim",))

    def l0_attn():
        em.dma("q", xspill, xbuf[:, :, :], reads=("x",), writes=("xsp",))
        ln_affine(0, composed=True)
        dbg("xn1", xnbuf[:, 0, :], ("xn:0",))
        em.dma("g", scr[0:128, 0:16].rearrange("p (a c) -> p a c", c=2),
               dram["l0a_wpot"].rearrange("(a p) c -> p a c", p=128),
               writes=("scr0",))
        em.dma("q", biasb[0:2, 134:135], dram["l0a_bpot"], writes=("bpot",))
        wpotv = scr[0:128, 0:16].rearrange("p (a c) -> p a c", c=2)
        for ti in range(NTILE):
            c0 = ti * 512
            for d in range(NSLAB):
                em.mm(wpotv[:, d, :], xnbuf[:, d, c0:c0 + 512],
                      banks[3][0:2, :], BK[3], start=(d == 0),
                      stop=(d == NSLAB - 1), reads=("scr0", f"xn:{d}"))
            em.emit("s", lambda e, c0=c0: e.activation(
                    fbuf[0:2, c0:c0 + 512], banks[3][0:2, :], AF.Identity,
                    bias=biasb[0:2, 134:135]),
                    reads=(BK[3], "bpot"), writes=("pot", "fA"))
        dbg("potraw", fbuf[0:2, 0:2048], ("pot",))
        em.emit("s", lambda e: e.activation(fbuf[0:1, 0:2048],
                                            fbuf[0:1, 0:2048], AF.Exp),
                reads=("pot",), writes=("pot", "fA"))
        em.emit("s", lambda e: e.activation(fbuf[0:1, 0:2048],
                                            fbuf[0:1, 0:2048], AF.Ln,
                                            bias=consts[0:1, 1:2]),
                reads=("pot", "c_one"), writes=("pot", "fA"))
        em.emit("v", lambda e: e.tensor_scalar_add(fbuf[0:1, 0:2048],
                                                   fbuf[0:1, 0:2048], 0.1),
                reads=("pot",), writes=("pot", "fA"))

        em.dma("q", gspill, fbuf[0:1, 0:2048], reads=("pot",), writes=("gsp",))
        em.dma("q", vspill, fbuf[1:2, 0:2048], reads=("pot",), writes=("vsp",))
        if KCUT < 2:
            return
        for ch in range(16):
            em.emit("v", lambda e, ch=ch: e.tensor_tensor_scan(
                    fbuf[0:1, ch * 128:(ch + 1) * 128],
                    fbuf[0:1, ch * 128:(ch + 1) * 128], zrow[0:1, :],
                    0.0, ALU.add, ALU.add),
                    reads=("pot", "zrow", "gsp"), writes=("pot", "fA"))
        dbg("crow", fbuf[0:1, 0:2048], ("pot",))
        em.dma("q", crspill, fbuf[0:1, 0:2048], reads=("pot",),
               writes=("crsp",))
        em.dma("q", fbuf[:, 1024:1040], vspill, reads=("vsp",),
               writes=("cf_dre", "fC"))
        em.dma("q", fbuf[:, 1040:1056], gspill, reads=("gsp",),
               writes=("cf_dim", "fC"))
        em.emit("v", lambda e: e.tensor_scalar_mul(fbuf[:, 1040:1056],
                                                   fbuf[:, 1040:1056], -1.0),
                reads=("cf_dim",), writes=("cf_dim", "fC"))
        em.emit("v", lambda e: e.tensor_scalar_add(fbuf[:, 1024:1040],
                                                   fbuf[:, 1024:1040], 2.0),
                reads=("cf_dre",), writes=("cf_dre", "fC"))
        cf_run()
        if KCUT < 3:
            return
        em.dma("g", scr[32:33, 0:2048], fbuf[:, 1984:2000],
               reads=("cf_gre",), writes=("scr0", "scr512", "scr1024", "scr1536"))
        em.dma("g", scr[33:34, 0:2048], fbuf[:, 2000:2016],
               reads=("cf_gim",), writes=("scr0", "scr512", "scr1024", "scr1536"))
        em.dma("g", wb[32:34, 0:1024], dram["l0a_wbk"], writes=wbr(0, 1024))
        em.dma("q", biasb[:, 64:72], dram["l0a_bbk_col"], writes=("bbk",))
        for od in range(8):
            for ti in range(NTILE):
                c0 = ti * 512
                bank = (od % 2) * 2 + ti % 2
                em.mm(wb[32:34, od * 128:(od + 1) * 128],
                      scr[32:34, c0:c0 + 512],
                      banks[bank][:, :], BK[bank], start=True, stop=True,
                      reads=("scr0", "scr512", "scr1024", "scr1536") + wbr(0, 1024))
                em.emit("s", lambda e, bank=bank, od=od:
                        e.activation(scrf[:, 0:512], banks[bank][:, :],
                                     AF.Identity,
                                     bias=biasb[:, 64 + od:65 + od]),
                        reads=(BK[bank], "bbk"), writes=("rf0",))
                em.emit("v", lambda e, od=od, c0=c0:
                        e.tensor_add(xbuf[:, od, c0:c0 + 512],
                                     xbuf[:, od, c0:c0 + 512], scrf[:, 0:512]),
                        reads=("rf0", "x"), writes=("x",))
        dbg("xbk", xbuf[:, :, :], ("x",))
        em.dma("q", fbuf[64:65, 0:2048], crspill,
               reads=("crsp",), writes=("crow64", "fA", "fC"))
        if KCUT < 4:
            return
        ln_affine(1, composed=False)
        qkv_and_hebbian()
        if KCUT < 8:
            return
        ln_affine(2, composed=False)
        inner_ffn()
        for od in range(NSLAB):
            for ti in range(NTILE):
                c0 = ti * 512
                rr = ti % 2
                em.dma("q", scrf[:, rr * 512:rr * 512 + 512],
                       xspill[:, od, c0:c0 + 512],
                       reads=("xsp",), writes=(f"rf{rr}",))
                em.emit("v", lambda e, od=od, c0=c0, rr=rr:
                        e.tensor_add(xbuf[:, od, c0:c0 + 512],
                                     xbuf[:, od, c0:c0 + 512],
                                     scrf[:, rr * 512:rr * 512 + 512]),
                        reads=(f"rf{rr}", "x"), writes=("x",))

    def qkv_and_hebbian():
        em.dma("q", biasb[:, 72:84], dram["l0a_bqkv_col"], writes=("bqkv",))
        for oc in range(8):
            w0 = (oc % 2) * 1024
            wkeys = wbr(w0, w0 + 1024)
            em.dma("g", wb[:, w0:w0 + 1024]
                   .rearrange("p (a c) -> p a c", c=128),
                   dram["l0a_wqkv"].rearrange("(a p) c -> p a c", p=128)
                   [:, :, oc * 128:(oc + 1) * 128], writes=wkeys)
            wv = wb[:, w0:w0 + 1024].rearrange("p (a c) -> p a c", c=128)
            base = QFM0 + oc * 2048 if oc < 4 else KFM0 + (oc - 4) * 2048
            for ti in range(NTILE):
                c0 = ti * 512
                bank = ti % 2 + 2 * (oc % 2)
                for d in range(NSLAB):
                    em.mm(wv[:, d, :], xnbuf[:, d, c0:c0 + 512],
                          banks[bank][:, :], BK[bank], start=(d == 0),
                          stop=(d == NSLAB - 1), reads=wkeys + (f"xn:{d}",))
                em.emit("s", lambda e, base=base, c0=c0, bank=bank, oc=oc:
                        e.activation(bigv[:, base + c0:base + c0 + 512],
                                     banks[bank][:, :], AF.Identity,
                                     bias=biasb[:, 72 + oc:73 + oc]),
                        reads=(BK[bank], "bqkv"),
                        writes=bgr(base + c0, base + c0 + 512))
        em.dma("q", fbuf[32:33, 0:1024], dram["l0a_bkv_row"],
               writes=("bkvrow", "fA"))
        for hh in range(2):
            em.mm(onesrow[32:33, :], fbuf[32:33, hh * 512:hh * 512 + 512],
                  banks[6][:, :], BK[6], start=True, stop=True,
                  reads=("bkvrow", "onesrow"))
            em.emit("v", lambda e, hh=hh: e.tensor_copy(
                    scr[:, hh * 512:hh * 512 + 512], banks[6][:, :]),
                    reads=(BK[6],), writes=(f"scr{hh * 512}",))
        for part in range(2):
            for g in range(2):
                em.dma("g", wb[:, 2048 + g * 2048:2048 + (g + 1) * 2048]
                       .rearrange("p (a c) -> p a c", c=512),
                       dram["l0a_wqkv"].rearrange("(a p) c -> p a c", p=128)
                       [:, g * 4:(g + 1) * 4,
                        512 + part * 512:1024 + part * 512],
                       writes=wbr(2048 + g * 2048, 2048 + (g + 1) * 2048))
            tbase = KTOK0 if part == 0 else VTOK0
            for tc in range(16):
                bank = 4 + tc % 2
                for d in range(NSLAB):
                    wv = wb[:, 2048 + d * 512:2048 + (d + 1) * 512]
                    em.mm(xnbuf[:, d, tc * 128:(tc + 1) * 128], wv,
                          banks[bank][:, :], BK[bank], start=(d == 0),
                          stop=(d == NSLAB - 1),
                          reads=wbr(2048 + d * 512, 2048 + (d + 1) * 512)
                          + (f"xn:{d}",))
                em.emit("v", lambda e, tbase=tbase, tc=tc, bank=bank, part=part:
                        e.tensor_add(bigv[:, tbase + tc * 512:
                                          tbase + (tc + 1) * 512],
                                     banks[bank][:, :],
                                     scr[:, part * 512:(part + 1) * 512]),
                        reads=(BK[bank], f"scr{part * 512}"),
                        writes=bgr(tbase + tc * 512, tbase + (tc + 1) * 512))
        dbg("qfm", bigv[:, QFM0:QFM0 + 2048], bgr(QFM0, QFM0 + 2048))
        dbg("ktok", bigv[:, KTOK0:KTOK0 + 512], bgr(KTOK0, KTOK0 + 512))
        dbg("vtok", bigv[:, VTOK0:VTOK0 + 512], bgr(VTOK0, VTOK0 + 512))
        if KCUT < 5:
            return
        em.emit("v", lambda e: e.tensor_copy(
                fbuf[64:65, 2048:2064],
                fbuf[64:65, 0:2048]
                .rearrange("p (a b) -> p a b", b=128)[:, :, 127:128]
                .rearrange("p a b -> p (a b)")),
                reads=("crow64",), writes=("hctot", "fD"))
        em.emit("v", lambda e: e.tensor_tensor_scan(
                fbuf[64:65, 2064:2080], fbuf[64:65, 2048:2064],
                zrow[64:65, 0:16], 0.0, ALU.add, ALU.add),
                reads=("hctot", "zrow"), writes=("hincl", "fD"))
        em.emit("v", lambda e: e.tensor_sub(
                fbuf[64:65, 2080:2096], fbuf[64:65, 2064:2080],
                fbuf[64:65, 2048:2064]),
                reads=("hincl", "hctot"), writes=("hexcl", "fD"))
        em.emit("g", lambda e: e.memset(fbuf[0:64, 1024:1536], 0.0),
                writes=("hebW", "fC"))
        for c in range(16):
            cc0 = c * 128
            em.transpose(banks[3][0:128, 0:1], fbuf[64:65, cc0:cc0 + 128],
                         identf[64:65, 64:65], BK[3],
                         reads=("crow64", "identf"))
            em.emit("v", lambda e: e.tensor_copy(fbuf[:, 2096:2097],
                                                 banks[3][0:128, 0:1]),
                    reads=(BK[3],), writes=("hcc", "fD"))
            em.mm(onesrow[64:65, :], fbuf[64:65, cc0 + 127:cc0 + 128],
                  banks[3][:, 0:1], BK[3], start=True, stop=True,
                  reads=("crow64", "onesrow"))
            em.emit("v", lambda e: e.tensor_sub(fbuf[:, 2097:2098],
                                                fbuf[:, 2096:2097],
                                                banks[3][:, 0:1]),
                    reads=("hcc", BK[3]), writes=("hccd", "fD"))
            em.emit("s", lambda e: e.activation(fbuf[:, 2098:2099],
                                                fbuf[:, 2097:2098], AF.Exp,
                                                bias=consts[:, 2:3],
                                                scale=DT_),
                    reads=("hccd", "c_lneta"), writes=("hscol", "fD"))
            em.emit("s", lambda e: e.activation(fbuf[:, 2099:2100],
                                                banks[3][:, 0:1], AF.Exp,
                                                scale=-DT_),
                    reads=(BK[3],), writes=("hdecC", "fD"))
            em.emit("v", lambda e, cc0=cc0: e.tensor_scalar_mul(
                    scr[:, 0:512],
                    bigv[:, KTOK0 + cc0 * 4:KTOK0 + cc0 * 4 + 512],
                    fbuf[:, 2098:2099]),
                    reads=bgr(KTOK0 + cc0 * 4, KTOK0 + cc0 * 4 + 512)
                    + ("hscol",), writes=("scr0",))
            em.dma("q", wspill[c, :, :], fbuf[0:64, 1024:1536],
                    reads=("hebW",), writes=(f"wsp:{c}",))
            for h in range(8):
                em.mm(scr[:, h * 64:(h + 1) * 64],
                      bigv[:, VTOK0 + cc0 * 4 + h * 64:
                           VTOK0 + cc0 * 4 + (h + 1) * 64],
                      banks[6][0:64, h * 64:(h + 1) * 64], BK[6],
                      start=(h == 0), stop=(h == 7),
                      reads=("scr0",) + bgr(VTOK0 + cc0 * 4,
                                            VTOK0 + cc0 * 4 + 512))
            em.emit("v", lambda e: e.scalar_tensor_tensor(
                    fbuf[0:64, 1024:1536], fbuf[0:64, 1024:1536],
                    fbuf[0:64, 2099:2100], banks[6][0:64, :],
                    ALU.mult, ALU.add),
                    reads=("hebW", "hdecC", BK[6]), writes=("hebW", "fC"))
        if KCUT < 6:
            return
        em.emit("s", lambda e: e.activation(fbuf[64:65, 2100:2101],
                                            fbuf[64:65, 2079:2080], AF.Exp,
                                            scale=-DT_),
                reads=("hincl",), writes=("hdtot", "fD"))
        em.dma("q", heb_in[0:64, :], fbuf[0:64, 1024:1536],
               reads=("hebW",), writes=("hin",))
        for kk in range(4):
            em.dma("q", heb_in[64:65, kk * 128:(kk + 1) * 128],
                   zrow[64:65, 0:128], reads=("zrow",), writes=("hin",))
        em.dma("q", heb_in[64:65, 0:1], fbuf[64:65, 2100:2101],
               reads=("hdtot",), writes=("hin",))
        em.collective("AllGather", ALU.bypass, heb_in.opt(), heb_out.opt(),
                      reads=("hin",), writes=("hout",))
        em.emit("g", lambda e: e.memset(scrf[0:64, 512:1024], 0.0),
                writes=("rf1",))
        for j in range(7):
            em.dma("q", scrf[0:64, 0:512], heb_out[j * 65:j * 65 + 64, :],
                   reads=("hout",), writes=("rf0",))
            em.dma("q", fbuf[64:65, 2101:2102],
                   heb_out[j * 65 + 64:j * 65 + 65, 0:1],
                   reads=("hout",), writes=("hdj", "fD"))
            em.mm(onesrow[64:65, :], fbuf[64:65, 2101:2102], banks[3][:, 0:1],
                  BK[3], start=True, stop=True, reads=("hdj", "onesrow"))
            em.emit("v", lambda e: e.tensor_scalar_sub(
                    fbuf[:, 2102:2103], banks[3][:, 0:1], 1.0),
                    reads=(BK[3],), writes=("hfac", "fD"))
            em.emit("v", lambda e, j=j: e.tensor_scalar(
                    fbuf[:, 2102:2103], fbuf[:, 2102:2103],
                    cmask[:, j:j + 1], 1.0, ALU.mult, ALU.add),
                    reads=("hfac", "cmask"), writes=("hfac", "fD"))
            em.emit("v", lambda e, j=j: e.tensor_scalar_mul(
                    scrf[0:64, 0:512], scrf[0:64, 0:512],
                    cmask[0:64, j:j + 1]),
                    reads=("rf0", "cmask"), writes=("rf0",))
            em.emit("v", lambda e: e.scalar_tensor_tensor(
                    scrf[0:64, 512:1024], scrf[0:64, 512:1024],
                    fbuf[0:64, 2102:2103], scrf[0:64, 0:512],
                    ALU.mult, ALU.add),
                    reads=("rf1", "hfac", "rf0"), writes=("rf1",))
        if KCUT < 7:
            return
        for c in range(16):
            cc0 = c * 128
            em.dma("q", scrf[0:64, 0:512], wspill[c, :, :],
                   reads=(f"wsp:{c}",), writes=("rf0",))
            em.mm(onesrow[64:65, :], fbuf[64:65, 2080 + c:2081 + c],
                  banks[3][:, 0:1], BK[3], start=True, stop=True,
                  reads=("hexcl", "onesrow"))
            em.emit("s", lambda e: e.activation(fbuf[:, 2105:2106],
                                                banks[3][:, 0:1], AF.Exp,
                                                scale=-DT_),
                    reads=(BK[3],), writes=("hcd", "fD"))
            em.emit("v", lambda e: e.scalar_tensor_tensor(
                    scr[0:64, 1536:2048], scrf[0:64, 512:1024],
                    fbuf[0:64, 2105:2106], scrf[0:64, 0:512],
                    ALU.mult, ALU.add),
                    reads=("rf1", "hcd", "rf0"), writes=("scr1536",))
            em.dma("q", scr[64:128, 1536:2048], scr[0:64, 1536:2048],
                   reads=("scr1536",), writes=("scr1536",))
            if KCUT2 < 2:
                continue
            em.transpose(banks[3][0:128, 0:1], fbuf[64:65, cc0:cc0 + 128],
                         identf[64:65, 64:65], BK[3],
                         reads=("crow64", "identf"))
            em.emit("v", lambda e: e.tensor_copy(fbuf[:, 2096:2097],
                                                 banks[3][0:128, 0:1]),
                    reads=(BK[3],), writes=("hcc", "fD"))
            em.mm(onesrow[64:65, :], fbuf[64:65, cc0:cc0 + 128],
                  banks[3][:, 0:128], BK[3], start=True, stop=True,
                  reads=("crow64", "onesrow"))
            em.emit("v", lambda e: e.tensor_scalar(
                    scrf[:, 0:128], banks[3][:, 0:128],
                    fbuf[:, 2096:2097], -DT_, ALU.subtract, ALU.mult),
                    reads=(BK[3], "hcc"), writes=("rf0",))
            em.emit("s", lambda e: e.activation(scrf[:, 128:256],
                                                scrf[:, 0:128], AF.Exp),
                    reads=("rf0",), writes=("rf0",))
            em.emit("v", lambda e: e.tensor_mul(
                    scr[:, 1024:1152].rearrange("p (o c) -> p o c", o=1),
                    scrf[:, 128:256].rearrange("p (o c) -> p o c", o=1),
                    etam[:, :].rearrange("p (o c) -> p o c", o=1)),
                    reads=("rf0", "etam"), writes=("scr1024",))
            if KCUT2 < 3:
                continue
            for h in range(8):
                bank = h // 4
                kslab = KFM0 + (h // 2) * 2048 + cc0
                qslab = QFM0 + (h // 2) * 2048 + cc0
                rds = bgr(kslab, kslab + 128) + bgr(qslab, qslab + 128)
                pb = (h % 2) * 64
                rhs = bigv[pb:pb + 64, qslab:qslab + 128]
                if h % 2 == 0:
                    em.mm(bigv[0:64, kslab:kslab + 128], rhs,
                          banks[bank][:, (h % 4) * 128:(h % 4) * 128 + 128],
                          BK[bank], start=True, stop=True, reads=rds)
                else:
                    for sh in range(2):
                        em.mm(bigv[64:128, kslab + sh * 64:kslab + sh * 64 + 64],
                              rhs,
                              banks[bank][sh * 64:sh * 64 + 64,
                                          (h % 4) * 128:(h % 4) * 128 + 128],
                              BK[bank], start=True, stop=True, reads=rds)
            for bank in range(2):
                em.emit("v", lambda e, bank=bank: e.tensor_mul(
                        scr[:, bank * 512:bank * 512 + 512]
                        .rearrange("p (a c) -> p a c", c=128),
                        banks[bank][:, :].rearrange("p (a c) -> p a c", c=128),
                        scr[:, 1024:1152].rearrange("p (o c) -> p o c", o=1)
                        .broadcast_to([128, 4, 128])),
                        reads=(BK[bank], "scr1024"),
                        writes=(f"scr{bank * 512}",))
            if KCUT2 < 4:
                continue
            for h in range(8):
                sec = h // 2
                ob = 2 if h % 2 == 0 else 4
                pb = (h % 2) * 64
                em.mm(bigv[:, VTOK0 + cc0 * 4 + h * 64:
                           VTOK0 + cc0 * 4 + (h + 1) * 64],
                      scr[:, (h // 4) * 512 + (h % 4) * 128:
                          (h // 4) * 512 + (h % 4) * 128 + 128],
                      banks[ob][pb:pb + 64, sec * 128:(sec + 1) * 128],
                      BK[ob], start=(h < 2), stop=False,
                      reads=bgr(VTOK0 + cc0 * 4, VTOK0 + cc0 * 4 + 512)
                      + (f"scr{(h // 4) * 512}",))
            if KCUT2 < 5:
                continue
            em.emit("v", lambda e: e.tensor_scalar_mul(
                    scrf[:, 0:128], banks[3][:, 0:128], -DT_),
                    reads=(BK[3],), writes=("rf0",))
            em.emit("s", lambda e: e.activation(scrf[:, 128:256],
                                                scrf[:, 0:128], AF.Exp),
                    reads=("rf0",), writes=("rf0",))
            for q in range(4):
                em.emit("v", lambda e, q=q, cc0=cc0: e.tensor_mul(
                        scr[:, q * 128:(q + 1) * 128],
                        bigv[:, QFM0 + q * 2048 + cc0:
                             QFM0 + q * 2048 + cc0 + 128],
                        scrf[:, 128:256]),
                        reads=bgr(QFM0 + q * 2048 + cc0,
                                  QFM0 + q * 2048 + cc0 + 128) + ("rf0",),
                        writes=("scr0",))
            for h in range(8):
                sec = h // 2
                ob = 2 if h % 2 == 0 else 4
                pb = (h % 2) * 64
                em.mm(scr[(h % 2) * 64:(h % 2) * 64 + 64,
                          1536 + h * 64:1536 + (h + 1) * 64],
                      scr[(h % 2) * 64:(h % 2) * 64 + 64,
                          (h // 2) * 128:(h // 2) * 128 + 128],
                      banks[ob][pb:pb + 64, sec * 128:(sec + 1) * 128],
                      BK[ob], start=False, stop=(h >= 6),
                      reads=("scr1536", "scr0"))
            if KCUT2 < 6:
                continue
            for s4 in range(4):
                em.emit("v", lambda e, s4=s4, cc0=cc0: e.tensor_copy(
                        bigv[0:64, KTOK0 + s4 * 2048 + cc0:
                             KTOK0 + s4 * 2048 + cc0 + 128],
                        banks[2][0:64, s4 * 128:(s4 + 1) * 128]),
                        reads=(BK[2],),
                        writes=bgr(KTOK0 + s4 * 2048 + cc0,
                                   KTOK0 + s4 * 2048 + cc0 + 128))
                em.emit("v", lambda e, s4=s4, cc0=cc0: e.tensor_copy(
                        bigv[64:128, KTOK0 + s4 * 2048 + cc0:
                             KTOK0 + s4 * 2048 + cc0 + 128],
                        banks[4][64:128, s4 * 128:(s4 + 1) * 128]),
                        reads=(BK[4],),
                        writes=bgr(KTOK0 + s4 * 2048 + cc0,
                                   KTOK0 + s4 * 2048 + cc0 + 128))
        for s4 in range(4):
            dbg(f"heb{s4}", bigv[:, KTOK0 + s4 * 2048:KTOK0 + (s4 + 1) * 2048],
                bgr(KTOK0 + s4 * 2048, KTOK0 + (s4 + 1) * 2048))
        if KCUT < 7.5:
            pass
        em.dma("g", wb[:, 0:4096].rearrange("p (a c) -> p a c", c=1024),
               dram["l0a_wout"].rearrange("(a p) c -> p a c", p=128),
               writes=wbr(0, 4096))
        em.dma("q", biasb[:, 84:92], dram["l0a_bout_col"], writes=("bout",))
        for od in range(8):
            for ti in range(NTILE):
                c0 = ti * 512
                bank = 4 + ti % 2
                for sl in range(4):
                    em.mm(wb[:, sl * 1024 + od * 128:sl * 1024 + (od + 1) * 128],
                          bigv[:, KTOK0 + sl * 2048 + c0:
                               KTOK0 + sl * 2048 + c0 + 512],
                          banks[bank][:, :], BK[bank], start=(sl == 0),
                          stop=(sl == 3),
                          reads=wbr(sl * 1024 + od * 128,
                                    sl * 1024 + (od + 1) * 128)
                          + bgr(KTOK0 + sl * 2048 + c0,
                                KTOK0 + sl * 2048 + c0 + 512))
                em.emit("s", lambda e, bank=bank, od=od:
                        e.activation(scrf[:, 0:512], banks[bank][:, :],
                                     AF.Identity,
                                     bias=biasb[:, 84 + od:85 + od]),
                        reads=(BK[bank], "bout"), writes=("rf0",))
                em.emit("v", lambda e, od=od, c0=c0:
                        e.tensor_add(xbuf[:, od, c0:c0 + 512],
                                     xbuf[:, od, c0:c0 + 512], scrf[:, 0:512]),
                        reads=("rf0", "x"), writes=("x",))
        dbg("xwo", xbuf[:, :, :], ("x",))

    def inner_ffn():
        em.dma("q", biasb[:, 92:124], dram["l0a_bff1_col"], writes=("bff1",))
        em.dma("q", biasb[:, 124:132], dram["l0a_bff2_col"], writes=("bff2",))
        for half in range(2):
            h0 = half * 1024
            for c in range(32):
                w0 = (c % 2) * 1024
                wkeys = wbr(w0, w0 + 1024)
                em.dma("g", wb[:, w0:w0 + 1024]
                       .rearrange("p (a c) -> p a c", c=128),
                       dram["l0a_wff1"].rearrange("(a p) c -> p a c", p=128)
                       [:, :, c * 128:(c + 1) * 128], writes=wkeys)
                wv = wb[:, w0:w0 + 1024].rearrange("p (a c) -> p a c", c=128)
                for tt in range(2):
                    c0 = h0 + tt * 512
                    bank = tt + 2 * (c % 2)
                    for d in range(NSLAB):
                        em.mm(wv[:, d, :], xnbuf[:, d, c0:c0 + 512],
                              banks[bank][:, :], BK[bank], start=(d == 0),
                              stop=(d == NSLAB - 1),
                              reads=wkeys + (f"xn:{d}",))
                    em.emit("s", lambda e, c=c, tt=tt, bank=bank:
                            e.activation(big1[:, c, tt * 512:tt * 512 + 512],
                                         banks[bank][:, :],
                                         AF.Gelu_apprx_tanh,
                                         bias=biasb[:, 92 + c:93 + c]),
                            reads=(BK[bank], "bff1"), writes=(f"bg{c}",))
            for od in range(8):
                gslots = [2048 + ((od * 4 + g) % 4) * 1024 for g in range(4)]
                gkeys = [wbr(s, s + 1024) for s in gslots]
                for g in range(4):
                    em.dma("g", wb[:, gslots[g]:gslots[g] + 1024]
                           .rearrange("p (a c) -> p a c", c=128),
                           dram["l0a_wff2"].rearrange("(a p) c -> p a c", p=128)
                           [:, g * 8:(g + 1) * 8, od * 128:(od + 1) * 128],
                           writes=gkeys[g])
                for tt in range(2):
                    c0 = tt * 512
                    bank = tt + 2 * (od % 2)
                    for f in range(32):
                        g = f // 8
                        wv = wb[:, gslots[g]:gslots[g] + 1024].rearrange(
                            "p (a c) -> p a c", c=128)
                        em.mm(wv[:, f % 8, :], big1[:, f, c0:c0 + 512],
                              banks[bank][:, :], BK[bank], start=(f == 0),
                              stop=(f == 31), reads=gkeys[g] + (f"bg{f}",))
                    g0 = h0 + c0
                    em.emit("s", lambda e, bank=bank, od=od, tt=tt:
                            e.activation(scrf[:, tt * 512:tt * 512 + 512],
                                         banks[bank][:, :], AF.Identity,
                                         bias=biasb[:, 124 + od:125 + od]),
                            reads=(BK[bank], "bff2"), writes=(f"rf{tt}",))
                    em.emit("v", lambda e, od=od, g0=g0, tt=tt:
                            e.tensor_add(xbuf[:, od, g0:g0 + 512],
                                         xbuf[:, od, g0:g0 + 512],
                                         scrf[:, tt * 512:tt * 512 + 512]),
                            reads=(f"rf{tt}", "x"), writes=("x",))

    H1 = 16
    KR1, SK1 = 0, 16384

    l1ln_sb = sb("l1ln_sb", (128, 16), F32)
    cm32_sb = sb("cm32_sb", (64, 32), F32)
    cmf_sb = sb("cmf_sb", (128, 128), F32)
    segdiag = sb("segdiag", (128, 4), F32)
    betam = sb("betam", (128, 16), F32)
    musg_sb = xnbuf[:, 5, 0:128]
    identb = xnbuf[:, 5, 128:256]
    stage_sb = xnbuf[:, 5, 256:512]
    zt_sb = xnbuf[:, 5, 512:1024]
    cumA_sb = xnbuf[0:64, 6, 0:1024]
    cumAT_sb = xnbuf[0:64, 6, 1024:2048]
    memb_sb = xnbuf[:, 7, 0:1024]
    min_sb = xnbuf[0:64, 7, 1024:2048]
    memst = fbuf[0:64, 1024:2048]
    for _k in ("musg", "identb", "stg0", "stg1", "zt"):
        em.aliases[_k] = ("xn:5",)
    for _k in ("cA", "cAT"):
        em.aliases[_k] = ("xn:6",)
    for _k in ("mbb", "min"):
        em.aliases[_k] = ("xn:7",)
    em.aliases["mst"] = ("fC",)
    xvf = xbuf[:, :, :].rearrange("p a c -> p (a c)")
    wqkv1 = dram["l1a_wqkv"].rearrange("(a p) c -> p a c", p=128)

    def l1_qkv_passes():
        for pp, wcol in (("k", 1024), ("q", 0)):
            for oc in range(8):
                w0 = (oc % 2) * 1024
                wkeys = wbr(w0, w0 + 1024)
                em.dma("g", wb[:, w0:w0 + 1024]
                       .rearrange("p (a c) -> p a c", c=128),
                       wqkv1[:, :, wcol + oc * 128:wcol + (oc + 1) * 128],
                       writes=wkeys)
                wv = wb[:, w0:w0 + 1024].rearrange("p (a c) -> p a c", c=128)
                for ti in range(NTILE):
                    c0 = ti * 512
                    bank = ti % 2 + 2 * (oc % 2)
                    for d in range(NSLAB):
                        em.mm(wv[:, d, :], xnbuf[:, d, c0:c0 + 512],
                              banks[bank][:, :], BK[bank], start=(d == 0),
                              stop=(d == NSLAB - 1), reads=wkeys + (f"xn:{d}",))
                    if pp == "k":
                        a0 = KR1 + oc * 2048 + c0
                        em.emit("s", lambda e, a0=a0, bank=bank: e.activation(
                                bigv[:, a0:a0 + 512], banks[bank][:, :],
                                AF.Copy),
                                reads=(BK[bank],), writes=bgr(a0, a0 + 512))
                    else:
                        rr = ((oc * NTILE + ti) % 4) * 512
                        em.emit("s", lambda e, rr=rr, bank=bank: e.activation(
                                scr[:, rr:rr + 512], banks[bank][:, :],
                                AF.Copy),
                                reads=(BK[bank],), writes=(f"scr{rr}",))
                        em.dma("q", qspill[:, oc, c0:c0 + 512],
                               scr[:, rr:rr + 512],
                               reads=(f"scr{rr}",), writes=("qsp",))
        for d in range(NSLAB):
            ks = bigv[:, KR1 + d * 2048:KR1 + (d + 1) * 2048]
            ss = bigv[:, SK1 + d * 2048:SK1 + (d + 1) * 2048]
            kk = bgr(KR1 + d * 2048, KR1 + (d + 1) * 2048)
            sk = bgr(SK1 + d * 2048, SK1 + (d + 1) * 2048)
            em.emit("s", lambda e, ks=ks: e.activation(
                    fbuf[:, 0:2048], ks, AF.Relu, scale=-1.0),
                    reads=kk, writes=("fA", "fC"))
            em.emit("s", lambda e: e.activation(
                    fbuf[:, 0:2048], fbuf[:, 0:2048], AF.Exp, scale=-1.0),
                    reads=("fA", "fC"), writes=("fA", "fC"))
            em.emit("s", lambda e, ss=ss, ks=ks: e.activation(ss, ks, AF.Relu),
                    reads=kk, writes=sk)
            em.emit("v", lambda e, ss=ss: e.tensor_add(ss, ss,
                                                       fbuf[:, 0:2048]),
                    reads=sk + ("fA", "fC"), writes=sk)
        for hf in range(2):
            wkeys = wbr(0, 4096)
            em.dma("g", wb[:, 0:4096].rearrange("p (a c) -> p a c", c=512),
                   wqkv1[:, :, 2048 + hf * 512:2048 + hf * 512 + 512],
                   writes=wkeys)
            for tc in range(16):
                bank = 4 + tc % 2
                for d in range(NSLAB):
                    em.mm(xnbuf[:, d, tc * 128:(tc + 1) * 128],
                          wb[:, d * 512:(d + 1) * 512],
                          banks[bank][:, :], BK[bank], start=(d == 0),
                          stop=(d == NSLAB - 1), reads=wkeys + (f"xn:{d}",))
                em.emit("v", lambda e, tc=tc, hf=hf, bank=bank: e.tensor_copy(
                        xvf[:, tc * 1024 + hf * 512:tc * 1024 + hf * 512 + 512],
                        banks[bank][:, :]),
                        reads=(BK[bank],), writes=(f"vt{tc}", "x"))
        em.dma("g", wb[:, 2048:4096], dram["l1a_cos"], writes=wbr(2048, 4096))
        em.dma("g", wb[:, 4096:6144], dram["l1a_sin"], writes=wbr(4096, 6144))
        for d in range(NSLAB):
            ks = bigv[:, KR1 + d * 2048:KR1 + (d + 1) * 2048]
            kk = bgr(KR1 + d * 2048, KR1 + (d + 1) * 2048)
            wkeys = wbr(0, 2048)
            for (d0, s0) in ((0, 32), (32, 0), (64, 96), (96, 64)):
                em.dma("q", wb[d0:d0 + 32, 0:2048],
                       bigv[s0:s0 + 32, KR1 + d * 2048:KR1 + (d + 1) * 2048],
                       reads=kk, writes=wkeys)
            em.emit("v", lambda e, ks=ks: e.tensor_mul(ks, ks,
                                                       wb[:, 2048:4096]),
                    reads=kk + wbr(2048, 4096), writes=kk)
            em.emit("v", lambda e: e.tensor_mul(wb[:, 0:2048], wb[:, 0:2048],
                                                wb[:, 4096:6144]),
                    reads=wkeys + wbr(4096, 6144), writes=wkeys)
            em.emit("v", lambda e, ks=ks: e.tensor_add(ks, ks, wb[:, 0:2048]),
                    reads=kk + wkeys, writes=kk)
        dbg("l1kr", bigv[:, KR1:KR1 + 2048], bgr(KR1, KR1 + 2048))
        dbg("l1sk", bigv[:, SK1:SK1 + 2048], bgr(SK1, SK1 + 2048))

    def l1_z():
        em.emit("g", lambda e: e.memset(fbuf[:, 2048:2112], 0.0),
                writes=("fD",))
        for d in range(NSLAB):
            ss = bigv[:, SK1 + d * 2048:SK1 + (d + 1) * 2048]
            em.emit("v", lambda e, ss=ss, d=d: e.tensor_reduce(
                    fbuf[:, d * 64:(d + 1) * 64],
                    ss.rearrange("p (j t) -> p j t", t=32), AX.X, ALU.add),
                    reads=bgr(SK1 + d * 2048, SK1 + (d + 1) * 2048),
                    writes=("fA",))
            em.emit("v", lambda e, d=d: e.tensor_tensor_scan(
                    fbuf[:, 1024 + d * 64:1024 + (d + 1) * 64],
                    fbuf[:, d * 64:(d + 1) * 64], fbuf[:, 2048:2112],
                    0.0, ALU.add, ALU.add),
                    reads=("fA", "fD"), writes=("fC",))
        em.emit("v", lambda e: e.tensor_sub(fbuf[:, 0:512],
                                            fbuf[:, 1024:1536],
                                            fbuf[:, 0:512]),
                reads=("fA", "fC"), writes=("fA",))
        em.emit("v", lambda e: e.tensor_copy(
                fbuf[:, 2048:2056],
                fbuf[:, 1024:1536].rearrange("p (d j) -> p d j", j=64)
                [:, :, 63:64].rearrange("p d j -> p (d j)")),
                reads=("fC",), writes=("fD",))
        em.dma("q", z_in, fbuf[:, 2048:2056], reads=("fD",), writes=("zin",))
        em.collective("AllGather", ALU.bypass, z_in.opt(), z_out.opt(),
                      reads=("zin",), writes=("zout",))
        em.emit("g", lambda e: e.memset(fbuf[:, 2056:2064], 0.0),
                writes=("fD",))
        for j in range(7):
            em.dma("q", fbuf[:, 2064:2072], z_out[j * 128:(j + 1) * 128, :],
                   reads=("zout",), writes=("fD",))
            em.emit("v", lambda e, j=j: e.scalar_tensor_tensor(
                    fbuf[:, 2056:2064], fbuf[:, 2064:2072], cmask[:, j:j + 1],
                    fbuf[:, 2056:2064], ALU.mult, ALU.add),
                    reads=("fD", "cmask"), writes=("fD",))
        em.emit("v", lambda e: e.tensor_add(
                fbuf[:, 0:512].rearrange("p (d j) -> p d j", j=64),
                fbuf[:, 0:512].rearrange("p (d j) -> p d j", j=64),
                fbuf[:, 2056:2064].rearrange("p (d o) -> p d o", o=1)
                .broadcast_to([128, 8, 64])),
                reads=("fA", "fD"), writes=("fA",))
        em.emit("v", lambda e: e.tensor_scalar_add(fbuf[:, 0:512],
                                                   fbuf[:, 0:512], 1e-6),
                reads=("fA",), writes=("fA",))
        em.emit("v", lambda e: e.tensor_copy(zt_sb[:, :], fbuf[:, 0:512]),
                reads=("fA",), writes=("zt",))
        dbg("l1zt", fbuf[:, 0:512], ("fA",))

    def l1_den(c, sq_mode):
        cc0 = c * 128
        dbk = 6 if sq_mode else 4
        for h in range(H1):
            pb = (h % 2) * 64
            zsl = zt_sb[pb:pb + 64,
                        (h // 2) * 64 + c * 4:(h // 2) * 64 + c * 4 + 4]
            if sq_mode:
                l_ap = scr[:, 1024 + (h // 2) * 128:1024 + (h // 2) * 128 + 128]
                lks = ("scr1024", "scr1536")
            else:
                a0 = SK1 + (h // 2) * 2048 + cc0
                l_ap = bigv[:, a0:a0 + 128]
                lks = bgr(a0, a0 + 128)
            if h % 2 == 0:
                em.mm(l_ap[0:64, 0:128], zsl, banks[dbk][:, h * 4:h * 4 + 4],
                      BK[dbk], start=True, stop=True, reads=lks + ("zt",))
            else:
                for sh in range(2):
                    em.mm(l_ap[64:128, sh * 64:sh * 64 + 64], zsl,
                          banks[dbk][sh * 64:sh * 64 + 64, h * 4:h * 4 + 4],
                          BK[dbk], start=True, stop=True, reads=lks + ("zt",))
        em.emit("v", lambda e, dbk=dbk: e.tensor_mul(
                fbuf[:, 2048:2112].rearrange("p (h j) -> p h j", j=4),
                banks[dbk][:, 0:64].rearrange("p (h j) -> p h j", j=4),
                segdiag[:, :].rearrange("p (o j) -> p o j", o=1)
                .broadcast_to([128, 16, 4])),
                reads=(BK[dbk], "sgd"), writes=("fD",))
        em.emit("v", lambda e: e.tensor_reduce(
                biasb[:, 144:160],
                fbuf[:, 2048:2112].rearrange("p (h j) -> p h j", j=4),
                AX.X, ALU.add),
                reads=("fD",), writes=("denP",))
        if sq_mode:
            em.emit("v", lambda e: e.tensor_scalar_add(
                    biasb[:, 144:160], biasb[:, 144:160], 1e-6),
                    reads=("denP",), writes=("denP",))
            em.emit("v", lambda e: e.reciprocal(fbuf[:, 2048:2064],
                                                biasb[:, 144:160]),
                    reads=("denP",), writes=("fD",))
        else:
            em.emit("v", lambda e: e.reciprocal(fbuf[:, 2048:2064],
                                                biasb[:, 144:160]),
                    reads=("denP",), writes=("fD",))

    def l1_p5():
        allscr = ("scr0", "scr512", "scr1024", "scr1536")
        for c in range(16):
            cc0 = c * 128
            for d in range(NSLAB):
                a0 = SK1 + d * 2048 + cc0
                em.transpose(psb[:, d * 128:(d + 1) * 128], bigv[:, a0:a0 + 128],
                             identb[:, :], PSB,
                             reads=bgr(a0, a0 + 128) + ("identb",))
            em.emit("v", lambda e: e.tensor_copy(scr[:, 0:1024], psb[:, :]),
                    reads=(PSB,), writes=("scr0", "scr512"))
            l1_den(c, sq_mode=False)
            em.emit("v", lambda e: e.tensor_mul(
                    scr[:, 1024:2048].rearrange("p (h f) -> p h f", f=64),
                    scr[:, 0:1024].rearrange("p (h f) -> p h f", f=64),
                    fbuf[:, 2048:2064].rearrange("p (h o) -> p h o", o=1)
                    .broadcast_to([128, 16, 64])),
                    reads=("scr0", "scr512", "fD"),
                    writes=("scr1024", "scr1536"))
            em.emit("v", lambda e: e.tensor_copy(
                    fbuf[:, 0:2048].rearrange("p (h f) -> p h f", f=128)
                    [:, :, 0:64],
                    scr[:, 1024:2048].rearrange("p (h f) -> p h f", f=64)),
                    reads=("scr1024", "scr1536"), writes=("fA", "fC"))
            em.emit("v", lambda e, c=c: e.tensor_copy(
                    fbuf[:, 0:2048].rearrange("p (h f) -> p h f", f=128)
                    [:, :, 64:128],
                    xvf[:, c * 1024:(c + 1) * 1024]
                    .rearrange("p (h f) -> p h f", f=64)),
                    reads=(f"vt{c}",), writes=("fA", "fC"))
            for j in range(NSLAB):
                em.transpose(psb[:, j * 128:(j + 1) * 128],
                             scr[:, 1024 + j * 128:1024 + (j + 1) * 128],
                             identb[:, :], PSB,
                             reads=("scr1024", "scr1536", "identb"))
            em.emit("v", lambda e: e.tensor_copy(scr[:, 0:1024], psb[:, :]),
                    reads=(PSB,), writes=("scr0", "scr512"))
            for h in range(H1):
                bank = h // 4
                a0 = SK1 + (h // 2) * 2048 + cc0
                lk = bgr(a0, a0 + 128)
                pb = (h % 2) * 64
                rhs = scr[pb:pb + 64, (h // 2) * 128:(h // 2) * 128 + 128]
                if h % 2 == 0:
                    em.mm(bigv[0:64, a0:a0 + 128], rhs,
                          banks[bank][:, (h % 4) * 128:(h % 4) * 128 + 128],
                          BK[bank], start=True, stop=True,
                          reads=lk + ("scr0", "scr512"))
                else:
                    for sh in range(2):
                        em.mm(bigv[64:128, a0 + sh * 64:a0 + sh * 64 + 64],
                              rhs,
                              banks[bank][sh * 64:sh * 64 + 64,
                                          (h % 4) * 128:(h % 4) * 128 + 128],
                              BK[bank], start=True, stop=True,
                              reads=lk + ("scr0", "scr512"))
            for bank in range(4):
                em.emit("v", lambda e, bank=bank: e.tensor_mul(
                        wb[:, bank * 512:(bank + 1) * 512]
                        .rearrange("p (a f) -> p a f", f=128),
                        banks[bank][:, :].rearrange("p (a f) -> p a f", f=128),
                        musg_sb[:, :].rearrange("p (o f) -> p o f", o=1)
                        .broadcast_to([128, 4, 128])),
                        reads=(BK[bank], "musg"),
                        writes=wbr(bank * 512, (bank + 1) * 512))
            em.emit("v", lambda e: e.tensor_copy(scr[:, 0:1024],
                                                 fbuf[:, 0:1024]),
                    reads=("fA",), writes=("scr0", "scr512"))
            em.emit("v", lambda e: e.tensor_copy(scr[:, 1024:2048],
                                                 fbuf[:, 1024:2048]),
                    reads=("fC",), writes=("scr1024", "scr1536"))
            for it in range(3):
                for bank in range(4):
                    fkey = "fA" if bank < 2 else "fC"
                    skey = f"scr{bank * 512}"
                    for hh in range(4):
                        h = bank * 4 + hh
                        em.mm(wb[:, h * 128:(h + 1) * 128],
                              scr[:, h * 128:(h + 1) * 128],
                              banks[bank][:, hh * 128:hh * 128 + 128],
                              BK[bank], start=True, stop=True,
                              reads=wbr(h * 128, (h + 1) * 128) + (skey,))
                    em.emit("v", lambda e, bank=bank: e.tensor_sub(
                            scr[:, bank * 512:(bank + 1) * 512],
                            fbuf[:, bank * 512:(bank + 1) * 512],
                            banks[bank][:, :]),
                            reads=(fkey, BK[bank]), writes=(skey,))
            em.dma("g", yspill[c, :, :], scr[:, 0:2048], reads=allscr,
                   writes=(f"ysp{c}",))
            for d in range(NSLAB):
                a0 = SK1 + d * 2048 + cc0
                em.transpose(psb[:, (d % 2) * 128:(d % 2) * 128 + 128],
                             bigv[:, a0:a0 + 128], identb[:, :], PSB,
                             reads=bgr(a0, a0 + 128) + ("identb",))
                em.emit("v", lambda e, d=d: e.tensor_copy(
                        stage_sb[:, (d % 2) * 128:(d % 2) * 128 + 128],
                        psb[:, (d % 2) * 128:(d % 2) * 128 + 128]),
                        reads=(PSB,), writes=(f"stg{d % 2}",))
                for hh in range(2):
                    h = 2 * d + hh
                    yk = scr[:, h * 128:h * 128 + 64]
                    yv = scr[:, h * 128 + 64:h * 128 + 128]
                    st = stage_sb[:, (d % 2) * 128 + hh * 64:
                                  (d % 2) * 128 + hh * 64 + 64]
                    scrk = (f"scr{(h // 4) * 512}", f"stg{d % 2}")
                    em.mm(yk, st,
                          banks[5 + h // 8][0:64, (h % 8) * 64:(h % 8) * 64 + 64],
                          BK[5 + h // 8], start=True, stop=True, reads=scrk)
                    em.mm(st, yv,
                          banks[h // 8][0:64, (h % 8) * 64:(h % 8) * 64 + 64],
                          BK[h // 8], start=True, stop=True, reads=scrk)
            em.emit("v", lambda e: e.tensor_copy(wb[0:64, 0:512],
                                                 banks[5][0:64, :]),
                    reads=(BK[5],), writes=wbr(0, 512))
            em.emit("v", lambda e: e.tensor_copy(wb[0:64, 512:1024],
                                                 banks[6][0:64, :]),
                    reads=(BK[6],), writes=wbr(512, 1024))
            em.emit("v", lambda e: e.tensor_copy(wb[0:64, 1024:1536],
                                                 banks[0][0:64, :]),
                    reads=(BK[0],), writes=wbr(1024, 1536))
            em.emit("v", lambda e: e.tensor_copy(wb[0:64, 1536:2048],
                                                 banks[1][0:64, :]),
                    reads=(BK[1],), writes=wbr(1536, 2048))
            em.dma("q", abspill[c, :, :], wb[0:64, 0:2048],
                   reads=wbr(0, 2048), writes=(f"absp{c}",))

    def l1_chain():
        em.emit("g", lambda e: e.memset(memst[:, :], 0.0), writes=("mst",))
        em.emit("v", lambda e: e.tensor_copy(
                cumA_sb[:, :].rearrange("p (h f) -> p h f", f=64),
                identb[0:64, 0:64].rearrange("p (o f) -> p o f", o=1)
                .broadcast_to([64, 16, 64])),
                reads=("identb",), writes=("cA",))
        em.emit("v", lambda e: e.tensor_copy(
                cumAT_sb[:, :].rearrange("p (h f) -> p h f", f=64),
                identb[0:64, 0:64].rearrange("p (o f) -> p o f", o=1)
                .broadcast_to([64, 16, 64])),
                reads=("identb",), writes=("cAT",))
        for c in range(16):
            em.dma("q", mcm[c, :, :], memst[:, :], reads=("mst",),
                   writes=(f"mcm{c}",))
            em.dma("q", mca[c, :, :], cumAT_sb[:, :], reads=("cAT",),
                   writes=(f"mca{c}",))
            em.dma("q", wb[0:64, 0:2048], abspill[c, :, :],
                   reads=(f"absp{c}",), writes=wbr(0, 2048))
            em.emit("v", lambda e: e.tensor_copy(memb_sb[0:64, :],
                                                 memst[:, :]),
                    reads=("mst",), writes=("mbb",))
            for h in range(H1):
                em.mm(wb[0:64, h * 64:(h + 1) * 64],
                      memb_sb[0:64, h * 64:(h + 1) * 64],
                      banks[h // 8][0:64, (h % 8) * 64:(h % 8) * 64 + 64],
                      BK[h // 8], start=True, stop=True,
                      reads=wbr(h * 64, (h + 1) * 64) + ("mbb",))
            for hb in range(2):
                em.emit("v", lambda e, hb=hb: e.tensor_sub(
                        memst[:, hb * 512:(hb + 1) * 512],
                        memst[:, hb * 512:(hb + 1) * 512],
                        banks[hb][0:64, :]),
                        reads=("mst", BK[hb]), writes=("mst",))
                em.emit("v", lambda e, hb=hb: e.tensor_add(
                        memst[:, hb * 512:(hb + 1) * 512],
                        memst[:, hb * 512:(hb + 1) * 512],
                        wb[0:64, 1024 + hb * 512:1024 + (hb + 1) * 512]),
                        reads=("mst",) + wbr(1024 + hb * 512,
                                             1024 + (hb + 1) * 512),
                        writes=("mst",))
            for h in range(H1):
                em.mm(wb[0:64, h * 64:(h + 1) * 64],
                      cumA_sb[0:64, h * 64:(h + 1) * 64],
                      banks[2 + h // 8][0:64, (h % 8) * 64:(h % 8) * 64 + 64],
                      BK[2 + h // 8], start=True, stop=True,
                      reads=wbr(h * 64, (h + 1) * 64) + ("cA",))
            for hb in range(2):
                em.emit("v", lambda e, hb=hb: e.tensor_sub(
                        cumA_sb[:, hb * 512:(hb + 1) * 512],
                        cumA_sb[:, hb * 512:(hb + 1) * 512],
                        banks[2 + hb][0:64, :]),
                        reads=("cA", BK[2 + hb]), writes=("cA",))
            for h in range(H1):
                em.transpose(psb[0:64, h * 64:(h + 1) * 64],
                             cumA_sb[0:64, h * 64:(h + 1) * 64],
                             identb[0:64, 0:64], PSB, reads=("cA", "identb"))
            em.emit("v", lambda e: e.tensor_copy(cumAT_sb[:, :],
                                                 psb[0:64, 0:1024]),
                    reads=(PSB,), writes=("cAT",))
        dbg("l1mst", memst[:, :], ("mst",))

    def l1_ag():
        em.dma("q", mag_in[0:64, :], memst[:, :], reads=("mst",),
               writes=("magi",))
        em.emit("v", lambda e: e.tensor_copy(scrf[0:64, :], cumA_sb[:, :]),
                reads=("cA",), writes=("rf0", "rf1"))
        em.dma("q", mag_in[64:128, :], scrf[0:64, :], reads=("rf0", "rf1"),
               writes=("magi",))
        em.collective("AllGather", ALU.bypass, mag_in.opt(), mag_out.opt(),
                      reads=("magi",), writes=("mago",))
        em.emit("g", lambda e: e.memset(fbuf[0:64, 0:1024], 0.0),
                writes=("fA",))
        for j in range(7):
            em.dma("q", scrf[0:64, :], mag_out[j * 128:j * 128 + 64, :],
                   reads=("mago",), writes=("rf0", "rf1"))
            em.dma("g", memb_sb[0:64, :], mag_out[j * 128 + 64:(j + 1) * 128, :],
                   reads=("mago",), writes=("mbb",))
            for h in range(H1):
                em.transpose(psb[0:64, h * 64:(h + 1) * 64],
                             memb_sb[0:64, h * 64:(h + 1) * 64],
                             identb[0:64, 0:64], PSB, reads=("mbb", "identb"))
            em.emit("v", lambda e: e.tensor_copy(cumAT_sb[:, :],
                                                 psb[0:64, 0:1024]),
                    reads=(PSB,), writes=("cAT",))
            em.emit("v", lambda e: e.tensor_copy(scr[0:64, 0:1024],
                                                 fbuf[0:64, 0:1024]),
                    reads=("fA",), writes=("scr0", "scr512"))
            for h in range(H1):
                em.mm(cumAT_sb[0:64, h * 64:(h + 1) * 64],
                      scr[0:64, h * 64:(h + 1) * 64],
                      banks[h // 8][0:64, (h % 8) * 64:(h % 8) * 64 + 64],
                      BK[h // 8], start=True, stop=True,
                      reads=("cAT", "scr0", "scr512"))
            for hb in range(2):
                em.emit("v", lambda e, hb=hb: e.tensor_sub(
                        fbuf[0:64, 1024 + hb * 512:1024 + (hb + 1) * 512],
                        banks[hb][0:64, :],
                        fbuf[0:64, hb * 512:(hb + 1) * 512]),
                        reads=(BK[hb], "fA"), writes=("fC",))
                em.emit("v", lambda e, hb=hb: e.tensor_add(
                        fbuf[0:64, 1024 + hb * 512:1024 + (hb + 1) * 512],
                        fbuf[0:64, 1024 + hb * 512:1024 + (hb + 1) * 512],
                        scrf[0:64, hb * 512:(hb + 1) * 512]),
                        reads=("fC", "rf0", "rf1"), writes=("fC",))
                em.emit("v", lambda e, hb=hb, j=j: e.scalar_tensor_tensor(
                        fbuf[0:64, hb * 512:(hb + 1) * 512],
                        fbuf[0:64, 1024 + hb * 512:1024 + (hb + 1) * 512],
                        cmask[0:64, j:j + 1],
                        fbuf[0:64, hb * 512:(hb + 1) * 512],
                        ALU.mult, ALU.add),
                        reads=("fC", "cmask", "fA"), writes=("fA",))
        em.emit("v", lambda e: e.tensor_copy(min_sb[:, :],
                                             fbuf[0:64, 0:1024]),
                reads=("fA",), writes=("min",))
        dbg("l1min", fbuf[0:64, 0:1024], ("fA",))

    def l1_out():
        for c in range(16):
            cc0 = c * 128
            em.dma("g", memb_sb[0:64, :], mca[c, :, :], reads=(f"mca{c}",),
                   writes=("mbb",))
            em.dma("q", fbuf[0:64, 0:1024], mcm[c, :, :], reads=(f"mcm{c}",),
                   writes=("fA",))
            em.dma("g", wb[:, 0:2048], yspill[c, :, :], reads=(f"ysp{c}",),
                   writes=wbr(0, 2048))
            em.dma("q", scr[:, 0:1024].rearrange("p (a f) -> p a f", f=128),
                   qspill[:, :, cc0:cc0 + 128], reads=("qsp",),
                   writes=("scr0", "scr512"))
            for h in range(H1):
                em.mm(memb_sb[0:64, h * 64:(h + 1) * 64],
                      min_sb[0:64, h * 64:(h + 1) * 64],
                      banks[h // 8][0:64, (h % 8) * 64:(h % 8) * 64 + 64],
                      BK[h // 8], start=True, stop=True, reads=("mbb", "min"))
            for hb in range(2):
                em.emit("v", lambda e, hb=hb: e.tensor_add(
                        fbuf[0:64, hb * 512:(hb + 1) * 512],
                        fbuf[0:64, hb * 512:(hb + 1) * 512],
                        banks[hb][0:64, :]),
                        reads=("fA", BK[hb]), writes=("fA",))
            em.emit("v", lambda e: e.tensor_copy(memb_sb[0:64, :],
                                                 fbuf[0:64, 0:1024]),
                    reads=("fA",), writes=("mbb",))
            em.dma("q", memb_sb[64:128, :], memb_sb[0:64, :], reads=("mbb",),
                   writes=("mbb",))
            em.emit("s", lambda e: e.activation(scrf[:, :], scr[:, 0:1024],
                                                AF.Relu, scale=-1.0),
                    reads=("scr0", "scr512"), writes=("rf0", "rf1"))
            em.emit("s", lambda e: e.activation(scrf[:, :], scrf[:, :],
                                                AF.Exp, scale=-1.0),
                    reads=("rf0", "rf1"), writes=("rf0", "rf1"))
            em.emit("s", lambda e: e.activation(scr[:, 1024:2048],
                                                scr[:, 0:1024], AF.Relu),
                    reads=("scr0", "scr512"),
                    writes=("scr1024", "scr1536"))
            em.emit("v", lambda e: e.tensor_add(scr[:, 1024:2048],
                                                scr[:, 1024:2048],
                                                scrf[:, :]),
                    reads=("scr1024", "scr1536", "rf0", "rf1"),
                    writes=("scr1024", "scr1536"))
            l1_den(c, sq_mode=True)
            for (d0, s0) in ((0, 32), (32, 0), (64, 96), (96, 64)):
                em.dma("g", scrf[d0:d0 + 32, :], scr[s0:s0 + 32, 0:1024],
                       reads=("scr0", "scr512"), writes=("rf0", "rf1"))
            em.emit("v", lambda e, cc0=cc0: e.tensor_mul(
                    scr[:, 0:1024].rearrange("p (a f) -> p a f", f=128),
                    scr[:, 0:1024].rearrange("p (a f) -> p a f", f=128),
                    wb[:, 2048 + cc0:2048 + cc0 + 128]
                    .rearrange("p (o f) -> p o f", o=1)
                    .broadcast_to([128, 8, 128])),
                    reads=("scr0", "scr512") + wbr(2048, 4096),
                    writes=("scr0", "scr512"))
            em.emit("v", lambda e, cc0=cc0: e.tensor_mul(
                    scrf[:, :].rearrange("p (a f) -> p a f", f=128),
                    scrf[:, :].rearrange("p (a f) -> p a f", f=128),
                    wb[:, 4096 + cc0:4096 + cc0 + 128]
                    .rearrange("p (o f) -> p o f", o=1)
                    .broadcast_to([128, 8, 128])),
                    reads=("rf0", "rf1") + wbr(4096, 6144),
                    writes=("rf0", "rf1"))
            em.emit("v", lambda e: e.tensor_add(scr[:, 0:1024],
                                                scr[:, 0:1024], scrf[:, :]),
                    reads=("scr0", "scr512", "rf0", "rf1"),
                    writes=("scr0", "scr512"))
            for h in range(H1):
                pb = (h % 2) * 64
                nb = 4 + h // 8
                nsec = (h % 8) * 64
                sq0 = 1024 + (h // 2) * 128
                em.transpose(psb[:, (h % 2) * 128:(h % 2) * 128 + 128],
                             wb[:, h * 128:(h + 1) * 128], identb[:, :], PSB,
                             reads=wbr(h * 128, (h + 1) * 128) + ("identb",))
                em.emit("v", lambda e, h=h: e.tensor_copy(
                        stage_sb[:, (h % 2) * 128:(h % 2) * 128 + 128],
                        psb[:, (h % 2) * 128:(h % 2) * 128 + 128]),
                        reads=(PSB,), writes=(f"stg{h % 2}",))
                em.mm(stage_sb[0:64, (h % 2) * 128:(h % 2) * 128 + 128],
                      memb_sb[0:64, h * 64:(h + 1) * 64],
                      banks[h // 8][:, (h % 8) * 64:(h % 8) * 64 + 64],
                      BK[h // 8], start=True, stop=True,
                      reads=(f"stg{h % 2}", "mbb"))
                em.emit("v", lambda e, h=h: e.tensor_sub(
                        wb[:, h * 128:h * 128 + 64],
                        wb[:, h * 128 + 64:h * 128 + 128],
                        banks[h // 8][:, (h % 8) * 64:(h % 8) * 64 + 64]),
                        reads=wbr(h * 128, (h + 1) * 128) + (BK[h // 8],),
                        writes=wbr(h * 128, (h + 1) * 128))
                a0 = SK1 + (h // 2) * 2048 + cc0
                lk = bgr(a0, a0 + 128)
                gb = 2 + h // 8
                gsec = (h % 4) * 128
                rhs_sq = scr[pb:pb + 64, sq0:sq0 + 128]
                if h % 2 == 0:
                    em.mm(bigv[0:64, a0:a0 + 128], rhs_sq,
                          banks[gb][:, gsec:gsec + 128], BK[gb],
                          start=True, stop=True,
                          reads=lk + ("scr1024", "scr1536"))
                else:
                    for sh in range(2):
                        em.mm(bigv[64:128, a0 + sh * 64:a0 + sh * 64 + 64],
                              rhs_sq,
                              banks[gb][sh * 64:sh * 64 + 64, gsec:gsec + 128],
                              BK[gb], start=True, stop=True,
                              reads=lk + ("scr1024", "scr1536"))
                em.emit("v", lambda e, h=h, gb=gb, gsec=gsec: e.tensor_mul(
                        stage_sb[:, (h % 2) * 128:(h % 2) * 128 + 128],
                        banks[gb][:, gsec:gsec + 128], musg_sb[:, :]),
                        reads=(BK[gb], "musg"), writes=(f"stg{h % 2}",))
                if h % 2 == 0:
                    em.mm(scr[0:64, sq0:sq0 + 128],
                          memb_sb[0:64, h * 64:(h + 1) * 64],
                          banks[nb][:, nsec:nsec + 64], BK[nb],
                          start=True, stop=False,
                          reads=("scr1024", "scr1536", "mbb"))
                    em.mm(stage_sb[:, (h % 2) * 128:(h % 2) * 128 + 128],
                          wb[:, h * 128:h * 128 + 64],
                          banks[nb][:, nsec:nsec + 64], BK[nb],
                          start=False, stop=True,
                          reads=(f"stg{h % 2}",) + wbr(h * 128, (h + 1) * 128))
                else:
                    for sh in range(2):
                        em.mm(scr[64:128, sq0 + sh * 64:sq0 + sh * 64 + 64],
                              memb_sb[64:128, h * 64:(h + 1) * 64],
                              banks[nb][sh * 64:sh * 64 + 64, nsec:nsec + 64],
                              BK[nb], start=True, stop=False,
                              reads=("scr1024", "scr1536", "mbb"))
                        em.mm(stage_sb[:, (h % 2) * 128 + sh * 64:
                                       (h % 2) * 128 + sh * 64 + 64],
                              wb[:, h * 128:h * 128 + 64],
                              banks[nb][sh * 64:sh * 64 + 64, nsec:nsec + 64],
                              BK[nb], start=False, stop=True,
                              reads=(f"stg{h % 2}",) + wbr(h * 128,
                                                           (h + 1) * 128))
            for hb in range(2):
                em.emit("v", lambda e, hb=hb: e.tensor_mul(
                        scrf[:, hb * 512:(hb + 1) * 512]
                        .rearrange("p (h f) -> p h f", f=64),
                        banks[4 + hb][:, :].rearrange("p (h f) -> p h f", f=64),
                        fbuf[:, 2048 + hb * 8:2056 + hb * 8]
                        .rearrange("p (h o) -> p h o", o=1)
                        .broadcast_to([128, 8, 64])),
                        reads=(BK[4 + hb], "fD"), writes=(f"rf{hb}",))
            em.dma("g", scr[:, 1024:2048], xvf[:, c * 1024:(c + 1) * 1024],
                   reads=(f"vt{c}",), writes=("scr1024", "scr1536"))
            for half in range(2):
                h0 = half * 8
                for hh in range(8):
                    h = h0 + hh
                    pb = (h % 2) * 64
                    sb_ = banks[2 * half + hh // 4]
                    sbk = BK[2 * half + hh // 4]
                    sc0 = (hh % 4) * 128
                    ka = KR1 + (h // 2) * 2048 + cc0
                    qr_l = scr[pb:pb + 64,
                               (h // 2) * 128:(h // 2) * 128 + 128]
                    if h % 2 == 0:
                        em.mm(qr_l, bigv[0:64, ka:ka + 128],
                              sb_[:, sc0:sc0 + 128], sbk,
                              start=True, stop=True,
                              reads=("scr0", "scr512") + bgr(ka, ka + 128))
                    else:
                        for sh in range(2):
                            em.mm(scr[64:128, (h // 2) * 128 + sh * 64:
                                      (h // 2) * 128 + sh * 64 + 64],
                                  bigv[64:128, ka:ka + 128],
                                  sb_[sh * 64:sh * 64 + 64, sc0:sc0 + 128],
                                  sbk, start=True, stop=True,
                                  reads=("scr0", "scr512") + bgr(ka, ka + 128))
                for b in range(2):
                    bb = 2 * half + b
                    em.emit("v", lambda e, bb=bb, b=b: e.scalar_tensor_tensor(
                            fbuf[:, b * 512:(b + 1) * 512]
                            .rearrange("p (h k) -> p h k", k=128),
                            banks[bb][:, :]
                            .rearrange("p (h k) -> p h k", k=128),
                            0.125,
                            cmf_sb[:, :].rearrange("p (o k) -> p o k", o=1)
                            .broadcast_to([128, 4, 128]),
                            ALU.mult, ALU.add),
                            reads=(BK[bb], "cmf"), writes=("fA",))
                em.emit("v", lambda e: e.tensor_reduce(
                        fbuf[:, 2064:2072],
                        fbuf[:, 0:1024].rearrange("p (h k) -> p h k", k=128),
                        AX.X, ALU.max),
                        reads=("fA",), writes=("fD",))
                em.emit("v", lambda e: e.tensor_sub(
                        fbuf[:, 0:1024].rearrange("p (h k) -> p h k", k=128),
                        fbuf[:, 0:1024].rearrange("p (h k) -> p h k", k=128),
                        fbuf[:, 2064:2072].rearrange("p (h o) -> p h o", o=1)
                        .broadcast_to([128, 8, 128])),
                        reads=("fA", "fD"), writes=("fA",))
                em.emit("s", lambda e: e.activation(
                        fbuf[:, 0:1024], fbuf[:, 0:1024], AF.Exp),
                        reads=("fA",), writes=("fA",))
                em.emit("v", lambda e: e.tensor_reduce(
                        fbuf[:, 2072:2080],
                        fbuf[:, 0:1024].rearrange("p (h k) -> p h k", k=128),
                        AX.X, ALU.add),
                        reads=("fA",), writes=("fD",))
                em.emit("v", lambda e: e.reciprocal(
                        fbuf[:, 2080:2088], fbuf[:, 2072:2080]),
                        reads=("fD",), writes=("fD",))
                em.emit("v", lambda e: e.tensor_mul(
                        wb[:, 0:1024].rearrange("p (h k) -> p h k", k=128),
                        fbuf[:, 0:1024].rearrange("p (h k) -> p h k", k=128),
                        fbuf[:, 2080:2088].rearrange("p (h o) -> p h o", o=1)
                        .broadcast_to([128, 8, 128])),
                        reads=("fA", "fD"), writes=wbr(0, 1024))
                em.emit("v", lambda e: e.transpose(
                        wb[:, 1024:2048], wb[:, 0:1024]),
                        reads=wbr(0, 1024), writes=wbr(1024, 2048))
                for hh in range(8):
                    h = h0 + hh
                    em.mm(wb[:, 1024 + hh * 128:1024 + (hh + 1) * 128],
                          scr[:, 1024 + h * 64:1024 + (h + 1) * 64],
                          banks[4 + half][:, hh * 64:(hh + 1) * 64],
                          BK[4 + half], start=True, stop=True,
                          reads=wbr(1024 + hh * 128, 1024 + (hh + 1) * 128)
                          + ("scr1024", "scr1536"))
            for hb in range(2):
                em.emit("v", lambda e, hb=hb: e.tensor_sub(
                        scrf[:, hb * 512:(hb + 1) * 512],
                        scrf[:, hb * 512:(hb + 1) * 512],
                        banks[4 + hb][:, :]),
                        reads=(f"rf{hb}", BK[4 + hb]), writes=(f"rf{hb}",))
                em.emit("v", lambda e, hb=hb: e.tensor_mul(
                        scrf[:, hb * 512:(hb + 1) * 512]
                        .rearrange("p (h f) -> p h f", f=64),
                        scrf[:, hb * 512:(hb + 1) * 512]
                        .rearrange("p (h f) -> p h f", f=64),
                        betam[:, hb * 8:(hb + 1) * 8]
                        .rearrange("p (h o) -> p h o", o=1)
                        .broadcast_to([128, 8, 64])),
                        reads=(f"rf{hb}", "beta"), writes=(f"rf{hb}",))
                em.emit("v", lambda e, hb=hb: e.tensor_add(
                        wb[:, 1024 + hb * 512:1024 + (hb + 1) * 512],
                        scrf[:, hb * 512:(hb + 1) * 512],
                        banks[4 + hb][:, :]),
                        reads=(f"rf{hb}", BK[4 + hb]),
                        writes=wbr(1024 + hb * 512, 1024 + (hb + 1) * 512))
            for d in range(NSLAB):
                em.transpose(psb[:, d * 128:(d + 1) * 128],
                             wb[:, 1024 + d * 128:1024 + (d + 1) * 128],
                             identb[:, :], PSB,
                             reads=wbr(1024 + d * 128, 1024 + (d + 1) * 128)
                             + ("identb",))
            okeys = tuple(sorted({bgr(KR1 + d * 2048 + cc0,
                                      KR1 + d * 2048 + cc0 + 128)[0]
                                  for d in range(NSLAB)}))
            em.emit("v", lambda e, cc0=cc0: e.tensor_copy(
                    bigv[:, KR1:KR1 + 16384]
                    .rearrange("p (d t) -> p d t", t=2048)[:, :, cc0:cc0 + 128],
                    psb[:, :].rearrange("p (d t) -> p d t", t=128)),
                    reads=(PSB,), writes=okeys)
        dbg("l1ofm", bigv[:, KR1:KR1 + 2048], bgr(KR1, KR1 + 2048))
        wout1 = dram["l1a_wout"].rearrange("(a p) c -> p a c", p=128)
        for od in range(8):
            s0 = (od % 4) * 1024
            wkeys = wbr(s0, s0 + 1024)
            em.dma("g", wb[:, s0:s0 + 1024]
                   .rearrange("p (a f) -> p a f", f=128),
                   wout1[:, :, od * 128:(od + 1) * 128], writes=wkeys)
            wv = wb[:, s0:s0 + 1024].rearrange("p (a f) -> p a f", f=128)
            for ti in range(NTILE):
                c0 = ti * 512
                bank = 4 + ti % 2
                for sl in range(NSLAB):
                    em.mm(wv[:, sl, :],
                          bigv[:, KR1 + sl * 2048 + c0:KR1 + sl * 2048 + c0 + 512],
                          banks[bank][:, :], BK[bank], start=(sl == 0),
                          stop=(sl == NSLAB - 1),
                          reads=wkeys + bgr(KR1 + sl * 2048 + c0,
                                            KR1 + sl * 2048 + c0 + 512))
                em.emit("s", lambda e, bank=bank: e.activation(
                        scrf[:, 0:512], banks[bank][:, :], AF.Copy),
                        reads=(BK[bank],), writes=("rf0",))
                em.dma("q", scrf[:, 512:1024], xspill[:, od, c0:c0 + 512],
                       reads=("xsp",), writes=("rf1",))
                em.emit("v", lambda e, od=od, c0=c0: e.tensor_add(
                        xbuf[:, od, c0:c0 + 512], scrf[:, 0:512],
                        scrf[:, 512:1024]),
                        reads=("rf0", "rf1"),
                        writes=("x", f"vt{od * 2 + c0 // 1024}"))

    def l1_attn():
        em.dma("q", xspill, xbuf[:, :, :], reads=("x",), writes=("xsp",))
        em.dma("q", l1ln_sb[:, :], dram["l1a_lnr"], writes=("l1ln",))
        em.dma("q", cm32_sb[:, :], dram["l1a_cm32"], writes=("cm32",))
        em.dma("q", cmf_sb[:, :], dram["l1a_cmfull"], writes=("cmf",))
        em.dma("q", segdiag[:, :], dram["l1a_segdiag"], writes=("sgd",))
        em.dma("q", betam[:, :], dram["l1a_beta"], writes=("beta",))
        em.emit("s", lambda e: e.activation(betam[:, :], betam[:, :], AF.Exp,
                                            scale=-1.0),
                reads=("beta",), writes=("beta",))
        em.emit("v", lambda e: e.tensor_scalar_add(betam[:, :], betam[:, :],
                                                   1.0),
                reads=("beta",), writes=("beta",))
        em.emit("v", lambda e: e.reciprocal(betam[:, :], betam[:, :]),
                reads=("beta",), writes=("beta",))
        ln_affine(3, composed=True, write_y=False)
        l1_qkv_passes()
        em.dma("g", musg_sb[:, :], dram["l1a_musg"], writes=("musg",))
        em.emit("g", lambda e: e.memset(identb[:, :], 0.0),
                writes=("identb",))
        em.emit("g", lambda e: e.affine_select(
                out=identb[:, :], in_=identb[:, :], compare_op=ALU.not_equal,
                fill=1.0, base=0, pattern=[[-1, 128]], channel_multiplier=1),
                reads=("identb",), writes=("identb",))
        if KCUT3 < 2:
            return
        l1_z()
        if KCUT3 < 3:
            return
        l1_p5()
        if KCUT3 < 4:
            return
        l1_chain()
        l1_ag()
        if KCUT3 < 5:
            return
        l1_out()

    module_ln()
    l0_ffn(first_call=True)
    if not STUB_L0ATTN:
        l0_attn()
    if KCUT >= 99:
        module_ln()
        l0_ffn(first_call=False)
        module_ln()
        l1_ffn()
        if not STUB_L1ATTN:
            l1_attn()
        module_ln()
        l1_ffn()

    otok = em.dma("q", out_ap, xbuf[:, :, :], reads=("x",))
    em._wait("q", otok)

    em.finalize(ctx.enter_context(nc.Block()))
    return nc, ctx


def host_layout(inputs):
    i = inputs
    x = np.asarray(i["L0_attn_x"], np.float32).reshape(SEQ, D)
    xT = np.ascontiguousarray(x.T).reshape(NSLAB, 128, SEQ)

    def coltab(b):
        return np.ascontiguousarray(
            np.asarray(b, np.float32).reshape(-1, 128).T)

    common = {
        "l0f_wfcT": np.ascontiguousarray(
            np.asarray(i["L0_ffn_W_fc"], np.float32).T),
        "l0f_wprT": np.ascontiguousarray(
            np.asarray(i["L0_ffn_W_proj"], np.float32).T),
        "l0f_afcT": np.ascontiguousarray(
            np.asarray(i["L0_ffn_A_fc"], np.float32).T),
        "l0f_bfcT": np.ascontiguousarray(
            np.asarray(i["L0_ffn_B_fc"], np.float32).T),
        "l0f_aprT": np.ascontiguousarray(
            np.asarray(i["L0_ffn_A_proj"], np.float32).T),
        "l0f_bprT": np.ascontiguousarray(
            np.asarray(i["L0_ffn_B_proj"], np.float32).T),
        "l0f_bfc_col": coltab(i["L0_ffn_b_fc"]),
        "l0f_bpr_col": coltab(i["L0_ffn_b_proj"]),
        "l0a_lng_col": np.concatenate([coltab(i["L0_attn_ln1_g"]),
                                       coltab(i["L0_attn_ln2_g"]),
                                       coltab(i["L0_attn_ln3_g"])], axis=1),
        "l0a_lnb_col": np.concatenate([coltab(i["L0_attn_ln1_b"]),
                                       coltab(i["L0_attn_ln2_b"]),
                                       coltab(i["L0_attn_ln3_b"])], axis=1),
        "l0a_wpot": np.ascontiguousarray(
            np.asarray(i["L0_attn_w_pot"], np.float32)[:, ::-1]),
        "l0a_bpot": np.ascontiguousarray(
            np.asarray(i["L0_attn_b_pot"], np.float32)[::-1].reshape(2, 1)),
        "l0a_wbk": np.ascontiguousarray(np.asarray(i["L0_attn_w_bk"], np.float32)),
        "l0a_bbk_col": coltab(i["L0_attn_b_bk"]),
        "l0a_wqkv": np.ascontiguousarray(np.asarray(i["L0_attn_w_qkv"], np.float32)),
        "l0a_bqkv_col": coltab(i["L0_attn_b_qkv"]),
        "l0a_bkv_row": np.ascontiguousarray(
            np.asarray(i["L0_attn_b_qkv"], np.float32)[512:].reshape(1, 1024)),
        "l0a_wout": np.ascontiguousarray(np.asarray(i["L0_attn_w_out"], np.float32)),
        "l0a_bout_col": coltab(i["L0_attn_b_out"]),
        "l0a_wff1": np.ascontiguousarray(np.asarray(i["L0_attn_w_ff1"], np.float32)),
        "l0a_bff1_col": coltab(i["L0_attn_b_ff1"]),
        "l0a_wff2": np.ascontiguousarray(np.asarray(i["L0_attn_w_ff2"], np.float32)),
        "l0a_bff2_col": coltab(i["L0_attn_b_ff2"]),
        "l1f_wgT": np.ascontiguousarray(
            np.asarray(i["L1_ffn_w_gate"], np.int32).T),
        "l1f_wuT": np.ascontiguousarray(
            np.asarray(i["L1_ffn_w_up"], np.int32).T),
        "l1f_wdT": np.ascontiguousarray(
            np.asarray(i["L1_ffn_w_down"], np.int32).T),
    }
    qkperm = np.concatenate([np.arange(0, 64, 2), np.arange(1, 64, 2)])
    permf = np.concatenate([h * 64 + qkperm for h in range(16)])
    wq1 = np.asarray(i["L1_attn_w_qkv"], np.float32).copy()
    wq1[:, 0:1024] = wq1[:, permf]
    wq1[:, 1024:2048] = wq1[:, 1024 + permf]
    common["l1a_wqkv"] = np.ascontiguousarray(wq1)
    common["l1a_wout"] = np.ascontiguousarray(
        np.asarray(i["L1_attn_w_out"], np.float32))
    common["l1a_lnr"] = np.ascontiguousarray(np.concatenate(
        [coltab(i["L1_attn_ln_g"]), coltab(i["L1_attn_ln_b"])], axis=1))
    common["l1a_musg"] = np.ascontiguousarray(
        (np.arange(128)[:, None] // 32
         < np.arange(128)[None, :] // 32).astype(np.float32))
    cm32 = np.where(np.arange(32)[None, :] <= np.arange(32)[:, None],
                    0.0, -1e30).astype(np.float32)
    common["l1a_cm32"] = np.ascontiguousarray(np.concatenate([cm32, cm32], 0))
    qq = np.arange(128)
    cmfull = np.where((qq[None, :] <= qq[:, None])
                      & (qq[None, :] // 32 == qq[:, None] // 32),
                      0.0, -1e30).astype(np.float32)
    common["l1a_cmfull"] = np.ascontiguousarray(cmfull)
    common["l1a_segdiag"] = np.ascontiguousarray(
        (np.arange(128)[:, None] // 32
         == np.arange(4)[None, :]).astype(np.float32))
    beta1 = np.asarray(i["L1_attn_mem_beta"], np.float32).reshape(-1)
    common["l1a_beta"] = np.ascontiguousarray(
        np.broadcast_to(beta1[None, :], (128, 16)).copy())
    rrows = np.arange(128) % 64
    rfreq = 10000.0 ** (-(rrows % 32).astype(np.float64) / 32.0)
    rsgn = np.where(rrows < 32, -1.0, 1.0)

    in_maps = []
    for c in range(NCORES):
        m = dict(common)
        xs = xT[:, :, c * T:(c + 1) * T]
        m["x_in"] = np.ascontiguousarray(xs.transpose(1, 0, 2))
        cm = np.zeros((1, 16), np.float32)
        cm[0, 0:c] = 1.0
        cm[0, 8 + c + 1:16] = 1.0
        m["coremask"] = np.ascontiguousarray(np.repeat(cm, 128, axis=0))
        pos = np.arange(c * T, (c + 1) * T, dtype=np.float64)
        ang = rfreq[:, None] * pos[None, :]
        m["l1a_cos"] = np.ascontiguousarray(np.cos(ang).astype(np.float32))
        m["l1a_sin"] = np.ascontiguousarray(
            (rsgn[:, None] * np.sin(ang)).astype(np.float32))
        in_maps.append(m)
    return in_maps


def kernel(**inputs):
    nc, ctx = build_nc()
    in_maps = host_layout(inputs)
    res = run_bass_kernel_spmd(nc, in_maps, core_ids=list(range(NCORES)))
    global LAST_RES
    LAST_RES = res
    ctx.close()
    shards = [np.asarray(res.results[c]["out"]).transpose(1, 0, 2)
              .reshape(D, T) for c in range(NCORES)]
    full = np.concatenate(shards, axis=1)
    return np.ascontiguousarray(full.T).reshape(1, SEQ, D)
